# version 07
# baseline (speedup 1.0000x reference)
import contextlib
import numpy as np
import ml_dtypes

import concourse.bass as bass
import concourse.mybir as mybir
from concourse.bass_utils import run_bass_kernel_spmd

F32 = mybir.dt.float32
BF16 = mybir.dt.bfloat16
AF = mybir.ActivationFunctionType
ALU = mybir.AluOpType
BF = ml_dtypes.bfloat16

D_MODEL = 2048
D_INNER = 4096
H = 64
D_HEAD = 64
NSTATE = 128
G = 8
CHUNK = 256
GN = G * NSTATE
CONV_DIM = D_INNER + 2 * GN
IP_COLS = 10368
INTER = 5632
TCORE = 4096
HALO = 128
TT = TCORE + HALO
NCHUNK = TCORE // CHUNK
NCB_XBC = CONV_DIM // 128
NCB_ALL = IP_COLS // 128
NKB = D_MODEL // 128
NIB = INTER // 128
NYB = D_INNER // 128

ENGINES = ("tensor", "vector", "scalar", "gpsimd", "sync")
DMA_QUEUES = ("sync", "scalar", "gpsimd", "vector")


class Sched:

    def __init__(self, ctx):
        self.ctx = ctx
        self.items = {e: [] for e in ENGINES}
        self.pending_barrier = {}

    def barrier(self):
        ms = [(n, c) for (n, c) in self.ctx.cnt.items() if c > 0]
        self.pending_barrier = {e: list(ms) for e in ENGINES}

    def add(self, eng, fn, after=(), sig=False, stream=None):
        ctx = self.ctx
        waits = []
        for m in after:
            if m is None:
                continue
            (sem, val) = m
            if sem == "e_" + eng:
                assert val <= ctx.cnt[sem], f"same-engine forward wait {sem} {val} > {ctx.cnt[sem]}"
            if sem.startswith("s_"):
                assert val == ctx.cnt[sem], \
                    f"unsound prefix wait on dma stream {sem}: {val} != {ctx.cnt[sem]}"
            waits.append((sem, val))
        pb = self.pending_barrier.pop(eng, None)
        if pb:
            seen = {s for (s, _) in waits}
            waits = [(s, v) for (s, v) in pb if s not in seen] + waits
        if eng in ("vector", "scalar") and stream is None:
            sig = True
        rec = {"fn": fn, "waits": waits, "inc": None}
        ms = None
        if stream is not None:
            sem = ctx.stream(stream)
            ctx.cnt[sem] += 16
            rec["inc"] = (sem, 16)
            ms = (sem, ctx.cnt[sem])
        elif sig:
            sem = "e_" + eng
            ctx.cnt[sem] += 1
            rec["inc"] = (sem, 1)
            ms = (sem, ctx.cnt[sem])
        self.items[eng].append(rec)
        return ms

    def dma(self, eng, out, in_, stream=None, after=(), **kw):
        assert stream is not None, "every dma needs a stream"
        return self.add(eng, lambda e: e.dma_start(out=out, in_=in_, **kw),
                        after=after, stream=stream)

    def emit(self, block):
        ctx = self.ctx
        for eng in ENGINES:
            recs = self.items[eng]
            if not recs:
                continue

            def make(recs):
                def body(e):
                    last_wait = {}
                    for r in recs:
                        for (sem, val) in r["waits"]:
                            if last_wait.get(sem, -1) >= val:
                                continue
                            last_wait[sem] = val
                            e.wait_ge(ctx.sems[sem], val)
                        instr = r["fn"](e)
                        if r["inc"] is not None:
                            sem, n = r["inc"]
                            assert instr is not None, "inc on op returning no instruction"
                            instr.then_inc(ctx.sems[sem], n)
                return body

            getattr(block, eng)(make(recs))
        self.items = {e: [] for e in ENGINES}


class Ctx:
    def __init__(self, nc):
        self.nc = nc
        self.sems = {}
        self.cnt = {}

    def make_sems(self, stack):
        self.stack = stack
        names = ["e_" + e for e in ENGINES] + ["cc"]
        for n in names:
            self.sems[n] = stack.enter_context(self.nc.semaphore(n))
            self.cnt[n] = 0

    def stream(self, name):
        n = "s_" + name
        if n not in self.sems:
            self.sems[n] = self.stack.enter_context(self.nc.semaphore(n))
            self.cnt[n] = 0
        return n

    def full(self, name):
        n = self.stream(name)
        return (n, self.cnt[n])


def ceil_div(a, b):
    return (a + b - 1) // b



def build_program(dbg_level=99):
    nc = bass.Bass("TRN2", target_bir_lowering=False)

    def din(name, shape, dt=F32):
        return nc.dram_tensor(name, shape, dt, kind="ExternalInput").ap()

    x_h = din("x_h", [TT, D_MODEL])
    ip_wT = din("ip_wT", [D_MODEL, IP_COLS])
    convw_r = din("convw_r", [128, NCB_XBC * 4])
    convb_r = din("convb_r", [128, NCB_XBC])
    cnw_r = din("cnw_r", [128, NCB_XBC])
    a_log = din("a_log", [H, 1])
    dt_b = din("dt_b", [H, 1])
    dt_wT = din("dt_wT", [H, H])
    won_r = din("won_r", [128, NYB])
    op_wT = din("op_wT", [D_INNER, D_MODEL])
    gate_wT = din("gate_wT", [D_MODEL, INTER])
    up_wT = din("up_wT", [D_MODEL, INTER])
    down_wT = din("down_wT", [INTER, D_MODEL])
    gs_r = din("gs_r", [128, NIB])
    us_r = din("us_r", [128, NIB])
    ds_rep = din("ds_rep", [128, D_MODEL])
    nw_rep = din("nw_rep", [128, D_MODEL])
    eye128 = din("eye128", [128, 128], BF16)
    ones8 = din("ones8", [8, 128], BF16)
    ones1f = din("ones1f", [1, 128])
    eye128f = din("eye128f", [128, 128])
    onesT = din("onesT", [128, 128], BF16)
    cb_mask = din("cb_mask", [128, 384], BF16)
    rmask = din("rmask", [128, 4])
    rmask1 = din("rmask1", [128, 4])

    out = nc.dram_tensor("out", [TCORE, D_MODEL], F32, kind="ExternalOutput").ap()

    s_xr = nc.dram_tensor("s_xr", [TT, D_MODEL], BF16).ap()
    s_xn = nc.dram_tensor("s_xn", [TCORE, D_MODEL], BF16).ap()
    s_r0 = nc.dram_tensor("s_r0", [TCORE, D_MODEL], F32).ap()
    s_co = nc.dram_tensor("s_co", [CONV_DIM, TCORE], BF16).ap()
    s_xbc = nc.dram_tensor("s_xbc", [CONV_DIM, TCORE], BF16).ap()
    s_gate = nc.dram_tensor("s_gate", [D_INNER, TCORE], BF16).ap()
    s_hdn = nc.dram_tensor("s_hdn", [INTER, TCORE], BF16).ap()
    s_yg = nc.dram_tensor("s_yg", [D_INNER, TCORE], BF16).ap()
    s_inner = nc.dram_tensor("s_inner", [D_MODEL, TCORE], BF16).ap()
    ag_src_t = nc.dram_tensor("ag_src", [H * NSTATE + H, D_HEAD], F32)
    ag_dst_t = nc.dram_tensor("ag_dst", [4 * (H * NSTATE + H), D_HEAD], F32)
    ag_src = ag_src_t.ap()
    ag_dst = ag_dst_t.ap()

    dbg = {}
    if dbg_level < 99:
        dbg["xn"] = nc.dram_tensor("d_xn", [TCORE, D_MODEL], BF16, kind="ExternalOutput").ap()
        dbg["xr"] = nc.dram_tensor("d_xr", [TT, D_MODEL], BF16, kind="ExternalOutput").ap()
        dbg["r0"] = nc.dram_tensor("d_r0", [TCORE, D_MODEL], F32, kind="ExternalOutput").ap()
    if dbg_level in (1, 2, 3):
        dbg["co"] = nc.dram_tensor("d_co", [CONV_DIM, TCORE], BF16, kind="ExternalOutput").ap()
        dbg["xbc"] = nc.dram_tensor("d_xbc", [CONV_DIM, TCORE], BF16, kind="ExternalOutput").ap()
        dbg["gate"] = nc.dram_tensor("d_gate", [D_INNER, TCORE], BF16, kind="ExternalOutput").ap()
        dbg["acs"] = nc.dram_tensor("d_acs", [H, TCORE], F32, kind="ExternalOutput").ap()
    if dbg_level in (2, 3):
        dbg["agd"] = nc.dram_tensor("d_agd", [4 * (H * NSTATE + H), D_HEAD], F32, kind="ExternalOutput").ap()
    if dbg_level == 3:
        dbg["yg"] = nc.dram_tensor("d_yg", [D_INNER, TCORE], BF16, kind="ExternalOutput").ap()
        dbg["hdn"] = nc.dram_tensor("d_hdn", [INTER, TCORE], BF16, kind="ExternalOutput").ap()

    ctx = Ctx(nc)

    with contextlib.ExitStack() as stack:
        ctx.make_sems(stack)
        block = stack.enter_context(nc.Block())
        sch = Sched(ctx)

        sb = lambda name, shape, dt=F32: stack.enter_context(nc.sbuf_tensor(name, shape, dt))
        c_convw = sb("c_convw", [128, NCB_XBC * 4])
        c_convb = sb("c_convb", [128, NCB_XBC])
        c_cnw = sb("c_cnw", [128, NCB_XBC])
        c_alog = sb("c_alog", [H, 1])
        c_dtb = sb("c_dtb", [H, 1])
        c_dtwT = sb("c_dtwT", [H, H], BF16)
        c_won = sb("c_won", [128, NYB])
        c_gs = sb("c_gs", [128, NIB])
        c_us = sb("c_us", [128, NIB])
        c_eye = sb("c_eye", [128, 128], BF16)
        c_ones8 = sb("c_ones8", [8, 128], BF16)
        c_onesf = sb("c_onesf", [1, 128])
        c_eyef = sb("c_eyef", [128, 128])
        c_onesT = sb("c_onesT", [128, 128], BF16)
        c_cbmask = sb("c_cbmask", [128, 384], BF16)
        c_rmask = sb("c_rmask", [128, 4])
        c_rmask1 = sb("c_rmask1", [128, 4])
        mstack = stack.enter_context(contextlib.ExitStack())
        msb = lambda name, shape, dt=F32: mstack.enter_context(nc.sbuf_tensor(name, shape, dt))
        mstackA = stack.enter_context(contextlib.ExitStack())
        st_hi = msb("st_hi", [H, TCORE], BF16)
        st_lo = msb("st_lo", [H, TCORE], BF16)
        st_alast = msb("st_alast", [H, NCHUNK])
        st_offs = msb("st_offs", [H, NCHUNK])
        st_wb = msb("st_wb", [H, NCHUNK])
        st_lend = msb("st_lend", [H, 1])
        st_cdrep = msb("st_cdrep", [128, NCHUNK * H])
        hA = msb("p3_hA", [128, H * D_HEAD], BF16)
        hB = msb("p3_hB", [128, H * D_HEAD], BF16)
        d1tk_all = msb("d1tk_all", [128, NCHUNK * 128], F32)
        negacs = msb("negacs", [128, NCHUNK * 128], F32)
        st_adt = mstackA.enter_context(nc.sbuf_tensor("st_adt", [H, TCORE], BF16))
        st_acs = mstackA.enter_context(nc.sbuf_tensor("st_acs", [H, TCORE], F32))
        wtk_all = mstackA.enter_context(nc.sbuf_tensor("wtk_all", [128, NCHUNK * 128], BF16))

        pre = [
            (c_convw, convw_r), (c_convb, convb_r), (c_cnw, cnw_r),
            (c_alog, a_log), (c_dtb, dt_b), (c_won, won_r),
            (c_gs, gs_r), (c_us, us_r),
            (c_eye, eye128), (c_ones8, ones8), (c_onesT, onesT), (c_onesf, ones1f),
            (c_eyef, eye128f),
            (c_cbmask, cb_mask), (c_rmask, rmask), (c_rmask1, rmask1),
        ]
        for (dst, srcap) in pre:
            sch.dma("sync", dst[:, :], srcap, stream="cld")
        sch.dma("gpsimd", c_dtwT[:, :], dt_wT, stream="cld2")
        ALL_CONSTS = [ctx.full("cld"), ctx.full("cld2")]

        NT0 = TT // 128
        import os
        NT0 = min(NT0, int(os.environ.get("K_NT0", "99")))
        with contextlib.ExitStack() as ph:
            c_nwrep = ph.enter_context(nc.sbuf_tensor("c_nwrep", [128, D_MODEL], F32))
            sch.dma("sync", c_nwrep[:, :], nw_rep, stream="cld")
            ALL_CONSTS = [ctx.full("cld"), ctx.full("cld2")]
            NB0 = 3
            xt = [ph.enter_context(nc.sbuf_tensor(f"p0_x{i}", [128, D_MODEL], F32)) for i in range(NB0)]
            xnb = [ph.enter_context(nc.sbuf_tensor(f"p0_xn{i}", [128, D_MODEL], BF16)) for i in range(NB0)]
            r0t = [ph.enter_context(nc.sbuf_tensor(f"p0_r0{i}", [128, D_MODEL], F32)) for i in range(NB0)]
            xrt = [ph.enter_context(nc.sbuf_tensor(f"p0_xr{i}", [128, D_MODEL], BF16)) for i in range(NB0)]
            bst = ph.enter_context(nc.sbuf_tensor("p0_bst", [128, 24], F32))
            mv = ph.enter_context(nc.sbuf_tensor("p0_mv", [128, 2], F32))
            tiny = ph.enter_context(nc.sbuf_tensor("p0_tiny", [128, 8], F32))
            last_use = [[] for _ in range(NB0)]
            for i in range(NT0):
                b = i % NB0
                ld = sch.dma("sync", xt[b][:, :], x_h[i * 128:(i + 1) * 128, :],
                             stream=f"p0ld{b}",
                             after=(ALL_CONSTS if i == 0 else []) + last_use[b])
                last_use[b] = []
                ms_bs = None
                for j in range(4):
                    ms_bs = sch.add("vector", lambda e, b=b, j=j: e.bn_stats(
                        out=bst[:, j * 6:(j + 1) * 6], in_=xt[b][:, j * 512:(j + 1) * 512]),
                        after=[ld], sig=True)
                ms_ag = sch.add("vector", lambda e: e.bn_aggr(out=mv[:, :], in_=bst[:, :]),
                                after=[ms_bs], sig=True)
                ms_e1 = sch.add("vector", lambda e: e.tensor_scalar(
                    out=tiny[:, 0:1], in0=mv[:, 1:2], scalar1=1e-5, scalar2=None, op0=ALU.add),
                    after=[ms_ag], sig=True)
                ms_r1 = sch.add("vector", lambda e: e.reciprocal(out=tiny[:, 1:2], in_=tiny[:, 0:1]),
                                after=[ms_e1], sig=True)
                ms_rstd = sch.add("scalar", lambda e: e.activation(
                    out=tiny[:, 2:3], in_=tiny[:, 1:2], func=AF.Sqrt), after=[ms_r1], sig=True)
                ms_fsq = sch.add("vector", lambda e: e.tensor_scalar(
                    out=tiny[:, 3:4], in0=mv[:, 1:2], scalar1=tiny[:, 1:2], scalar2=1e-6,
                    op0=ALU.mult, op1=ALU.add), after=[ms_r1], sig=True)
                ms_f0 = sch.add("vector", lambda e: e.reciprocal(out=tiny[:, 4:5], in_=tiny[:, 3:4]),
                                after=[ms_fsq], sig=True)
                ms_f = sch.add("scalar", lambda e: e.activation(
                    out=tiny[:, 4:5], in_=tiny[:, 4:5], func=AF.Sqrt), after=[ms_f0], sig=True)
                ms_nb = sch.add("vector", lambda e: e.tensor_scalar(
                    out=tiny[:, 5:6], in0=mv[:, 0:1], scalar1=tiny[:, 2:3], scalar2=-1.0,
                    op0=ALU.mult, op1=ALU.mult), after=[ms_rstd], sig=True)
                ms_r1p = sch.add("vector", lambda e: e.tensor_scalar(
                    out=tiny[:, 6:7], in0=tiny[:, 2:3], scalar1=1.0, scalar2=None, op0=ALU.add),
                    after=[ms_rstd], sig=True)
                ms_xn = sch.add("scalar", lambda e, b=b: e.activation(
                    out=xnb[b][:, :], in_=xt[b][:, :], func=AF.Identity,
                    bias=tiny[:, 5:6], scale=tiny[:, 2:3]), after=[ms_nb], sig=True)
                ms_r0 = None
                if i >= 1:
                    ms_r0 = sch.add("scalar", lambda e, b=b: e.activation(
                        out=r0t[b][:, :], in_=xt[b][:, :], func=AF.Identity,
                        bias=tiny[:, 5:6], scale=tiny[:, 6:7]), after=[ms_r1p, ms_nb], sig=True)
                ms_xr = sch.add("vector", lambda e, b=b: e.scalar_tensor_tensor(
                    out=xrt[b][:, :], in0=xnb[b][:, :], scalar=tiny[:, 4:5], in1=c_nwrep[:, :],
                    op0=ALU.mult, op1=ALU.mult), after=[ms_xn, ms_f], sig=True)
                sch.dma("gpsimd", s_xr[i * 128:(i + 1) * 128, :], xrt[b][:, :],
                        stream=f"p0xr{b}", after=[ms_xr])
                if dbg_level < 99:
                    sch.dma("gpsimd", dbg["xr"][i * 128:(i + 1) * 128, :], xrt[b][:, :],
                            stream=f"p0xr{b}", after=[ms_xr])
                if i >= 1:
                    t0 = (i - 1) * 128
                    sch.dma("scalar", s_xn[t0:t0 + 128, :], xnb[b][:, :],
                            stream=f"p0st{b}", after=[ms_xn])
                    sch.dma("scalar", s_r0[t0:t0 + 128, :], r0t[b][:, :],
                            stream=f"p0st{b}", after=[ms_r0])
                    if dbg_level < 99:
                        sch.dma("scalar", dbg["xn"][t0:t0 + 128, :], xnb[b][:, :],
                                stream=f"p0st{b}", after=[ms_xn])
                        sch.dma("scalar", dbg["r0"][t0:t0 + 128, :], r0t[b][:, :],
                                stream=f"p0st{b}", after=[ms_r0])
                last_use[b] = [ms_xr, ctx.full(f"p0xr{b}"), ctx.full(f"p0st{b}")]
            sch.emit(block)

        if dbg_level == 0:
            fin = Sched(ctx)
            z = stack.enter_context(nc.sbuf_tensor("zz", [128, D_MODEL], F32))
            wl = [ctx.full(f"p0st{b}") for b in range(NB0)] + [ctx.full(f"p0xr{b}") for b in range(NB0)]
            ms = fin.add("vector", lambda e: e.memset(z[:, :], 0.0), after=wl, sig=True)
            for i in range(TCORE // 128):
                fin.dma("gpsimd", out[i * 128:(i + 1) * 128, :], z[:, :],
                        stream="fino", after=[ms])
            fms = ctx.full("fino")
            fin.add("gpsimd", lambda e, fms=fms: e.wait_ge(ctx.sems[fms[0]], fms[1]))
            fin.emit(block)
            return nc

        sch.barrier()
        import os
        NCB_RUN = min(NCB_ALL, int(os.environ.get("K_CB", "99")))
        NSUP = min(2, int(os.environ.get("K_SUP", "2")))
        with contextlib.ExitStack() as ph:
            xrT = ph.enter_context(nc.sbuf_tensor("p1_xrT", [128, NKB * 2048], BF16))
            xrTh = ph.enter_context(nc.sbuf_tensor("p1_xrTh", [128, NKB * 128], BF16))
            wsl = [ph.enter_context(nc.sbuf_tensor(f"p1_w{i}", [128, NKB * 128], BF16)) for i in range(2)]

            dg = [ph.enter_context(nc.sbuf_tensor(f"p1_dg{i}", [128, 4 * 128], BF16)) for i in range(2)]
            cin = [ph.enter_context(nc.sbuf_tensor(f"p1_cin{i}", [128, 2051], BF16)) for i in range(2)]
            stash = ph.enter_context(nc.sbuf_tensor("p1_stash", [128, NCB_XBC * 3], BF16))
            cot = [ph.enter_context(nc.sbuf_tensor(f"p1_co{i}", [128, 1024], BF16)) for i in range(2)]
            sqt = [ph.enter_context(nc.sbuf_tensor(f"p1_sq{i}", [128, 1024], BF16)) for i in range(2)]
            gstg = None
            rcb = ph.enter_context(nc.sbuf_tensor("p1_rcb", [128, 1024], BF16))
            rcf = ph.enter_context(nc.sbuf_tensor("p1_rcf", [128, 1024], F32))
            cob = [ph.enter_context(nc.sbuf_tensor(f"p1_cob{i}", [128, 1024], BF16)) for i in range(4)]
            xbb = [ph.enter_context(nc.sbuf_tensor(f"p1_xbb{i}", [128, 1024], BF16)) for i in range(4)]
            ps_ip = [ph.enter_context(nc.psum_tensor(f"p1_pip{i}", [128, 512], F32)) for i in range(2)]
            ps_cv = [ph.enter_context(nc.psum_tensor(f"p1_pcv{i}", [128, 512], F32)) for i in range(2)]
            ps_sq = [ph.enter_context(nc.psum_tensor(f"p1_psq{i}", [128, 512], F32)) for i in range(4)]

            ms_dguse = [None, None]
            p0_done = [ctx.full(f"p0xr{b}") for b in range(NB0)] + [ctx.full(f"p0st{b}") for b in range(NB0)]

            ms_xrh = [sch.dma("sync", xrTh[:, k * 128:(k + 1) * 128],
                              s_xr[0:HALO, k * 128:(k + 1) * 128], transpose=True,
                              stream="p1xrh", after=p0_done) for k in range(NKB)]
            xrh_full = ctx.full("p1xrh")
            W_ENG = "gpsimd"
            wuse = [None, None]
            ms_stash = {}
            ms_cinw = [None, None]
            ms_cinr = [None, None]
            ms_ipevac = [None, None]
            ms_convmm = [None, None]
            ms_coscl = [None, None]
            ms_gevac = [None, None]
            ms_sqmm = None
            ms_rcread = None
            for c in range(min(NCB_XBC, NCB_RUN)):
                sl = c % 2
                wld = sch.dma(W_ENG, wsl[sl][:, :],
                              bass.AP(ip_wT.tensor, c * 128,
                                      [[IP_COLS, 128], [128 * IP_COLS, NKB], [1, 128]]),
                              stream=f"p1w{sl}", after=[wuse[sl]] + (ALL_CONSTS if c < 2 else []))
                lw = [wld, xrh_full] + ([ms_ipevac[0]] if ms_ipevac[0] else [])
                mm = None
                for k in range(NKB):
                    mm = sch.add("tensor", lambda e, sl=sl, k=k: e.matmul(
                        ps_ip[0][:, 0:128], wsl[sl][:, k * 128:(k + 1) * 128],
                        xrTh[:, k * 128:(k + 1) * 128],
                        start=(k == 0), stop=(k == NKB - 1)),
                        after=(lw if k == 0 else []), sig=(k == NKB - 1))
                wuse[sl] = mm
                ms_stash[c] = sch.add("vector", lambda e, c=c: e.tensor_scalar(
                    out=stash[:, c * 3:c * 3 + 3], in0=ps_ip[0][:, 125:128], scalar1=1.0,
                    scalar2=None, op0=ALU.mult), after=[mm], sig=True)
                ms_ipevac[0] = ms_stash[c]

            ms_tt = [None] * 4
            ms_sil = [None] * 4
            p1b_mr4 = None
            for sup in range(NSUP):
                tbase = HALO + sup * 2048
                mbase = sup * 2048
                wl = p0_done if sup == 0 else []
                prev_mm = [m for m in (wuse + ms_ipevac) if m]
                ms_xrT = [sch.dma("sync", xrT[:, k * 2048:(k + 1) * 2048],
                                  s_xr[tbase:tbase + 2048, k * 128:(k + 1) * 128], transpose=True,
                                  stream="p1xrT", after=wl + (prev_mm if k == 0 else []))
                          for k in range(NKB)]
                xrT_full = ctx.full("p1xrT")
                pend_tail = []
                p1b_units = []
                if NCB_RUN >= NCB_XBC:
                    for half in range(2):
                        def rc_unit(half=half):
                            nonlocal ms_rcread, p1b_mr4
                            mr1 = []
                            for tt in range(2):
                                mr1.append(sch.add("vector", lambda e, tt=tt, half=half: e.tensor_scalar(
                                    out=rcf[:, tt * 512:tt * 512 + 512], in0=ps_sq[half * 2 + tt][:, :],
                                    scalar1=1.0 / CONV_DIM, scalar2=1e-6, op0=ALU.mult, op1=ALU.add),
                                    after=[ms_sqmm], sig=True))
                            ms_rcread = mr1[1]
                            mr2 = sch.add("vector", lambda e: e.reciprocal(out=rcf[:, :], in_=rcf[:, :]),
                                          after=[mr1[0], mr1[1]], sig=True)
                            mr3 = sch.add("scalar", lambda e: e.activation(out=rcf[:, :], in_=rcf[:, :],
                                          func=AF.Sqrt), after=[mr2], sig=True)
                            p1b_mr4 = sch.add("vector", lambda e: e.tensor_scalar(
                                out=rcb[:, :], in0=rcf[:, :], scalar1=1.0, scalar2=None, op0=ALU.mult),
                                after=[mr3], sig=True)
                        p1b_units.append(rc_unit)
                        for cb in range(NCB_XBC):
                            def blk(cb=cb, half=half, mbase=mbase):
                                pofs = half * 1024
                                sl = cb % 4
                                ldb = sch.dma("gpsimd", cob[sl][:, :],
                                              s_co[cb * 128:(cb + 1) * 128, mbase + pofs:mbase + pofs + 1024],
                                              stream=f"p1cb{sl}",
                                              after=[ctx.full("p1co0"), ctx.full("p1co1"),
                                                     ms_tt[sl], ms_sil[sl]])
                                mtt = sch.add("vector", lambda e, sl=sl: e.tensor_tensor(
                                    out=cob[sl][:, :], in0=cob[sl][:, :], in1=rcb[:, :], op=ALU.mult),
                                    after=[ldb, p1b_mr4, ms_sil[sl]], sig=True)
                                ms_tt[sl] = mtt
                                msl = sch.add("scalar", lambda e, cb=cb, sl=sl: e.activation(
                                    out=xbb[sl][:, :], in_=cob[sl][:, :], func=AF.Silu,
                                    scale=c_cnw[:, cb:cb + 1]),
                                    after=[mtt, ms_sil[sl], ctx.full(f"p1xb{sl}")], sig=True)
                                ms_sil[sl] = msl
                                sch.dma("scalar",
                                        s_xbc[cb * 128:(cb + 1) * 128, mbase + pofs:mbase + pofs + 1024],
                                        xbb[sl][:, :], stream=f"p1xb{sl}", after=[msl])
                                if dbg_level in (1, 2, 3):
                                    sch.dma("scalar",
                                            dbg["xbc"][cb * 128:(cb + 1) * 128, mbase + pofs:mbase + pofs + 1024],
                                            xbb[sl][:, :], stream=f"p1xb{sl}", after=[msl])
                            p1b_units.append(blk)
                for c in range(NCB_RUN):
                    sl = c % 2
                    is_xbc = c < NCB_XBC
                    is_adt = c == 48
                    wld = sch.dma(W_ENG, wsl[sl][:, :],
                                  bass.AP(ip_wT.tensor, c * 128,
                                          [[IP_COLS, 128], [128 * IP_COLS, NKB], [1, 128]]),
                                  stream=f"p1w{sl}", after=[wuse[sl]])
                    mm = None
                    m_ev = [None, None, None, None]
                    m1 = None
                    for tt in range(4):
                        pb = tt % 2
                        for k in range(NKB):
                            mm = sch.add("tensor", lambda e, sl=sl, k=k, tt=tt, pb=pb: e.matmul(
                                ps_ip[pb][:, :], wsl[sl][:, k * 128:(k + 1) * 128],
                                xrT[:, k * 2048 + tt * 512:k * 2048 + tt * 512 + 512],
                                start=(k == 0), stop=(k == NKB - 1)),
                                after=(([wld, xrT_full] if tt == 0 else []) +
                                       ([ms_ipevac[pb]] if k == 0 else [])),
                                sig=(k == NKB - 1))
                        if is_xbc and tt == 0:
                            m1 = sch.add("vector", lambda e, c=c, sl=sl: e.tensor_scalar(
                                out=cin[sl][:, 0:3], in0=stash[:, c * 3:c * 3 + 3], scalar1=1.0,
                                scalar2=None, op0=ALU.mult),
                                after=[ms_stash[c], ms_cinr[sl]], sig=True)
                        extra = [m1] if (is_xbc and tt == 0) else []
                        if (not is_xbc) and (not is_adt):
                            extra = [ms_gevac[sl], ctx.full(f"p1g{sl}"), ctx.full(f"p1co{sl}")]
                        if is_xbc:
                            dstap = lambda sl=sl, tt=tt: cin[sl][:, 3 + tt * 512:3 + tt * 512 + 512]
                        elif is_adt:
                            dstap = lambda sl=sl, tt=tt, mbase=mbase: st_adt[:, mbase + tt * 512:mbase + tt * 512 + 512]
                        else:
                            dstap = lambda sl=sl, tt=tt: cot[sl][:, (tt % 2) * 512:(tt % 2) * 512 + 512]
                        if is_adt:
                            mev = sch.add("vector", lambda e, pb=pb, dstap=dstap: e.tensor_scalar(
                                out=dstap(), in0=ps_ip[pb][64:128, :], scalar1=1.0,
                                scalar2=None, op0=ALU.mult), after=[mm], sig=True)
                        elif tt % 2 == 0:
                            mev = sch.add("vector", lambda e, pb=pb, dstap=dstap: e.tensor_scalar(
                                out=dstap(), in0=ps_ip[pb][:, :], scalar1=1.0,
                                scalar2=None, op0=ALU.mult), after=[mm] + extra, sig=True)
                        else:
                            mev = sch.add("scalar", lambda e, pb=pb, dstap=dstap: e.activation(
                                out=dstap(), in_=ps_ip[pb][:, :], func=AF.Copy),
                                after=[mm] + extra, sig=True)
                        m_ev[tt] = mev
                        ms_ipevac[pb] = mev
                        if (not is_xbc) and (not is_adt) and tt % 2 == 1:
                            gc = c - 49
                            half = tt // 2
                            gst = sch.dma("scalar",
                                          s_gate[gc * 128:(gc + 1) * 128,
                                                 mbase + half * 1024:mbase + half * 1024 + 1024],
                                          cot[sl][:, 0:1024],
                                          stream=f"p1g{sl}", after=[m_ev[tt - 1], m_ev[tt]])
                            if dbg_level in (1, 2, 3):
                                gst = sch.dma("scalar",
                                              dbg["gate"][gc * 128:(gc + 1) * 128,
                                                          mbase + half * 1024:mbase + half * 1024 + 1024],
                                              cot[sl][:, 0:1024], stream=f"p1g{sl}",
                                              after=[m_ev[tt - 1], m_ev[tt]])
                            ms_gevac[sl] = gst
                    wuse[sl] = m_ev[3]
                    if pend_tail and c >= 1:
                        pend_tail.pop(0)()
                    if is_xbc:
                        ms_cinw[sl] = m_ev[3]
                        ms_stash[c] = sch.add("vector", lambda e, c=c, sl=sl: e.tensor_scalar(
                            out=stash[:, c * 3:c * 3 + 3], in0=cin[sl][:, 2048:2051], scalar1=1.0,
                            scalar2=None, op0=ALU.mult), after=[m_ev[2], m_ev[3]], sig=True)

                        def conv_tail(c=c, sl=sl, m_ev=m_ev, sup=sup, mbase=mbase):
                            nonlocal ms_sqmm, ms_rcread
                            mdg = None
                            for k4 in range(4):
                                mdg = sch.add("vector", lambda e, c=c, sl=sl, k4=k4: e.tensor_scalar(
                                    out=dg[sl][:, k4 * 128:(k4 + 1) * 128],
                                    in0=c_eye[:, :], scalar1=c_convw[:, c * 4 + k4:c * 4 + k4 + 1],
                                    scalar2=None, op0=ALU.mult),
                                    after=([ms_dguse[sl]] + (ALL_CONSTS if sup == 0 and c < 2 else [])
                                           if k4 == 0 else []),
                                    sig=(k4 == 3))
                            cmm = None
                            for rnd in range(2):
                                for k4 in range(4):
                                    for tt in range(2):
                                        cmm = sch.add("tensor", lambda e, sl=sl, k4=k4, tt=tt, rnd=rnd: e.matmul(
                                            ps_cv[tt][:, :],
                                            dg[sl][:, k4 * 128:(k4 + 1) * 128],
                                            cin[sl][:, rnd * 1024 + k4 + tt * 512:rnd * 1024 + k4 + tt * 512 + 512],
                                            start=(k4 == 0), stop=(k4 == 3)),
                                            after=([m_ev[0], m_ev[1], m_ev[2], m_ev[3], mdg,
                                                    ms_coscl[0], ms_coscl[1]]
                                                   if (k4 == 0 and tt == 0 and rnd == 0) else
                                                   ([ms_coscl[0], ms_coscl[1]] if (k4 == 0 and tt == 0) else [])),
                                            sig=(k4 == 3 and tt == 1))
                                s1 = []
                                s2 = []
                                for tt in range(2):
                                    s1.append(sch.add("scalar", lambda e, c=c, sl=sl, tt=tt: e.activation(
                                        out=cot[sl][:, tt * 512:tt * 512 + 512], in_=ps_cv[tt][:, :],
                                        func=AF.Identity, bias=c_convb[:, c:c + 1], scale=1.0),
                                        after=([cmm, ctx.full(f"p1co{sl}"), ctx.full(f"p1g{sl}")] if tt == 0 else []),
                                        sig=(tt == 1)))
                                for tt in range(2):
                                    s2.append(sch.add("scalar", lambda e, c=c, sl=sl, tt=tt: e.activation(
                                        out=sqt[sl][:, tt * 512:tt * 512 + 512], in_=ps_cv[tt][:, :],
                                        func=AF.Square, bias=c_convb[:, c:c + 1], scale=1.0),
                                        after=[cmm], sig=(tt == 1)))
                                ms_coscl[0] = s1[1]
                                ms_coscl[1] = s2[1]
                                sch.dma("gpsimd", s_co[c * 128:(c + 1) * 128,
                                                       mbase + rnd * 1024:mbase + rnd * 1024 + 1024],
                                        cot[sl][:, :], stream=f"p1co{sl}", after=[s1[1]])
                                if dbg_level in (1, 2, 3):
                                    sch.dma("gpsimd", dbg["co"][c * 128:(c + 1) * 128,
                                                                mbase + rnd * 1024:mbase + rnd * 1024 + 1024],
                                            cot[sl][:, :], stream=f"p1co{sl}", after=[s1[1]])
                                for tt in range(2):
                                    ms_sqmm = sch.add("tensor", lambda e, tt=tt, sl=sl, c=c, rnd=rnd: e.matmul(
                                        ps_sq[rnd * 2 + tt][:, :], c_onesT[:, :],
                                        sqt[sl][:, tt * 512:tt * 512 + 512],
                                        start=(c == 0), stop=(c == min(NCB_XBC, NCB_RUN) - 1)),
                                        after=[s2[1]] + ([ms_rcread] if c == 0 else []),
                                        sig=(tt == 1 and c == min(NCB_XBC, NCB_RUN) - 1 and rnd == 1))
                            ms_cinr[sl] = cmm
                            ms_dguse[sl] = cmm

                        pend_tail.append(conv_tail)
                    if c >= 49:
                        for _ in range(4):
                            if p1b_units:
                                p1b_units.pop(0)()
                while pend_tail:
                    pend_tail.pop(0)()
                while p1b_units:
                    p1b_units.pop(0)()
                ms_sqmm = None
            sch.emit(block)

        if dbg_level == 1:
            fin = Sched(ctx)
            z = stack.enter_context(nc.sbuf_tensor("zz", [128, D_MODEL], F32))
            wl = [ctx.full(s) for s in ["p1xb0", "p1xb1", "p1xb2", "p1xb3", "p1g0", "p1g1", "p1co0", "p1co1"]]
            ms = fin.add("vector", lambda e: e.memset(z[:, :], 0.0), after=wl, sig=True)
            for i in range(TCORE // 128):
                fin.dma("gpsimd", out[i * 128:(i + 1) * 128, :], z[:, :], stream="fino", after=[ms])
            fms = ctx.full("fino")
            fin.add("gpsimd", lambda e, fms=fms: e.wait_ge(ctx.sems[fms[0]], fms[1]))
            fin.emit(block)
            return nc

        sch.barrier()
        with contextlib.ExitStack() as ph:
            pA = ph.enter_context(nc.sbuf_tensor("p1c_A", [H, TCORE], F32))
            pq = ph.enter_context(nc.sbuf_tensor("p1c_q", [H, 512], F32))
            pz = ph.enter_context(nc.sbuf_tensor("p1c_z", [H, 16], F32))
            zer = ph.enter_context(nc.sbuf_tensor("p1c_zer", [H, CHUNK], F32))
            cdc = ph.enter_context(nc.sbuf_tensor("p1c_cdc", [H, NCHUNK], F32))
            cdflat = ph.enter_context(nc.sbuf_tensor("p1c_cdf", [1, NCHUNK * H], F32))
            ps_z = ph.enter_context(nc.psum_tensor("p1c_pz", [H, 512], F32))
            ps_cd = [ph.enter_context(nc.psum_tensor(f"p1c_pcd{i}", [128, 512], F32)) for i in range(2)]

            m_ne = sch.add("scalar", lambda e: e.activation(
                out=pz[:, 0:1], in_=c_alog[:, :], func=AF.Exp), sig=True)
            m_ne2 = sch.add("vector", lambda e: e.tensor_scalar(
                out=pz[:, 0:1], in0=pz[:, 0:1], scalar1=-1.0, scalar2=None, op0=ALU.mult),
                after=[m_ne], sig=True)
            m_zer = sch.add("vector", lambda e: e.memset(zer[:, :], 0.0), sig=True)
            m_sc = None
            m_Aprev = None
            for it in range(8):
                mm = sch.add("tensor", lambda e, it=it: e.matmul(
                    ps_z[:, :], c_dtwT[:, :], st_adt[:, it * 512:(it + 1) * 512],
                    start=True, stop=True), after=([m_sc] if m_sc else []), sig=True)
                mq = sch.add("scalar", lambda e: e.activation(
                    out=pq[:, :], in_=ps_z[:, :], func=AF.Exp, bias=c_dtb[:, :]),
                    after=[mm, m_Aprev], sig=True)
                m_sc = mq
                ml = sch.add("scalar", lambda e: e.activation(
                    out=pq[:, :], in_=pq[:, :], func=AF.Ln, bias=1.0), after=[mq], sig=True)
                m_A = sch.add("vector", lambda e, it=it: e.tensor_scalar(
                    out=pA[:, it * 512:(it + 1) * 512], in0=pq[:, :], scalar1=pz[:, 0:1],
                    scalar2=None, op0=ALU.mult), after=[ml, m_ne2], sig=True)
                m_Aprev = m_A
            m_lastA = m_A
            m_scan = None
            for cch in range(NCHUNK):
                m_scan = sch.add("vector", lambda e, cch=cch: e.tensor_tensor_scan(
                    out=st_acs[:, cch * CHUNK:(cch + 1) * CHUNK],
                    data0=pA[:, cch * CHUNK:(cch + 1) * CHUNK], data1=zer[:, :],
                    initial=0.0, op0=ALU.add, op1=ALU.add),
                    after=[m_lastA, m_zer] + ([m_scan] if m_scan else []), sig=True)
            m_al = sch.add("vector", lambda e: e.tensor_scalar(
                out=st_alast[:, :],
                in0=bass.AP(st_acs, CHUNK - 1, [[TCORE, H], [CHUNK, NCHUNK]]),
                scalar1=1.0, scalar2=None, op0=ALU.mult), after=[m_scan], sig=True)
            m_oi = sch.add("vector", lambda e: e.tensor_tensor_scan(
                out=st_offs[:, :], data0=st_alast[:, :], data1=zer[:, 0:NCHUNK],
                initial=0.0, op0=ALU.add, op1=ALU.add), after=[m_al], sig=True)
            m_le = sch.add("vector", lambda e: e.tensor_scalar(
                out=st_lend[:, :], in0=st_offs[:, NCHUNK - 1:NCHUNK], scalar1=1.0,
                scalar2=None, op0=ALU.mult), after=[m_oi], sig=True)
            m_ox = sch.add("vector", lambda e: e.tensor_tensor(
                out=st_offs[:, :], in0=st_offs[:, :], in1=st_alast[:, :], op=ALU.subtract),
                after=[m_le], sig=True)
            m_wb = sch.add("vector", lambda e: e.tensor_scalar(
                out=st_wb[:, :], in0=st_offs[:, :], scalar1=-1.0, scalar2=st_lend[:, :],
                op0=ALU.mult, op1=ALU.add), after=[m_ox], sig=True)
            m_dk = sch.add("scalar", lambda e: e.activation(
                out=pz[:, 1:2], in_=st_lend[:, :], func=AF.Exp), after=[m_le], sig=True)
            sch.dma("gpsimd", ag_src[H * NSTATE:H * NSTATE + H, :], zer[0:64, 0:64],
                    stream="agd", after=[m_zer])
            sch.dma("gpsimd", ag_src[H * NSTATE:H * NSTATE + 1, 0:H], pz[:, 1:2],
                    stream="agd", after=[m_dk, ctx.full("agd")])
            m_hi = sch.add("vector", lambda e: e.tensor_scalar(
                out=st_hi[:, :], in0=st_acs[:, :], scalar1=1.0, scalar2=None, op0=ALU.mult),
                after=[m_al], sig=True)
            m_r1 = sch.add("vector", lambda e: e.tensor_tensor(
                out=pA[:, :], in0=st_acs[:, :], in1=st_hi[:, :], op=ALU.subtract),
                after=[m_hi], sig=True)
            m_lo = sch.add("vector", lambda e: e.tensor_scalar(
                out=st_lo[:, :], in0=pA[:, :], scalar1=1.0, scalar2=None, op0=ALU.mult),
                after=[m_r1], sig=True)
            m_cd = sch.add("scalar", lambda e: e.activation(
                out=cdc[:, :], in_=st_alast[:, :], func=AF.Exp), after=[m_al], sig=True)
            m_cdf = sch.dma("sync", cdflat[0:1, :], cdc[:, :], stream="cdf", after=[m_cd])
            m_rep = []
            for half in range(2):
                mr = sch.add("tensor", lambda e, half=half: e.matmul(
                    ps_cd[half][:, :], c_onesf[:, :], cdflat[0:1, half * 512:(half + 1) * 512],
                    start=True, stop=True), after=[m_cdf], sig=True)
                m_rep.append(sch.add("vector", lambda e, half=half: e.tensor_scalar(
                    out=st_cdrep[:, half * 512:(half + 1) * 512], in0=ps_cd[half][:, :],
                    scalar1=1.0, scalar2=None, op0=ALU.mult), after=[mr], sig=True))
            if dbg_level in (1, 2, 3):
                sch.dma("gpsimd", dbg["acs"], st_acs[:, :], stream="dacs", after=[m_scan])
            sch.emit(block)

        sch.barrier()
        with contextlib.ExitStack() as ph:
            wcm = ph.enter_context(nc.sbuf_tensor("p2_wcm", [H, CHUNK], BF16))
            dcm = ph.enter_context(nc.sbuf_tensor("p2_dcm", [H, CHUNK], BF16))
            ps_tr = ph.enter_context(nc.psum_tensor("p2_ptr", [128, 64], BF16))
            ps_trf = ph.enter_context(nc.psum_tensor("p2_ptrf", [128, 64], F32))
            m_wu = None
            m_du = None
            m_tre = None
            m_trf = None
            for cch in range(NCHUNK):
                for tt in range(2):
                    mt = sch.add("tensor", lambda e, cch=cch, tt=tt: e.transpose(
                        ps_trf[:, 0:64], st_acs[:, cch * CHUNK + tt * 128:cch * CHUNK + tt * 128 + 128],
                        c_eyef[0:64, 0:64]), after=[m_trf], sig=True)
                    m_trf = sch.add("vector", lambda e, cch=cch, tt=tt: e.tensor_scalar(
                        out=negacs[:, cch * 128 + tt * 64:cch * 128 + tt * 64 + 64],
                        in0=ps_trf[:, 0:64], scalar1=-1.0, scalar2=None, op0=ALU.mult),
                        after=[mt], sig=True)
            for cch in range(NCHUNK):
                tb = cch * CHUNK
                m_wcm = sch.add("scalar", lambda e, cch=cch, tb=tb: e.activation(
                    out=wcm[:, :], in_=st_acs[:, tb:tb + CHUNK], func=AF.Exp,
                    bias=st_wb[:, cch:cch + 1], scale=-1.0), after=[m_wu], sig=True)
                m_dcm = sch.add("scalar", lambda e, cch=cch, tb=tb: e.activation(
                    out=dcm[:, :], in_=st_acs[:, tb:tb + CHUNK], func=AF.Exp,
                    bias=st_alast[:, cch:cch + 1], scale=-1.0), after=[m_du], sig=True)
                for (src_t, m_src, dst, tag) in ((wcm, m_wcm, wtk_all, 0), (dcm, m_dcm, d1tk_all, 1)):
                    for tt in range(2):
                        mt = sch.add("tensor", lambda e, src_t=src_t, tt=tt: e.transpose(
                            ps_tr[:, :], src_t[:, tt * 128:(tt + 1) * 128], c_eye[0:64, 0:64]),
                            after=[m_src, m_tre], sig=True)
                        m_tre = sch.add("vector", lambda e, dst=dst, cch=cch, tt=tt: e.tensor_scalar(
                            out=dst[:, cch * 128 + tt * 64:cch * 128 + tt * 64 + 64],
                            in0=ps_tr[:, :], scalar1=1.0, scalar2=None, op0=ALU.mult),
                            after=[mt], sig=True)
                    if tag == 0:
                        m_wu = m_tre
                    else:
                        m_du = m_tre
            sch.emit(block)

        sch.barrier()
        with contextlib.ExitStack() as ph:
            xtk = [ph.enter_context(nc.sbuf_tensor(f"p2_x{i}", [128, 2 * D_INNER], BF16)) for i in range(2)]
            btk = [ph.enter_context(nc.sbuf_tensor(f"p2_b{i}", [128, 2 * GN], BF16)) for i in range(2)]
            dxw = [ph.enter_context(nc.sbuf_tensor(f"p2_dx{i}", [128, 2 * D_INNER], BF16)) for i in range(2)]
            skst = ph.enter_context(nc.sbuf_tensor("p2_sk", [128, H * D_HEAD], F32))
            ps_sk = [ph.enter_context(nc.psum_tensor(f"p2_psk{i}", [128, 512], F32)) for i in range(8)]

            m_dxu = [None, None]
            m_mm = None
            for cch in range(NCHUNK):
                tb = cch * CHUNK
                sl2 = cch % 2
                lx, lb = [], []
                for tt in range(2):
                    lx.append(sch.dma("sync", xtk[sl2][:, tt * D_INNER:(tt + 1) * D_INNER],
                                      s_xbc[0:D_INNER, tb + tt * 128:tb + tt * 128 + 128],
                                      transpose=True, stream=f"p2x{sl2}",
                                      after=([m_dxu[sl2]] if tt == 0 else [])))
                    lb.append(sch.dma("scalar", btk[sl2][:, tt * GN:(tt + 1) * GN],
                                      s_xbc[D_INNER:D_INNER + GN, tb + tt * 128:tb + tt * 128 + 128],
                                      transpose=True, stream=f"p2b{sl2}",
                                      after=([m_mm] if tt == 0 else [])))
                m_dx = []
                for tt in range(2):
                    m_dx.append(sch.add("vector", lambda e, sl2=sl2, tt=tt, cch=cch: e.tensor_tensor(
                        out=dxw[sl2][:, tt * D_INNER:(tt + 1) * D_INNER],
                        in0=xtk[sl2][:, tt * D_INNER:(tt + 1) * D_INNER],
                        in1=bass.AP(wtk_all, cch * 128 + tt * 64, [[NCHUNK * 128, 128], [1, H], [0, D_HEAD]]),
                        op=ALU.mult), after=[lx[1], m_dxu[sl2]], sig=(tt == 1)))
                for h in range(H):
                    g = h // (H // G)
                    bank, col = h // 8, (h % 8) * 64
                    for tt in range(2):
                        m_mm = sch.add("tensor", lambda e, sl2=sl2, tt=tt, h=h, g=g, bank=bank, col=col, cch=cch: e.matmul(
                            ps_sk[bank][:, col:col + 64],
                            btk[sl2][:, tt * GN + g * 128:tt * GN + g * 128 + 128],
                            dxw[sl2][:, tt * D_INNER + h * 64:tt * D_INNER + h * 64 + 64],
                            start=(cch == 0 and tt == 0 and h % 8 == 0),
                            stop=(cch == NCHUNK - 1 and tt == 1 and h % 8 == 7)),
                            after=([m_dx[1], lb[1]] if (h == 0 and tt == 0) else []),
                            sig=(h == H - 1 and tt == 1))
                m_dxu[sl2] = m_mm
            m_ev = []
            for bank in range(8):
                m_ev.append(sch.add("vector", lambda e, bank=bank: e.tensor_scalar(
                    out=skst[:, bank * 512:(bank + 1) * 512], in0=ps_sk[bank][:, :],
                    scalar1=1.0, scalar2=None, op0=ALU.mult), after=[m_mm], sig=True))
            m_sk = sch.dma("gpsimd",
                           bass.AP(ag_src_t, 0, [[64, 128], [128 * 64, H], [1, 64]]),
                           skst[:, :], stream="agd", after=[m_ev[-1]] + m_ev[:-1])
            agd_full = ctx.full("agd")

            def do_cc(e):
                rg = ([[0, 1, 2, 3]] if os.environ.get("K_CC4") else [[0, 1, 2, 3], [4, 5, 6, 7]])
                instr = e.collective_compute(
                    "AllGather", ALU.bypass,
                    replica_groups=rg,
                    ins=[ag_src.opt()], outs=[ag_dst.opt()])
                instr.then_inc(ctx.sems["cc"], 1)
                return instr
            sch.add("gpsimd", do_cc, after=[agd_full])
            ctx.cnt["cc"] += 1
            sch.emit(block)
        mstackA.close()

        sch.barrier()
        with contextlib.ExitStack() as ph:
            xnT = ph.enter_context(nc.sbuf_tensor("p4_xnT", [128, NKB * 2048], BF16))
            wg = [ph.enter_context(nc.sbuf_tensor(f"p4_wg{i}", [128, NKB * 128], BF16)) for i in range(2)]
            wu = [ph.enter_context(nc.sbuf_tensor(f"p4_wu{i}", [128, NKB * 128], BF16)) for i in range(2)]

            sg = [ph.enter_context(nc.sbuf_tensor(f"p4_sg{i}", [128, 2048], BF16)) for i in range(2)]
            hd = [ph.enter_context(nc.sbuf_tensor(f"p4_hd{i}", [128, 2048], BF16)) for i in range(2)]
            ps_g = ph.enter_context(nc.psum_tensor("p4_pg", [128, 2048], F32))
            ps_u = ph.enter_context(nc.psum_tensor("p4_pu", [128, 2048], F32))

            m_wgu = [None, None]
            m_wuu = [None, None]
            m_sil_t = [None] * 4
            m_hdn_t4 = [None] * 4
            m_sg_u = [None, None]
            m_hdn_last = None
            for th in range(2):
                tbb = th * 2048
                prevmm = [m for m in (m_wgu + m_wuu) if m]
                m_xn = [sch.dma("sync", xnT[:, k * 2048:(k + 1) * 2048],
                                s_xn[tbb:tbb + 2048, k * 128:(k + 1) * 128], transpose=True,
                                stream="p4xn", after=(prevmm if k == 0 else []) +
                                ([ctx.full("p0st0"), ctx.full("p0st1"), ctx.full("p0st2")] if th == 0 else []))
                       for k in range(NKB)]
                xn_full = ctx.full("p4xn")
                for ib in range(NIB):
                    sl = ib % 2
                    lg = sch.dma("gpsimd", wg[sl][:, :],
                                 bass.AP(gate_wT.tensor, ib * 128,
                                         [[INTER, 128], [128 * INTER, NKB], [1, 128]]),
                                 stream=f"p4wg{sl}", after=[m_wgu[sl]])
                    lu = sch.dma("gpsimd", wu[sl][:, :],
                                 bass.AP(up_wT.tensor, ib * 128,
                                         [[INTER, 128], [128 * INTER, NKB], [1, 128]]),
                                 stream=f"p4wu{sl}", after=[m_wuu[sl]])
                    mmg_t = [None] * 4
                    sil_t = [None] * 4
                    for tt in range(4):
                        for k in range(NKB):
                            mmg_t[tt] = sch.add("tensor", lambda e, sl=sl, k=k, tt=tt: e.matmul(
                                ps_g[:, tt * 512:(tt + 1) * 512], wg[sl][:, k * 128:(k + 1) * 128],
                                xnT[:, k * 2048 + tt * 512:k * 2048 + tt * 512 + 512],
                                start=(k == 0), stop=(k == NKB - 1)),
                                after=(([lg, xn_full] if tt == 0 else []) +
                                       ([m_sil_t[tt]] if k == 0 else [])),
                                sig=(k == NKB - 1))
                        sil_t[tt] = sch.add("scalar", lambda e, sl=sl, ib=ib, tt=tt: e.activation(
                            out=sg[sl][:, tt * 512:(tt + 1) * 512], in_=ps_g[:, tt * 512:(tt + 1) * 512],
                            func=AF.Silu, scale=c_gs[:, ib:ib + 1]),
                            after=[mmg_t[tt], m_sg_u[sl]], sig=True)
                        m_sil_t[tt] = sil_t[tt]
                    m_wgu[sl] = mmg_t[3]
                    mmu_t = [None] * 4
                    hdn_t = [None] * 4
                    for tt in range(4):
                        for k in range(NKB):
                            mmu_t[tt] = sch.add("tensor", lambda e, sl=sl, k=k, tt=tt: e.matmul(
                                ps_u[:, tt * 512:(tt + 1) * 512], wu[sl][:, k * 128:(k + 1) * 128],
                                xnT[:, k * 2048 + tt * 512:k * 2048 + tt * 512 + 512],
                                start=(k == 0), stop=(k == NKB - 1)),
                                after=(([lu] if tt == 0 else []) +
                                       ([m_hdn_t4[tt]] if k == 0 else [])),
                                sig=(k == NKB - 1))
                        hdn_t[tt] = sch.add("vector", lambda e, sl=sl, ib=ib, tt=tt: e.scalar_tensor_tensor(
                            out=hd[sl][:, tt * 512:(tt + 1) * 512], in0=ps_u[:, tt * 512:(tt + 1) * 512],
                            scalar=c_us[:, ib:ib + 1], in1=sg[sl][:, tt * 512:(tt + 1) * 512],
                            op0=ALU.mult, op1=ALU.mult),
                            after=[mmu_t[tt], sil_t[tt]] + ([ctx.full(f"p4hd{sl}")] if tt == 0 else []),
                            sig=True)
                        m_hdn_t4[tt] = hdn_t[tt]
                    m_wuu[sl] = mmu_t[3]
                    m_sg_u[sl] = hdn_t[3]
                    m_hdn_last = hdn_t[3]
                    sch.dma("scalar", s_hdn[ib * 128:(ib + 1) * 128, tbb:tbb + 2048],
                            hd[sl][:, :], stream=f"p4hd{sl}", after=list(hdn_t))
                    if dbg_level == 3:
                        sch.dma("scalar", dbg["hdn"][ib * 128:(ib + 1) * 128, tbb:tbb + 2048],
                                hd[sl][:, :], stream=f"p4hd{sl}", after=list(hdn_t))
            sch.emit(block)

        sch.barrier()
        with contextlib.ExitStack() as ph:
            sS = [ph.enter_context(nc.sbuf_tensor(f"p3_S{i}", [128, H * D_HEAD], F32)) for i in range(2)]
            drep = ph.enter_context(nc.sbuf_tensor("p3_drep", [128, 192], F32))
            dt3 = ph.enter_context(nc.sbuf_tensor("p3_dt3", [H, 8], F32))
            dflat = ph.enter_context(nc.sbuf_tensor("p3_dfl", [1, 256], F32))
            ps_dr = ph.enter_context(nc.psum_tensor("p3_pdr", [128, 192], F32))

            cc_ms = ("cc", 1)
            l0 = sch.dma("sync", sS[0][:, :],
                         bass.AP(ag_dst_t, 0, [[64, 128], [128 * 64, H], [1, 64]]),
                         stream="p3s0", after=[cc_ms])
            m_h = sch.add("vector", lambda e: e.tensor_scalar(
                out=hA[:, :], in0=sS[0][:, :], scalar1=c_rmask[:, 0:1], scalar2=None,
                op0=ALU.mult), after=[l0], sig=True)
            if dbg_level in (2, 3):
                sch.dma("gpsimd", dbg["agd"], ag_dst, stream="dagd", after=[cc_ms])
            ldd = None
            for r_ in range(4):
                ldd = sch.dma("scalar", dt3[:, 4 + r_:5 + r_],
                              bass.AP(ag_dst_t, (r_ * (H * NSTATE + H) + H * NSTATE) * 64,
                                      [[1, H], [1, 1]]),
                              stream="p3dd", after=([cc_ms] if r_ == 0 else []))
            m_d3 = None
            for m in range(1, 4):
                m_d3 = sch.add("vector", lambda e, m=m: e.tensor_scalar(
                    out=dt3[:, m - 1:m], in0=dt3[:, 4 + m:5 + m],
                    scalar1=c_rmask[0:64, m:m + 1], scalar2=c_rmask1[0:64, m:m + 1],
                    op0=ALU.mult, op1=ALU.add), after=[ldd] + ([m_d3] if m_d3 else []), sig=True)
            m_df = sch.dma("sync", dflat[0:1, 0:192], dt3[:, 0:3], stream="p3df", after=[m_d3])
            m_dr1 = sch.add("tensor", lambda e: e.matmul(
                ps_dr[:, :], c_onesf[:, :], dflat[0:1, 0:192], start=True, stop=True),
                after=[m_df], sig=True)
            m_dr = sch.add("vector", lambda e: e.tensor_scalar(
                out=drep[:, :], in0=ps_dr[:, :], scalar1=1.0, scalar2=None, op0=ALU.mult),
                after=[m_dr1], sig=True)
            hcur, hnxt = hA, hB
            m_hready = [m_h]
            for m in range(1, 4):
                lS = sch.dma("sync", sS[m % 2][:, :],
                             bass.AP(ag_dst_t, (m * 8256) * 64, [[64, 128], [128 * 64, H], [1, 64]]),
                             stream=f"p3s{m % 2}", after=[cc_ms] if m == 1 else [m_hready[-1]])
                m_msk = sch.add("vector", lambda e, m=m: e.tensor_scalar(
                    out=sS[m % 2][:, :], in0=sS[m % 2][:, :], scalar1=c_rmask[:, m:m + 1],
                    scalar2=None, op0=ALU.mult), after=[lS], sig=True)
                m_last = None
                for h in range(H):
                    m_last = sch.add("vector", lambda e, m=m, h=h, hcur=hcur, hnxt=hnxt: e.scalar_tensor_tensor(
                        out=hnxt[:, h * 64:(h + 1) * 64], in0=hcur[:, h * 64:(h + 1) * 64],
                        scalar=drep[:, (h * 3 + m - 1):(h * 3 + m)], in1=sS[m % 2][:, h * 64:(h + 1) * 64],
                        op0=ALU.mult, op1=ALU.add),
                        after=[m_msk, m_dr] + m_hready, sig=(h == H - 1))
                m_hready = [m_last]
                hcur, hnxt = hnxt, hcur
            m_hcomb = m_hready[0]

            xtk3 = [ph.enter_context(nc.sbuf_tensor("p3_x0", [128, 2 * D_INNER], BF16))] * 2
            btk3 = [ph.enter_context(nc.sbuf_tensor(f"p3_b{i}", [128, 2 * GN], BF16)) for i in range(2)]
            bcm = [ph.enter_context(nc.sbuf_tensor("p3_bc0", [128, 2048], BF16))] * 2
            ccm = [ph.enter_context(nc.sbuf_tensor("p3_cc0", [128, 2048], BF16))] * 2
            gcm = ph.enter_context(nc.sbuf_tensor("p3_g", [128, 8192], BF16))
            ych = ph.enter_context(nc.sbuf_tensor("p3_y", [128, 8192], BF16))
            ybuf = ph.enter_context(nc.sbuf_tensor("p3_yb", [128, 8192], BF16))
            flat = ph.enter_context(nc.sbuf_tensor("p3_flat", [3, 32 * CHUNK], BF16))
            cbm = ph.enter_context(nc.sbuf_tensor("p3_cbm", [128, G * 384], BF16))
            psb = [ph.enter_context(nc.sbuf_tensor(f"p3_p{i}", [128, 384], BF16)) for i in range(4)]
            ctl = [ph.enter_context(nc.sbuf_tensor(f"p3_ct{i}", [128, 256], BF16)) for i in range(4)]
            dxs = [ph.enter_context(nc.sbuf_tensor(f"p3_dx{i}", [128, 128], BF16)) for i in range(4)]
            grf = ph.enter_context(nc.sbuf_tensor("p3_grf", [128, 256], F32))
            grb = ph.enter_context(nc.sbuf_tensor("p3_grb", [128, 256], BF16))
            ps_d0 = ph.enter_context(nc.psum_tensor("p3_psd0", [128, 512], F32))
            ps_d1 = ph.enter_context(nc.psum_tensor("p3_psd1", [128, 512], F32))
            psd_of = lambda par: (ps_d0 if par < 2 else ps_d1)
            ps_mt = ph.enter_context(nc.sbuf_tensor("p3_mtsb", [128, 4 * 384], BF16))
            ps_et = ph.enter_context(nc.sbuf_tensor("p3_etsb", [128, 4 * 256], BF16))
            ps_y = [ph.enter_context(nc.psum_tensor(f"p3_psy{i}", [128, 512], F32))[:, 0:256] for i in range(2)]
            ps_s = [ph.enter_context(nc.psum_tensor(f"p3_pss{i}", [128, 512], F32))[:, 0:256] for i in range(2)]

            m_xu = [None, None]
            m_bu = [None, None]
            m_bcu = [None, None]
            m_ccu = [None, None]
            m_gu = None
            m_ychu = None
            m_flatu = None
            m_exps = [None] * 4
            m_pstt = [None] * 4
            m_ctt = [None] * 4
            m_yev = [None, None]
            m_hstt = [None, None]
            m_dxu = [None] * 4
            m_ymm = [None, None]
            m_smm = [None, None]
            m_cbev = None
            m_ygdma = None
            m_rmsmm = None
            m_grb = None
            for cch in range(NCHUNK):
                tb = cch * CHUNK
                sl2 = cch % 2
                lx = [sch.dma("sync", xtk3[sl2][:, tt * D_INNER:(tt + 1) * D_INNER],
                              s_xbc[0:D_INNER, tb + tt * 128:tb + tt * 128 + 128],
                              transpose=True, stream="p3x0",
                              after=([m_xu[0], m_xu[1]] if tt == 0 else [])) for tt in range(2)]
                lb = [sch.dma("scalar", btk3[sl2][:, tt * GN:(tt + 1) * GN],
                              s_xbc[D_INNER:D_INNER + GN, tb + tt * 128:tb + tt * 128 + 128],
                              transpose=True, stream=f"p3b{sl2}",
                              after=([m_bu[sl2]] if tt == 0 else [])) for tt in range(2)]
                lbc = sch.dma("sync", bcm[sl2][:, :],
                              bass.AP(s_xbc.tensor, D_INNER * TCORE + tb,
                                      [[TCORE, 128], [128 * TCORE, G], [1, CHUNK]]),
                              stream=f"p3bc{sl2}", after=[m_bcu[sl2]])
                lcc = sch.dma("scalar", ccm[sl2][:, :],
                              bass.AP(s_xbc.tensor, (D_INNER + GN) * TCORE + tb,
                                      [[TCORE, 128], [128 * TCORE, G], [1, CHUNK]]),
                              stream="p3cc0",
                              after=[m_ccu[0], m_ccu[1]] + list(m_ctt))
                lg = sch.dma("sync", gcm[:, :],
                             bass.AP(s_gate.tensor, tb,
                                     [[TCORE, 128], [128 * TCORE, NYB], [1, CHUNK]]),
                             stream="p3g", after=[m_gu])
                flat_ms = [None, None]
                m_cb = None
                for g in range(G):
                    mm1 = sch.add("tensor", lambda e, g=g, sl2=sl2: e.matmul(
                        ps_d0[:, 0:256], bcm[sl2][:, g * 256:g * 256 + 128],
                        ccm[sl2][:, g * 256:(g + 1) * 256], start=True, stop=True),
                        after=([lbc, lcc, m_exps[0], m_exps[1], m_cbev] if g == 0 else [m_cb]),
                        sig=False)
                    mm2 = sch.add("tensor", lambda e, g=g, sl2=sl2: e.matmul(
                        ps_d0[:, 256:384], bcm[sl2][:, g * 256 + 128:g * 256 + 256],
                        ccm[sl2][:, g * 256 + 128:(g + 1) * 256], start=True, stop=True),
                        sig=True)
                    m_cb = sch.add("vector", lambda e, g=g: e.tensor_tensor(
                        out=cbm[:, g * 384:(g + 1) * 384], in0=ps_d0[:, 0:384],
                        in1=c_cbmask[:, :], op=ALU.mult), after=[mm2, m_pstt[0], m_pstt[1]], sig=True)
                m_cbev = m_cb
                m_bcu[sl2] = mm2
                m_ccu[sl2] = mm2
                for h in range(H):
                    g = h // (H // G)
                    par = h % 4
                    jpar = (h // 2) % 2
                    hh = h % 32
                    if hh == 0:
                        half32 = h // 32
                        hrows = slice(half32 * 32, half32 * 32 + 32)
                        sch.dma("gpsimd", flat[0:1, :], st_hi[hrows, tb:tb + CHUNK],
                                stream="p3f", after=[m_flatu])
                        sch.dma("gpsimd", flat[1:2, :], st_lo[hrows, tb:tb + CHUNK],
                                stream="p3f")
                        flat_full = ctx.full("p3f")
                    m_r = sch.add("tensor", lambda e, h=h, par=par, hh=hh: e.matmul(
                        psd_of(par)[:, (par % 2) * 256:(par % 2) * 256 + 256], c_ones8[0:2, :],
                        flat[0:2, hh * 256:(hh + 1) * 256], start=True, stop=True),
                        after=[flat_full, m_exps[par], m_cbev if h < 4 else None], sig=True)
                    if hh == 31:
                        m_flatu = m_r
                    base = par * 384
                    m_mt0 = sch.add("scalar", lambda e, h=h, par=par, cch=cch, base=base: e.activation(
                        out=ps_mt[:, base:base + 256], in_=psd_of(par)[:, (par % 2) * 256:(par % 2) * 256 + 256],
                        func=AF.Exp, bias=negacs[:, cch * 128 + h:cch * 128 + h + 1]),
                        after=[m_r, m_pstt[par]], sig=False)
                    m_mt1 = sch.add("scalar", lambda e, h=h, par=par, cch=cch, base=base: e.activation(
                        out=ps_mt[:, base + 256:base + 384],
                        in_=psd_of(par)[:, (par % 2) * 256 + 128:(par % 2) * 256 + 256],
                        func=AF.Exp, bias=negacs[:, cch * 128 + 64 + h:cch * 128 + 64 + h + 1]),
                        sig=False)
                    m_et = sch.add("scalar", lambda e, h=h, par=par: e.activation(
                        out=ps_et[:, par * 256:par * 256 + 256],
                        in_=psd_of(par)[:, (par % 2) * 256:(par % 2) * 256 + 256], func=AF.Exp),
                        after=[m_ctt[par]], sig=True)
                    m_exps[par] = m_et
                    m_p = sch.add("vector", lambda e, par=par, g=g, base=base: e.scalar_tensor_tensor(
                        out=psb[par][:, :], in0=ps_mt[:, base:base + 384], scalar=1.0,
                        in1=cbm[:, g * 384:(g + 1) * 384], op0=ALU.min, op1=ALU.mult),
                        after=[m_et, m_ymm[jpar]], sig=True)
                    m_pstt[par] = m_p
                    m_ct = sch.add("gpsimd", lambda e, par=par, g=g, sl2=sl2: e.tensor_tensor(
                        out=ctl[par][:, :], in0=ps_et[:, par * 256:par * 256 + 256],
                        in1=ccm[sl2][:, g * 256:(g + 1) * 256], op=ALU.mult),
                        after=[m_et, m_ymm[jpar]], sig=True)
                    m_ctt[par] = m_ct
                    m_dx = []
                    for half in range(2):
                        m_dx.append(sch.add("vector", lambda e, sl2=sl2, h=h, par=par, half=half, cch=cch: e.tensor_scalar(
                            out=dxs[par][:, half * 64:(half + 1) * 64],
                            in0=xtk3[sl2][:, half * D_INNER + h * 64:half * D_INNER + h * 64 + 64],
                            scalar1=d1tk_all[:, cch * 128 + half * 64 + h:cch * 128 + half * 64 + h + 1],
                            scalar2=None, op0=ALU.mult),
                            after=[lx[1]] + ([m_dxu[par]] if half == 0 else []), sig=True))
                    r0_ = (h % 2) * 64
                    ymm1 = sch.add("tensor", lambda e, sl2=sl2, h=h, par=par, jpar=jpar, r0_=r0_: e.matmul(
                        ps_y[jpar][r0_:r0_ + 64, 0:256],
                        xtk3[sl2][:, h * 64:h * 64 + 64], psb[par][:, 0:256],
                        start=True, stop=False),
                        after=[m_p, m_yev[jpar] if h % 2 == 0 else None,
                               lx[1] if h == 0 else None], sig=False)
                    ymm2 = sch.add("tensor", lambda e, sl2=sl2, h=h, par=par, jpar=jpar, r0_=r0_: e.matmul(
                        ps_y[jpar][r0_:r0_ + 64, 128:256],
                        xtk3[sl2][:, D_INNER + h * 64:D_INNER + h * 64 + 64], psb[par][:, 256:384],
                        start=False, stop=False), sig=False)
                    hc = hcur
                    ymm3 = sch.add("tensor", lambda e, h=h, par=par, jpar=jpar, r0_=r0_, hc=hc: e.matmul(
                        ps_y[jpar][r0_:r0_ + 64, 0:256],
                        hc[:, h * 64:(h + 1) * 64], ctl[par][:, :],
                        start=False, stop=True),
                        after=[m_ct, m_hcomb if cch == 0 and h < 2 else None], sig=True)
                    m_ymm[jpar] = ymm3
                    spar = h % 2
                    smm = None
                    for half in range(2):
                        smm = sch.add("tensor", lambda e, sl2=sl2, h=h, par=par, g=g, half=half, spar=spar: e.matmul(
                            ps_s[spar][:, 0:64],
                            btk3[sl2][:, half * GN + g * 128:half * GN + g * 128 + 128],
                            dxs[par][:, half * 64:(half + 1) * 64],
                            start=(half == 0), stop=(half == 1)),
                            after=([m_dx[1], lb[1], m_hstt[spar], m_rmsmm if spar == 0 else None]
                                   if half == 0 else []), sig=(half == 1))
                    m_smm[spar] = smm
                    m_dxu[par] = smm
                    m_hstt[spar] = sch.add("vector", lambda e, h=h, spar=spar, cch=cch, hc=hcur, hn=hnxt: e.scalar_tensor_tensor(
                        out=hn[:, h * 64:(h + 1) * 64], in0=hc[:, h * 64:(h + 1) * 64],
                        scalar=st_cdrep[:, h * NCHUNK + cch:h * NCHUNK + cch + 1],
                        in1=ps_s[spar][:, 0:64], op0=ALU.mult, op1=ALU.add),
                        after=[smm] + ([m_hcomb] if cch == 0 and h < 2 else []), sig=True)
                    if h % 2 == 1:
                        jj = h // 2
                        m_yev[jpar] = sch.add("scalar", lambda e, jpar=jpar, jj=jj: e.activation(
                            out=ych[:, jj * 256:(jj + 1) * 256], in_=ps_y[jpar][:, :],
                            func=AF.Copy), after=[ymm3, m_ygdma], sig=True)
                m_xu[sl2] = smm
                m_bu[sl2] = smm
                hcur, hnxt = hnxt, hcur
                m_sq = sch.add("scalar", lambda e: e.activation(
                    out=ybuf[:, :], in_=ych[:, :], func=AF.Square),
                    after=[m_yev[0], m_yev[1], m_ygdma], sig=True)
                mrms = None
                for jb in range(NYB):
                    mrms = sch.add("tensor", lambda e, jb=jb: e.matmul(
                        ps_s[0][:, :], c_onesT[:, :], ybuf[:, jb * 256:(jb + 1) * 256],
                        start=(jb == 0), stop=(jb == NYB - 1)),
                        after=([m_sq, m_hstt[0], m_grb] if jb == 0 else []), sig=(jb == NYB - 1))
                m_rmsmm = mrms
                mg1 = sch.add("vector", lambda e: e.tensor_scalar(
                    out=grf[:, :], in0=ps_s[0][:, :], scalar1=1.0 / D_INNER, scalar2=1e-6,
                    op0=ALU.mult, op1=ALU.add), after=[mrms], sig=True)
                mg2 = sch.add("vector", lambda e: e.reciprocal(out=grf[:, :], in_=grf[:, :]),
                              after=[mg1], sig=True)
                mg3 = sch.add("scalar", lambda e: e.activation(out=grf[:, :], in_=grf[:, :],
                              func=AF.Sqrt), after=[mg2], sig=True)
                m_grb = sch.add("vector", lambda e: e.tensor_scalar(
                    out=grb[:, :], in0=grf[:, :], scalar1=1.0, scalar2=None, op0=ALU.mult),
                    after=[mg3], sig=True)
                m_sg = sch.add("scalar", lambda e: e.activation(
                    out=gcm[:, :], in_=gcm[:, :], func=AF.Silu), after=[lg, m_sq], sig=True)
                myg1 = sch.add("vector", lambda e: e.tensor_tensor(
                    out=ybuf[:, :], in0=ych[:, :],
                    in1=bass.AP(grb, 0, [[256, 128], [0, NYB], [1, 256]]),
                    op=ALU.mult), after=[m_grb, m_rmsmm], sig=True)
                myg2 = sch.add("vector", lambda e: e.tensor_tensor(
                    out=ybuf[:, :], in0=ybuf[:, :], in1=gcm[:, :], op=ALU.mult),
                    after=[myg1, m_sg], sig=True)
                m_gu = myg2
                m_ychu = myg2
                myd = sch.dma("gpsimd",
                              bass.AP(s_yg.tensor, tb, [[TCORE, 128], [128 * TCORE, NYB], [1, CHUNK]]),
                              ybuf[:, :], stream="p3yg", after=[myg2])
                if dbg_level == 3:
                    myd = sch.dma("gpsimd",
                                  bass.AP(dbg["yg"].tensor, tb, [[TCORE, 128], [128 * TCORE, NYB], [1, CHUNK]]),
                                  ybuf[:, :], stream="p3yg", after=[myg2])
                m_ygdma = myd
            sch.emit(block)

        if dbg_level in (2, 3):
            fin = Sched(ctx)
            z = stack.enter_context(nc.sbuf_tensor("zz", [128, D_MODEL], F32))
            wl = [ctx.full(s) for s in ("p3yg", "p4hd0", "p4hd1", "dagd")] + ([ctx.full("dacs")] if dbg_level in (1,2,3) else [])
            ms = fin.add("vector", lambda e: e.memset(z[:, :], 0.0), after=wl, sig=True)
            for i in range(TCORE // 128):
                fin.dma("gpsimd", out[i * 128:(i + 1) * 128, :], z[:, :], stream="fino", after=[ms])
            fms = ctx.full("fino")
            fin.add("gpsimd", lambda e, fms=fms: e.wait_ge(ctx.sems[fms[0]], fms[1]))
            fin.emit(block)
            return nc

        mstack.close()
        sch.barrier()
        NKTOT = NYB + NIB
        with contextlib.ExitStack() as ph:
            c_dsrep = ph.enter_context(nc.sbuf_tensor("c_dsrep", [128, D_MODEL], F32))
            mdsl = sch.dma("sync", c_dsrep[:, :], ds_rep, stream="p5c")
            w5 = ph.enter_context(nc.sbuf_tensor("p5_w", [128, NKTOT * 1024], BF16))
            rhs5 = [ph.enter_context(nc.sbuf_tensor(f"p5_r{i}", [128, 512], BF16)) for i in range(3)]
            ost = [ph.enter_context(nc.sbuf_tensor(f"p5_o{i}", [128, 512], BF16)) for i in range(2)]
            ps5 = [ph.enter_context(nc.psum_tensor(f"p5_ps{i}", [128, 512], F32)) for i in range(8)]
            m_prev_half = None
            mm5 = None
            for dhalf in range(2):
                m_wsc = []
                for k in range(NKTOT):
                    if k < NYB:
                        ap = bass.AP(op_wT.tensor, k * 128 * D_MODEL + dhalf * 1024,
                                     [[D_MODEL, 128], [1, 1024]])
                    else:
                        ap = bass.AP(down_wT.tensor, (k - NYB) * 128 * D_MODEL + dhalf * 1024,
                                     [[D_MODEL, 128], [1, 1024]])
                    lw = sch.dma("gpsimd", w5[:, k * 1024:(k + 1) * 1024], ap,
                                 stream=f"p5w{k % 2}",
                                 after=([m_prev_half] if k == 0 else []) +
                                       ([m_wsc[k - 2]] if k >= 2 else []))
                    if k < NYB:
                        m_wsc.append(sch.add("vector", lambda e, k=k: e.tensor_scalar(
                            out=w5[:, k * 1024:(k + 1) * 1024], in0=w5[:, k * 1024:(k + 1) * 1024],
                            scalar1=c_won[:, k:k + 1], scalar2=None, op0=ALU.mult),
                            after=[lw], sig=True))
                    else:
                        m_wsc.append(sch.add("vector", lambda e, k=k, dhalf=dhalf: e.tensor_tensor(
                            out=w5[:, k * 1024:(k + 1) * 1024], in0=w5[:, k * 1024:(k + 1) * 1024],
                            in1=c_dsrep[:, dhalf * 1024:(dhalf + 1) * 1024], op=ALU.mult),
                            after=[lw] + ([mdsl] if k == NYB else []), sig=True))
                wsc_ms = m_wsc[-1]
                m_ev5 = [None] * 8
                mm_slot = {0: None, 1: None, 2: None}
                for ttile in range(8):
                    for k in range(NKTOT):
                        if k < NYB:
                            rap = s_yg[k * 128:(k + 1) * 128, ttile * 512:(ttile + 1) * 512]
                        else:
                            rap = s_hdn[(k - NYB) * 128:(k - NYB + 1) * 128, ttile * 512:(ttile + 1) * 512]
                        rsl = k % 3
                        lr = sch.dma("sync", rhs5[rsl][:, :], rap, stream=f"p5r{rsl}",
                                     after=[mm_slot[rsl] or mm5, ctx.full(f"p5r{rsl}")])
                        for d in range(8):
                            mm5 = sch.add("tensor", lambda e, k=k, d=d, rsl=rsl: e.matmul(
                                ps5[d][:, :], w5[:, k * 1024 + d * 128:k * 1024 + d * 128 + 128],
                                rhs5[rsl][:, :], start=(k == 0), stop=(k == NKTOT - 1)),
                                after=([lr, m_wsc[k]] + (m_ev5 if k == 0 else []) if d == 0 else []),
                                sig=(d == 7))
                        mm_slot[rsl] = mm5
                    for d in range(8):
                        osl = d % 2
                        mev = sch.add("vector", lambda e, d=d, osl=osl: e.tensor_scalar(
                            out=ost[osl][:, :], in0=ps5[d][:, :], scalar1=1.0, scalar2=None,
                            op0=ALU.mult), after=[mm5, ctx.full(f"p5o{osl}")], sig=True)
                        m_ev5[d] = mev
                        sch.dma("scalar", s_inner[(dhalf * 8 + d) * 128:(dhalf * 8 + d) * 128 + 128,
                                                  ttile * 512:(ttile + 1) * 512],
                                ost[osl][:, :], stream=f"p5o{osl}", after=[mev])
                m_prev_half = mm5
            sch.emit(block)

        sch.barrier()
        with contextlib.ExitStack() as ph:
            it6 = [ph.enter_context(nc.sbuf_tensor(f"p6_i{i}", [128, D_MODEL], BF16)) for i in range(2)]
            r06 = [ph.enter_context(nc.sbuf_tensor(f"p6_r{i}", [128, D_MODEL], F32)) for i in range(2)]
            ot6 = [ph.enter_context(nc.sbuf_tensor(f"p6_o{i}", [128, D_MODEL], F32)) for i in range(2)]
            m_add = [None, None]
            for i in range(TCORE // 128):
                sl = i % 2
                li = sch.dma("sync", it6[sl][:, :],
                             s_inner[0:D_MODEL, i * 128:(i + 1) * 128], transpose=True,
                             stream=f"p6i{sl}", after=[m_add[sl]])
                lr = sch.dma("scalar", r06[sl][:, :], s_r0[i * 128:(i + 1) * 128, :],
                             stream=f"p6r{sl}", after=[m_add[sl]])
                ma = sch.add("vector", lambda e, sl=sl: e.tensor_tensor(
                    out=ot6[sl][:, :], in0=r06[sl][:, :], in1=it6[sl][:, :], op=ALU.add),
                    after=[li, lr, ctx.full(f"p6o{sl}")], sig=True)
                m_add[sl] = ma
                sch.dma("gpsimd", out[i * 128:(i + 1) * 128, :], ot6[sl][:, :],
                        stream=f"p6o{sl}", after=[ma])
            for slx in range(2):
                fms = ctx.full(f"p6o{slx}")
                sch.add("gpsimd", lambda e, fms=fms: e.wait_ge(ctx.sems[fms[0]], fms[1]))
            sch.emit(block)


    return nc



def _prep_inputs(inputs):
    x = inputs["L0_attn_x"]
    W = inputs
    ip_w = W["L0_attn_in_proj_w"]
    ip_wT_full = np.ascontiguousarray(ip_w.T)
    ip_wT = np.zeros((D_MODEL, IP_COLS), np.float32)
    ip_wT[:, 0:4096] = ip_wT_full[:, 0:4096]
    ip_wT[:, 4096:5120] = ip_wT_full[:, 4160:5184]
    ip_wT[:, 5120:6144] = ip_wT_full[:, 5184:6208]
    ip_wT[:, 6208:6272] = ip_wT_full[:, 4096:4160]
    ip_wT[:, 6272:10368] = ip_wT_full[:, 6208:10304]

    convw_r = np.ascontiguousarray(
        W["L0_attn_conv_w"][:, 0, :].reshape(NCB_XBC, 128, 4).transpose(1, 0, 2).reshape(128, NCB_XBC * 4))
    convb_r = np.ascontiguousarray(W["L0_attn_conv_b"].reshape(NCB_XBC, 128).T)
    cnw_r = np.ascontiguousarray(W["L0_attn_conv_norm_w"].reshape(NCB_XBC, 128).T)
    won_r = np.ascontiguousarray(W["L0_attn_out_norm_w"].reshape(NYB, 128).T)
    op_wT = np.ascontiguousarray(W["L0_attn_out_proj_w"].T)
    gate_wT = np.ascontiguousarray(W["L0_ffn_gate_w"].T)
    up_wT = np.ascontiguousarray(W["L0_ffn_up_w"].T)
    down_wT = np.ascontiguousarray(W["L0_ffn_down_w"].T)
    gs_r = np.ascontiguousarray(W["L0_ffn_gate_s"].reshape(NIB, 128).T)
    us_r = np.ascontiguousarray(W["L0_ffn_up_s"].reshape(NIB, 128).T)
    ds_rep = np.ascontiguousarray(np.broadcast_to(W["L0_ffn_down_s"], (128, D_MODEL)))
    nw_rep = np.ascontiguousarray(np.broadcast_to(W["L0_attn_norm_w"], (128, D_MODEL)))

    eye = np.eye(128, dtype=BF)
    ones8 = np.ones((8, 128), BF)
    onesT = np.ones((128, 128), BF)
    cbm = np.zeros((128, 384), BF)
    p = np.arange(128)[:, None]
    c = np.arange(256)[None, :]
    cbm[:, 0:256] = (c >= p).astype(BF)
    cbm[:, 256:384] = (c[:, :128] >= p).astype(BF)

    shared = {
        "ip_wT": ip_wT, "convw_r": convw_r, "convb_r": convb_r, "cnw_r": cnw_r,
        "a_log": W["L0_attn_A_log"].reshape(H, 1).astype(np.float32),
        "dt_b": W["L0_attn_dt_b"].reshape(H, 1).astype(np.float32),
        "dt_wT": np.ascontiguousarray(W["L0_attn_dt_w"].T),
        "won_r": won_r, "op_wT": op_wT,
        "gate_wT": gate_wT, "up_wT": up_wT, "down_wT": down_wT,
        "gs_r": gs_r, "us_r": us_r, "ds_rep": ds_rep, "nw_rep": nw_rep,
        "eye128": eye, "ones8": ones8, "onesT": onesT, "cb_mask": cbm,
        "ones1f": np.ones((1, 128), np.float32),
        "eye128f": np.eye(128, dtype=np.float32),
    }

    in_maps = []
    for core in range(8):
        b, q = core // 4, core % 4
        t0 = q * TCORE
        xh = np.zeros((TT, D_MODEL), np.float32)
        if q > 0:
            xh[:HALO] = x[b, t0 - HALO:t0]
        xh[HALO:] = x[b, t0:t0 + TCORE]
        rm = (np.arange(4)[None, :] < q).astype(np.float32)
        m = dict(shared)
        m["x_h"] = xh
        m["rmask"] = np.ascontiguousarray(np.broadcast_to(rm, (128, 4)))
        m["rmask1"] = np.ascontiguousarray(np.broadcast_to(1.0 - rm, (128, 4)))
        in_maps.append(m)
    return in_maps


def kernel(**inputs):
    dbg_level = inputs.pop("_dbg_level", 99)
    in_maps = _prep_inputs(inputs)
    nc = build_program(dbg_level=dbg_level)
    res = run_bass_kernel_spmd(nc, in_maps, core_ids=list(range(8)))
    outs = [res.results[c]["out"] for c in range(8)]
    full = np.empty((2, 16384, D_MODEL), np.float32)
    for core in range(8):
        b, q = core // 4, core % 4
        full[b, q * TCORE:(q + 1) * TCORE] = outs[core]
    if dbg_level < 99:
        kernel._dbg = res.results
    return full


# version 08 (final)
# speedup vs baseline: 1.0940x; 1.0940x over previous
import contextlib
import numpy as np
import ml_dtypes

import concourse.bass as bass
import concourse.mybir as mybir
from concourse.bass_utils import run_bass_kernel_spmd

F32 = mybir.dt.float32
BF16 = mybir.dt.bfloat16
AF = mybir.ActivationFunctionType
ALU = mybir.AluOpType
BF = ml_dtypes.bfloat16

D_MODEL = 2048
D_INNER = 4096
H = 64
D_HEAD = 64
NSTATE = 128
G = 8
CHUNK = 256
GN = G * NSTATE
CONV_DIM = D_INNER + 2 * GN
IP_COLS = 10368
INTER = 5632
TCORE = 4096
HALO = 128
TT = TCORE + HALO
NCHUNK = TCORE // CHUNK
NCB_XBC = CONV_DIM // 128
NCB_ALL = IP_COLS // 128
NKB = D_MODEL // 128
NIB = INTER // 128
NYB = D_INNER // 128

ENGINES = ("tensor", "vector", "scalar", "gpsimd", "sync")
DMA_QUEUES = ("sync", "scalar", "gpsimd", "vector")


class Sched:

    def __init__(self, ctx):
        self.ctx = ctx
        self.items = {e: [] for e in ENGINES}
        self.pending_barrier = {}

    def barrier(self):
        ms = [(n, c) for (n, c) in self.ctx.cnt.items() if c > 0]
        self.pending_barrier = {e: list(ms) for e in ENGINES}

    def add(self, eng, fn, after=(), sig=False, stream=None):
        ctx = self.ctx
        waits = []
        for m in after:
            if m is None:
                continue
            (sem, val) = m
            if sem == "e_" + eng:
                assert val <= ctx.cnt[sem], f"same-engine forward wait {sem} {val} > {ctx.cnt[sem]}"
            if sem.startswith("s_"):
                assert val == ctx.cnt[sem], \
                    f"unsound prefix wait on dma stream {sem}: {val} != {ctx.cnt[sem]}"
            waits.append((sem, val))
        pb = self.pending_barrier.pop(eng, None)
        if pb:
            seen = {s for (s, _) in waits}
            waits = [(s, v) for (s, v) in pb if s not in seen] + waits
        if eng in ("vector", "scalar") and stream is None:
            sig = True
        rec = {"fn": fn, "waits": waits, "inc": None}
        ms = None
        if stream is not None:
            sem = ctx.stream(stream)
            ctx.cnt[sem] += 16
            rec["inc"] = (sem, 16)
            ms = (sem, ctx.cnt[sem])
        elif sig:
            sem = "e_" + eng
            ctx.cnt[sem] += 1
            rec["inc"] = (sem, 1)
            ms = (sem, ctx.cnt[sem])
        self.items[eng].append(rec)
        return ms

    def dma(self, eng, out, in_, stream=None, after=(), **kw):
        assert stream is not None, "every dma needs a stream"
        return self.add(eng, lambda e: e.dma_start(out=out, in_=in_, **kw),
                        after=after, stream=stream)

    def emit(self, block):
        ctx = self.ctx
        for eng in ENGINES:
            recs = self.items[eng]
            if not recs:
                continue

            def make(recs):
                def body(e):
                    last_wait = {}
                    for r in recs:
                        for (sem, val) in r["waits"]:
                            if last_wait.get(sem, -1) >= val:
                                continue
                            last_wait[sem] = val
                            e.wait_ge(ctx.sems[sem], val)
                        instr = r["fn"](e)
                        if r["inc"] is not None:
                            sem, n = r["inc"]
                            assert instr is not None, "inc on op returning no instruction"
                            instr.then_inc(ctx.sems[sem], n)
                return body

            getattr(block, eng)(make(recs))
        self.items = {e: [] for e in ENGINES}


class Ctx:
    def __init__(self, nc):
        self.nc = nc
        self.sems = {}
        self.cnt = {}

    def make_sems(self, stack):
        self.stack = stack
        names = ["e_" + e for e in ENGINES] + ["cc"]
        for n in names:
            self.sems[n] = stack.enter_context(self.nc.semaphore(n))
            self.cnt[n] = 0

    def stream(self, name):
        n = "s_" + name
        if n not in self.sems:
            self.sems[n] = self.stack.enter_context(self.nc.semaphore(n))
            self.cnt[n] = 0
        return n

    def full(self, name):
        n = self.stream(name)
        return (n, self.cnt[n])


def ceil_div(a, b):
    return (a + b - 1) // b



def build_program(dbg_level=99):
    nc = bass.Bass("TRN2", target_bir_lowering=False)

    def din(name, shape, dt=F32):
        return nc.dram_tensor(name, shape, dt, kind="ExternalInput").ap()

    x_h = din("x_h", [TT, D_MODEL])
    ip_wT = din("ip_wT", [D_MODEL, IP_COLS])
    convw_r = din("convw_r", [128, NCB_XBC * 4])
    convb_r = din("convb_r", [128, NCB_XBC])
    cnw_r = din("cnw_r", [128, NCB_XBC])
    a_log = din("a_log", [H, 1])
    dt_b = din("dt_b", [H, 1])
    dt_wT = din("dt_wT", [H, H])
    won_r = din("won_r", [128, NYB])
    op_wT = din("op_wT", [D_INNER, D_MODEL])
    gate_wT = din("gate_wT", [D_MODEL, INTER])
    up_wT = din("up_wT", [D_MODEL, INTER])
    down_wT = din("down_wT", [INTER, D_MODEL])
    gs_r = din("gs_r", [128, NIB])
    us_r = din("us_r", [128, NIB])
    ds_rep = din("ds_rep", [128, D_MODEL])
    nw_rep = din("nw_rep", [128, D_MODEL])
    eye128 = din("eye128", [128, 128], BF16)
    ones8 = din("ones8", [8, 128], BF16)
    ones1f = din("ones1f", [1, 128])
    eye128f = din("eye128f", [128, 128])
    onesT = din("onesT", [128, 128], BF16)
    cb_mask = din("cb_mask", [128, 384], BF16)
    rmask = din("rmask", [128, 4])
    rmask1 = din("rmask1", [128, 4])

    out = nc.dram_tensor("out", [TCORE, D_MODEL], F32, kind="ExternalOutput").ap()

    s_xr = nc.dram_tensor("s_xr", [TT, D_MODEL], BF16).ap()
    s_xn = nc.dram_tensor("s_xn", [TCORE, D_MODEL], BF16).ap()
    s_r0 = nc.dram_tensor("s_r0", [TCORE, D_MODEL], F32).ap()
    s_co = nc.dram_tensor("s_co", [CONV_DIM, TCORE], BF16).ap()
    s_xbc = nc.dram_tensor("s_xbc", [CONV_DIM, TCORE], BF16).ap()
    s_gate = nc.dram_tensor("s_gate", [D_INNER, TCORE], BF16).ap()
    s_hdn = nc.dram_tensor("s_hdn", [INTER, TCORE], BF16).ap()
    s_yg = nc.dram_tensor("s_yg", [D_INNER, TCORE], BF16).ap()
    s_inner = nc.dram_tensor("s_inner", [D_MODEL, TCORE], BF16).ap()
    ag_src_t = nc.dram_tensor("ag_src", [H * NSTATE + H, D_HEAD], F32)
    ag_dst_t = nc.dram_tensor("ag_dst", [4 * (H * NSTATE + H), D_HEAD], F32)
    ag_src = ag_src_t.ap()
    ag_dst = ag_dst_t.ap()

    dbg = {}
    if dbg_level < 99:
        dbg["xn"] = nc.dram_tensor("d_xn", [TCORE, D_MODEL], BF16, kind="ExternalOutput").ap()
        dbg["xr"] = nc.dram_tensor("d_xr", [TT, D_MODEL], BF16, kind="ExternalOutput").ap()
        dbg["r0"] = nc.dram_tensor("d_r0", [TCORE, D_MODEL], F32, kind="ExternalOutput").ap()
    if dbg_level in (1, 2, 3):
        dbg["co"] = nc.dram_tensor("d_co", [CONV_DIM, TCORE], BF16, kind="ExternalOutput").ap()
        dbg["xbc"] = nc.dram_tensor("d_xbc", [CONV_DIM, TCORE], BF16, kind="ExternalOutput").ap()
        dbg["gate"] = nc.dram_tensor("d_gate", [D_INNER, TCORE], BF16, kind="ExternalOutput").ap()
        dbg["acs"] = nc.dram_tensor("d_acs", [H, TCORE], F32, kind="ExternalOutput").ap()
    if dbg_level in (2, 3):
        dbg["agd"] = nc.dram_tensor("d_agd", [4 * (H * NSTATE + H), D_HEAD], F32, kind="ExternalOutput").ap()
    if dbg_level == 3:
        dbg["yg"] = nc.dram_tensor("d_yg", [D_INNER, TCORE], BF16, kind="ExternalOutput").ap()
        dbg["hdn"] = nc.dram_tensor("d_hdn", [INTER, TCORE], BF16, kind="ExternalOutput").ap()

    ctx = Ctx(nc)

    with contextlib.ExitStack() as stack:
        ctx.make_sems(stack)
        block = stack.enter_context(nc.Block())
        sch = Sched(ctx)

        sb = lambda name, shape, dt=F32: stack.enter_context(nc.sbuf_tensor(name, shape, dt))
        c_convw = sb("c_convw", [128, NCB_XBC * 4])
        c_convb = sb("c_convb", [128, NCB_XBC])
        c_cnw = sb("c_cnw", [128, NCB_XBC])
        c_alog = sb("c_alog", [H, 1])
        c_dtb = sb("c_dtb", [H, 1])
        c_dtwT = sb("c_dtwT", [H, H], BF16)
        c_won = sb("c_won", [128, NYB])
        c_gs = sb("c_gs", [128, NIB])
        c_us = sb("c_us", [128, NIB])
        c_eye = sb("c_eye", [128, 128], BF16)
        c_ones8 = sb("c_ones8", [8, 128], BF16)
        c_onesf = sb("c_onesf", [1, 128])
        c_eyef = sb("c_eyef", [128, 128])
        c_onesT = sb("c_onesT", [128, 128], BF16)
        c_cbmask = sb("c_cbmask", [128, 384], BF16)
        c_rmask = sb("c_rmask", [128, 4])
        c_rmask1 = sb("c_rmask1", [128, 4])
        mstack = stack.enter_context(contextlib.ExitStack())
        msb = lambda name, shape, dt=F32: mstack.enter_context(nc.sbuf_tensor(name, shape, dt))
        mstackA = stack.enter_context(contextlib.ExitStack())
        st_hi = msb("st_hi", [H, TCORE], BF16)
        st_lo = msb("st_lo", [H, TCORE], BF16)
        st_alast = msb("st_alast", [H, NCHUNK])
        st_offs = msb("st_offs", [H, NCHUNK])
        st_wb = msb("st_wb", [H, NCHUNK])
        st_lend = msb("st_lend", [H, 1])
        st_cdrep = msb("st_cdrep", [128, NCHUNK * H])
        hA = msb("p3_hA", [128, H * D_HEAD], BF16)
        hB = msb("p3_hB", [128, H * D_HEAD], BF16)
        d1tk_all = msb("d1tk_all", [128, NCHUNK * 128], F32)
        negacs = msb("negacs", [128, NCHUNK * 128], F32)
        st_adt = mstackA.enter_context(nc.sbuf_tensor("st_adt", [H, TCORE], BF16))
        st_acs = mstackA.enter_context(nc.sbuf_tensor("st_acs", [H, TCORE], F32))
        wtk_all = mstackA.enter_context(nc.sbuf_tensor("wtk_all", [128, NCHUNK * 128], BF16))

        pre = [
            (c_convw, convw_r), (c_convb, convb_r), (c_cnw, cnw_r),
            (c_alog, a_log), (c_dtb, dt_b), (c_won, won_r),
            (c_gs, gs_r), (c_us, us_r),
            (c_eye, eye128), (c_ones8, ones8), (c_onesT, onesT), (c_onesf, ones1f),
            (c_eyef, eye128f),
            (c_cbmask, cb_mask), (c_rmask, rmask), (c_rmask1, rmask1),
        ]
        for (dst, srcap) in pre:
            sch.dma("sync", dst[:, :], srcap, stream="cld")
        sch.dma("gpsimd", c_dtwT[:, :], dt_wT, stream="cld2")
        ALL_CONSTS = [ctx.full("cld"), ctx.full("cld2")]

        NT0 = TT // 128
        import os
        NT0 = min(NT0, int(os.environ.get("K_NT0", "99")))
        with contextlib.ExitStack() as ph:
            c_nwrep = ph.enter_context(nc.sbuf_tensor("c_nwrep", [128, D_MODEL], F32))
            sch.dma("sync", c_nwrep[:, :], nw_rep, stream="cld")
            ALL_CONSTS = [ctx.full("cld"), ctx.full("cld2")]
            NB0 = 3
            xt = [ph.enter_context(nc.sbuf_tensor(f"p0_x{i}", [128, D_MODEL], F32)) for i in range(NB0)]
            xnb = [ph.enter_context(nc.sbuf_tensor(f"p0_xn{i}", [128, D_MODEL], BF16)) for i in range(NB0)]
            r0t = [ph.enter_context(nc.sbuf_tensor(f"p0_r0{i}", [128, D_MODEL], F32)) for i in range(NB0)]
            xrt = [ph.enter_context(nc.sbuf_tensor(f"p0_xr{i}", [128, D_MODEL], BF16)) for i in range(NB0)]
            bst = ph.enter_context(nc.sbuf_tensor("p0_bst", [128, 24], F32))
            mv = ph.enter_context(nc.sbuf_tensor("p0_mv", [128, 2], F32))
            tiny = ph.enter_context(nc.sbuf_tensor("p0_tiny", [128, 8], F32))
            last_use = [[] for _ in range(NB0)]
            for i in range(NT0):
                b = i % NB0
                ld = sch.dma("sync", xt[b][:, :], x_h[i * 128:(i + 1) * 128, :],
                             stream=f"p0ld{b}",
                             after=(ALL_CONSTS if i == 0 else []) + last_use[b])
                last_use[b] = []
                ms_bs = None
                for j in range(4):
                    ms_bs = sch.add("vector", lambda e, b=b, j=j: e.bn_stats(
                        out=bst[:, j * 6:(j + 1) * 6], in_=xt[b][:, j * 512:(j + 1) * 512]),
                        after=[ld], sig=True)
                ms_ag = sch.add("vector", lambda e: e.bn_aggr(out=mv[:, :], in_=bst[:, :]),
                                after=[ms_bs], sig=True)
                ms_e1 = sch.add("vector", lambda e: e.tensor_scalar(
                    out=tiny[:, 0:1], in0=mv[:, 1:2], scalar1=1e-5, scalar2=None, op0=ALU.add),
                    after=[ms_ag], sig=True)
                ms_r1 = sch.add("vector", lambda e: e.reciprocal(out=tiny[:, 1:2], in_=tiny[:, 0:1]),
                                after=[ms_e1], sig=True)
                ms_rstd = sch.add("scalar", lambda e: e.activation(
                    out=tiny[:, 2:3], in_=tiny[:, 1:2], func=AF.Sqrt), after=[ms_r1], sig=True)
                ms_fsq = sch.add("vector", lambda e: e.tensor_scalar(
                    out=tiny[:, 3:4], in0=mv[:, 1:2], scalar1=tiny[:, 1:2], scalar2=1e-6,
                    op0=ALU.mult, op1=ALU.add), after=[ms_r1], sig=True)
                ms_f0 = sch.add("vector", lambda e: e.reciprocal(out=tiny[:, 4:5], in_=tiny[:, 3:4]),
                                after=[ms_fsq], sig=True)
                ms_f = sch.add("scalar", lambda e: e.activation(
                    out=tiny[:, 4:5], in_=tiny[:, 4:5], func=AF.Sqrt), after=[ms_f0], sig=True)
                ms_nb = sch.add("vector", lambda e: e.tensor_scalar(
                    out=tiny[:, 5:6], in0=mv[:, 0:1], scalar1=tiny[:, 2:3], scalar2=-1.0,
                    op0=ALU.mult, op1=ALU.mult), after=[ms_rstd], sig=True)
                ms_r1p = sch.add("vector", lambda e: e.tensor_scalar(
                    out=tiny[:, 6:7], in0=tiny[:, 2:3], scalar1=1.0, scalar2=None, op0=ALU.add),
                    after=[ms_rstd], sig=True)
                ms_xn = sch.add("scalar", lambda e, b=b: e.activation(
                    out=xnb[b][:, :], in_=xt[b][:, :], func=AF.Identity,
                    bias=tiny[:, 5:6], scale=tiny[:, 2:3]), after=[ms_nb], sig=True)
                ms_r0 = None
                if i >= 1:
                    ms_r0 = sch.add("scalar", lambda e, b=b: e.activation(
                        out=r0t[b][:, :], in_=xt[b][:, :], func=AF.Identity,
                        bias=tiny[:, 5:6], scale=tiny[:, 6:7]), after=[ms_r1p, ms_nb], sig=True)
                ms_xr = sch.add("vector", lambda e, b=b: e.scalar_tensor_tensor(
                    out=xrt[b][:, :], in0=xnb[b][:, :], scalar=tiny[:, 4:5], in1=c_nwrep[:, :],
                    op0=ALU.mult, op1=ALU.mult), after=[ms_xn, ms_f], sig=True)
                sch.dma("gpsimd", s_xr[i * 128:(i + 1) * 128, :], xrt[b][:, :],
                        stream=f"p0xr{b}", after=[ms_xr])
                if dbg_level < 99:
                    sch.dma("gpsimd", dbg["xr"][i * 128:(i + 1) * 128, :], xrt[b][:, :],
                            stream=f"p0xr{b}", after=[ms_xr])
                if i >= 1:
                    t0 = (i - 1) * 128
                    sch.dma("scalar", s_xn[t0:t0 + 128, :], xnb[b][:, :],
                            stream=f"p0st{b}", after=[ms_xn])
                    sch.dma("scalar", s_r0[t0:t0 + 128, :], r0t[b][:, :],
                            stream=f"p0st{b}", after=[ms_r0])
                    if dbg_level < 99:
                        sch.dma("scalar", dbg["xn"][t0:t0 + 128, :], xnb[b][:, :],
                                stream=f"p0st{b}", after=[ms_xn])
                        sch.dma("scalar", dbg["r0"][t0:t0 + 128, :], r0t[b][:, :],
                                stream=f"p0st{b}", after=[ms_r0])
                last_use[b] = [ms_xr, ctx.full(f"p0xr{b}"), ctx.full(f"p0st{b}")]
            sch.emit(block)

        if dbg_level == 0:
            fin = Sched(ctx)
            z = stack.enter_context(nc.sbuf_tensor("zz", [128, D_MODEL], F32))
            wl = [ctx.full(f"p0st{b}") for b in range(NB0)] + [ctx.full(f"p0xr{b}") for b in range(NB0)]
            ms = fin.add("vector", lambda e: e.memset(z[:, :], 0.0), after=wl, sig=True)
            for i in range(TCORE // 128):
                fin.dma("gpsimd", out[i * 128:(i + 1) * 128, :], z[:, :],
                        stream="fino", after=[ms])
            fms = ctx.full("fino")
            fin.add("gpsimd", lambda e, fms=fms: e.wait_ge(ctx.sems[fms[0]], fms[1]))
            fin.emit(block)
            return nc

        sch.barrier()
        import os
        NCB_RUN = min(NCB_ALL, int(os.environ.get("K_CB", "99")))
        NSUP = min(2, int(os.environ.get("K_SUP", "2")))
        with contextlib.ExitStack() as ph:
            xrT = ph.enter_context(nc.sbuf_tensor("p1_xrT", [128, NKB * 2048], BF16))
            xrTh = ph.enter_context(nc.sbuf_tensor("p1_xrTh", [128, NKB * 128], BF16))
            wsl = [ph.enter_context(nc.sbuf_tensor(f"p1_w{i}", [128, NKB * 128], BF16)) for i in range(2)]

            dg = [ph.enter_context(nc.sbuf_tensor(f"p1_dg{i}", [128, 4 * 128], BF16)) for i in range(2)]
            cin = [ph.enter_context(nc.sbuf_tensor(f"p1_cin{i}", [128, 2051], BF16)) for i in range(2)]
            stash = ph.enter_context(nc.sbuf_tensor("p1_stash", [128, NCB_XBC * 3], BF16))
            cot = [ph.enter_context(nc.sbuf_tensor(f"p1_co{i}", [128, 1024], BF16)) for i in range(2)]
            sqt = [ph.enter_context(nc.sbuf_tensor(f"p1_sq{i}", [128, 1024], BF16)) for i in range(2)]
            gstg = None
            rcb = ph.enter_context(nc.sbuf_tensor("p1_rcb", [128, 1024], BF16))
            rcf = ph.enter_context(nc.sbuf_tensor("p1_rcf", [128, 1024], F32))
            cob = [ph.enter_context(nc.sbuf_tensor(f"p1_cob{i}", [128, 1024], BF16)) for i in range(4)]
            xbb = [ph.enter_context(nc.sbuf_tensor(f"p1_xbb{i}", [128, 1024], BF16)) for i in range(4)]
            ps_ip = [ph.enter_context(nc.psum_tensor(f"p1_pip{i}", [128, 512], F32)) for i in range(2)]
            ps_cv = [ph.enter_context(nc.psum_tensor(f"p1_pcv{i}", [128, 512], F32)) for i in range(2)]
            ps_sq = [ph.enter_context(nc.psum_tensor(f"p1_psq{i}", [128, 512], F32)) for i in range(4)]

            ms_dguse = [None, None]
            p0_done = [ctx.full(f"p0xr{b}") for b in range(NB0)] + [ctx.full(f"p0st{b}") for b in range(NB0)]

            ms_xrh = [sch.dma("sync", xrTh[:, k * 128:(k + 1) * 128],
                              s_xr[0:HALO, k * 128:(k + 1) * 128], transpose=True,
                              stream="p1xrh", after=p0_done) for k in range(NKB)]
            xrh_full = ctx.full("p1xrh")
            W_ENG = "gpsimd"
            wuse = [None, None]
            ms_stash = {}
            ms_cinw = [None, None]
            ms_cinr = [None, None]
            ms_ipevac = [None, None]
            ms_convmm = [None, None]
            ms_coscl = [None, None]
            ms_gevac = [None, None]
            ms_sqmm = None
            ms_rcread = None
            for c in range(min(NCB_XBC, NCB_RUN)):
                sl = c % 2
                wld = sch.dma(W_ENG, wsl[sl][:, :],
                              bass.AP(ip_wT.tensor, c * 128,
                                      [[IP_COLS, 128], [128 * IP_COLS, NKB], [1, 128]]),
                              stream=f"p1w{sl}", after=[wuse[sl]] + (ALL_CONSTS if c < 2 else []))
                lw = [wld, xrh_full] + ([ms_ipevac[0]] if ms_ipevac[0] else [])
                mm = None
                for k in range(NKB):
                    mm = sch.add("tensor", lambda e, sl=sl, k=k: e.matmul(
                        ps_ip[0][:, 0:128], wsl[sl][:, k * 128:(k + 1) * 128],
                        xrTh[:, k * 128:(k + 1) * 128],
                        start=(k == 0), stop=(k == NKB - 1)),
                        after=(lw if k == 0 else []), sig=(k == NKB - 1))
                wuse[sl] = mm
                ms_stash[c] = sch.add("vector", lambda e, c=c: e.tensor_scalar(
                    out=stash[:, c * 3:c * 3 + 3], in0=ps_ip[0][:, 125:128], scalar1=1.0,
                    scalar2=None, op0=ALU.mult), after=[mm], sig=True)
                ms_ipevac[0] = ms_stash[c]

            ms_tt = [None] * 4
            ms_sil = [None] * 4
            p1b_mr4 = None
            for sup in range(NSUP):
                tbase = HALO + sup * 2048
                mbase = sup * 2048
                wl = p0_done if sup == 0 else []
                prev_mm = [m for m in (wuse + ms_ipevac) if m]
                ms_xrT = [sch.dma("sync", xrT[:, k * 2048:(k + 1) * 2048],
                                  s_xr[tbase:tbase + 2048, k * 128:(k + 1) * 128], transpose=True,
                                  stream="p1xrT", after=wl + (prev_mm if k == 0 else []))
                          for k in range(NKB)]
                xrT_full = ctx.full("p1xrT")
                pend_tail = []
                p1b_units = []
                if NCB_RUN >= NCB_XBC:
                    for half in range(2):
                        def rc_unit(half=half):
                            nonlocal ms_rcread, p1b_mr4
                            mr1 = []
                            for tt in range(2):
                                mr1.append(sch.add("vector", lambda e, tt=tt, half=half: e.tensor_scalar(
                                    out=rcf[:, tt * 512:tt * 512 + 512], in0=ps_sq[half * 2 + tt][:, :],
                                    scalar1=1.0 / CONV_DIM, scalar2=1e-6, op0=ALU.mult, op1=ALU.add),
                                    after=[ms_sqmm], sig=True))
                            ms_rcread = mr1[1]
                            mr2 = sch.add("vector", lambda e: e.reciprocal(out=rcf[:, :], in_=rcf[:, :]),
                                          after=[mr1[0], mr1[1]], sig=True)
                            mr3 = sch.add("scalar", lambda e: e.activation(out=rcf[:, :], in_=rcf[:, :],
                                          func=AF.Sqrt), after=[mr2], sig=True)
                            p1b_mr4 = sch.add("vector", lambda e: e.tensor_scalar(
                                out=rcb[:, :], in0=rcf[:, :], scalar1=1.0, scalar2=None, op0=ALU.mult),
                                after=[mr3], sig=True)
                        p1b_units.append(rc_unit)
                        for cb in range(NCB_XBC):
                            def blk(cb=cb, half=half, mbase=mbase):
                                pofs = half * 1024
                                sl = cb % 4
                                ldb = sch.dma("gpsimd", cob[sl][:, :],
                                              s_co[cb * 128:(cb + 1) * 128, mbase + pofs:mbase + pofs + 1024],
                                              stream=f"p1cb{sl}",
                                              after=[ctx.full("p1co0"), ctx.full("p1co1"),
                                                     ms_tt[sl], ms_sil[sl]])
                                mtt = sch.add("vector", lambda e, sl=sl: e.tensor_tensor(
                                    out=cob[sl][:, :], in0=cob[sl][:, :], in1=rcb[:, :], op=ALU.mult),
                                    after=[ldb, p1b_mr4, ms_sil[sl]], sig=True)
                                ms_tt[sl] = mtt
                                msl = sch.add("scalar", lambda e, cb=cb, sl=sl: e.activation(
                                    out=xbb[sl][:, :], in_=cob[sl][:, :], func=AF.Silu,
                                    scale=c_cnw[:, cb:cb + 1]),
                                    after=[mtt, ms_sil[sl], ctx.full(f"p1xb{sl}")], sig=True)
                                ms_sil[sl] = msl
                                sch.dma("scalar",
                                        s_xbc[cb * 128:(cb + 1) * 128, mbase + pofs:mbase + pofs + 1024],
                                        xbb[sl][:, :], stream=f"p1xb{sl}", after=[msl])
                                if dbg_level in (1, 2, 3):
                                    sch.dma("scalar",
                                            dbg["xbc"][cb * 128:(cb + 1) * 128, mbase + pofs:mbase + pofs + 1024],
                                            xbb[sl][:, :], stream=f"p1xb{sl}", after=[msl])
                            p1b_units.append(blk)
                for c in range(NCB_RUN):
                    sl = c % 2
                    is_xbc = c < NCB_XBC
                    is_adt = c == 48
                    wld = sch.dma(W_ENG, wsl[sl][:, :],
                                  bass.AP(ip_wT.tensor, c * 128,
                                          [[IP_COLS, 128], [128 * IP_COLS, NKB], [1, 128]]),
                                  stream=f"p1w{sl}", after=[wuse[sl]])
                    mm = None
                    m_ev = [None, None, None, None]
                    m1 = None
                    for tt in range(4):
                        pb = tt % 2
                        for k in range(NKB):
                            mm = sch.add("tensor", lambda e, sl=sl, k=k, tt=tt, pb=pb: e.matmul(
                                ps_ip[pb][:, :], wsl[sl][:, k * 128:(k + 1) * 128],
                                xrT[:, k * 2048 + tt * 512:k * 2048 + tt * 512 + 512],
                                start=(k == 0), stop=(k == NKB - 1)),
                                after=(([wld, xrT_full] if tt == 0 else []) +
                                       ([ms_ipevac[pb]] if k == 0 else [])),
                                sig=(k == NKB - 1))
                        if is_xbc and tt == 0:
                            m1 = sch.add("vector", lambda e, c=c, sl=sl: e.tensor_scalar(
                                out=cin[sl][:, 0:3], in0=stash[:, c * 3:c * 3 + 3], scalar1=1.0,
                                scalar2=None, op0=ALU.mult),
                                after=[ms_stash[c], ms_cinr[sl]], sig=True)
                        extra = [m1] if (is_xbc and tt == 0) else []
                        if (not is_xbc) and (not is_adt):
                            extra = [ms_gevac[sl], ctx.full(f"p1g{sl}"), ctx.full(f"p1co{sl}")]
                        if is_xbc:
                            dstap = lambda sl=sl, tt=tt: cin[sl][:, 3 + tt * 512:3 + tt * 512 + 512]
                        elif is_adt:
                            dstap = lambda sl=sl, tt=tt, mbase=mbase: st_adt[:, mbase + tt * 512:mbase + tt * 512 + 512]
                        else:
                            dstap = lambda sl=sl, tt=tt: cot[sl][:, (tt % 2) * 512:(tt % 2) * 512 + 512]
                        if is_adt:
                            mev = sch.add("vector", lambda e, pb=pb, dstap=dstap: e.tensor_scalar(
                                out=dstap(), in0=ps_ip[pb][64:128, :], scalar1=1.0,
                                scalar2=None, op0=ALU.mult), after=[mm], sig=True)
                        elif tt % 2 == 0:
                            mev = sch.add("vector", lambda e, pb=pb, dstap=dstap: e.tensor_scalar(
                                out=dstap(), in0=ps_ip[pb][:, :], scalar1=1.0,
                                scalar2=None, op0=ALU.mult), after=[mm] + extra, sig=True)
                        else:
                            mev = sch.add("scalar", lambda e, pb=pb, dstap=dstap: e.activation(
                                out=dstap(), in_=ps_ip[pb][:, :], func=AF.Copy),
                                after=[mm] + extra, sig=True)
                        m_ev[tt] = mev
                        ms_ipevac[pb] = mev
                        if (not is_xbc) and (not is_adt) and tt % 2 == 1:
                            gc = c - 49
                            half = tt // 2
                            gst = sch.dma("scalar",
                                          s_gate[gc * 128:(gc + 1) * 128,
                                                 mbase + half * 1024:mbase + half * 1024 + 1024],
                                          cot[sl][:, 0:1024],
                                          stream=f"p1g{sl}", after=[m_ev[tt - 1], m_ev[tt]])
                            if dbg_level in (1, 2, 3):
                                gst = sch.dma("scalar",
                                              dbg["gate"][gc * 128:(gc + 1) * 128,
                                                          mbase + half * 1024:mbase + half * 1024 + 1024],
                                              cot[sl][:, 0:1024], stream=f"p1g{sl}",
                                              after=[m_ev[tt - 1], m_ev[tt]])
                            ms_gevac[sl] = gst
                    wuse[sl] = m_ev[3]
                    if pend_tail and c >= 1:
                        pend_tail.pop(0)()
                    if is_xbc:
                        ms_cinw[sl] = m_ev[3]
                        ms_stash[c] = sch.add("vector", lambda e, c=c, sl=sl: e.tensor_scalar(
                            out=stash[:, c * 3:c * 3 + 3], in0=cin[sl][:, 2048:2051], scalar1=1.0,
                            scalar2=None, op0=ALU.mult), after=[m_ev[2], m_ev[3]], sig=True)

                        def conv_tail(c=c, sl=sl, m_ev=m_ev, sup=sup, mbase=mbase):
                            nonlocal ms_sqmm, ms_rcread
                            mdg = None
                            for k4 in range(4):
                                mdg = sch.add("vector", lambda e, c=c, sl=sl, k4=k4: e.tensor_scalar(
                                    out=dg[sl][:, k4 * 128:(k4 + 1) * 128],
                                    in0=c_eye[:, :], scalar1=c_convw[:, c * 4 + k4:c * 4 + k4 + 1],
                                    scalar2=None, op0=ALU.mult),
                                    after=([ms_dguse[sl]] + (ALL_CONSTS if sup == 0 and c < 2 else [])
                                           if k4 == 0 else []),
                                    sig=(k4 == 3))
                            cmm = None
                            for rnd in range(2):
                                for k4 in range(4):
                                    for tt in range(2):
                                        cmm = sch.add("tensor", lambda e, sl=sl, k4=k4, tt=tt, rnd=rnd: e.matmul(
                                            ps_cv[tt][:, :],
                                            dg[sl][:, k4 * 128:(k4 + 1) * 128],
                                            cin[sl][:, rnd * 1024 + k4 + tt * 512:rnd * 1024 + k4 + tt * 512 + 512],
                                            start=(k4 == 0), stop=(k4 == 3)),
                                            after=([m_ev[0], m_ev[1], m_ev[2], m_ev[3], mdg,
                                                    ms_coscl[0], ms_coscl[1]]
                                                   if (k4 == 0 and tt == 0 and rnd == 0) else
                                                   ([ms_coscl[0], ms_coscl[1]] if (k4 == 0 and tt == 0) else [])),
                                            sig=(k4 == 3 and tt == 1))
                                s1 = []
                                s2 = []
                                for tt in range(2):
                                    s1.append(sch.add("scalar", lambda e, c=c, sl=sl, tt=tt: e.activation(
                                        out=cot[sl][:, tt * 512:tt * 512 + 512], in_=ps_cv[tt][:, :],
                                        func=AF.Identity, bias=c_convb[:, c:c + 1], scale=1.0),
                                        after=([cmm, ctx.full(f"p1co{sl}"), ctx.full(f"p1g{sl}")] if tt == 0 else []),
                                        sig=(tt == 1)))
                                for tt in range(2):
                                    s2.append(sch.add("scalar", lambda e, c=c, sl=sl, tt=tt: e.activation(
                                        out=sqt[sl][:, tt * 512:tt * 512 + 512], in_=ps_cv[tt][:, :],
                                        func=AF.Square, bias=c_convb[:, c:c + 1], scale=1.0),
                                        after=[cmm], sig=(tt == 1)))
                                ms_coscl[0] = s1[1]
                                ms_coscl[1] = s2[1]
                                sch.dma("gpsimd", s_co[c * 128:(c + 1) * 128,
                                                       mbase + rnd * 1024:mbase + rnd * 1024 + 1024],
                                        cot[sl][:, :], stream=f"p1co{sl}", after=[s1[1]])
                                if dbg_level in (1, 2, 3):
                                    sch.dma("gpsimd", dbg["co"][c * 128:(c + 1) * 128,
                                                                mbase + rnd * 1024:mbase + rnd * 1024 + 1024],
                                            cot[sl][:, :], stream=f"p1co{sl}", after=[s1[1]])
                                for tt in range(2):
                                    ms_sqmm = sch.add("tensor", lambda e, tt=tt, sl=sl, c=c, rnd=rnd: e.matmul(
                                        ps_sq[rnd * 2 + tt][:, :], c_onesT[:, :],
                                        sqt[sl][:, tt * 512:tt * 512 + 512],
                                        start=(c == 0), stop=(c == min(NCB_XBC, NCB_RUN) - 1)),
                                        after=[s2[1]] + ([ms_rcread] if c == 0 else []),
                                        sig=(tt == 1 and c == min(NCB_XBC, NCB_RUN) - 1 and rnd == 1))
                            ms_cinr[sl] = cmm
                            ms_dguse[sl] = cmm

                        pend_tail.append(conv_tail)
                    if c >= 49:
                        for _ in range(4):
                            if p1b_units:
                                p1b_units.pop(0)()
                while pend_tail:
                    pend_tail.pop(0)()
                while p1b_units:
                    p1b_units.pop(0)()
                ms_sqmm = None
            sch.emit(block)

        if dbg_level == 1:
            fin = Sched(ctx)
            z = stack.enter_context(nc.sbuf_tensor("zz", [128, D_MODEL], F32))
            wl = [ctx.full(s) for s in ["p1xb0", "p1xb1", "p1xb2", "p1xb3", "p1g0", "p1g1", "p1co0", "p1co1"]]
            ms = fin.add("vector", lambda e: e.memset(z[:, :], 0.0), after=wl, sig=True)
            for i in range(TCORE // 128):
                fin.dma("gpsimd", out[i * 128:(i + 1) * 128, :], z[:, :], stream="fino", after=[ms])
            fms = ctx.full("fino")
            fin.add("gpsimd", lambda e, fms=fms: e.wait_ge(ctx.sems[fms[0]], fms[1]))
            fin.emit(block)
            return nc

        sch.barrier()
        with contextlib.ExitStack() as ph:
            pA = ph.enter_context(nc.sbuf_tensor("p1c_A", [H, TCORE], F32))
            pq = ph.enter_context(nc.sbuf_tensor("p1c_q", [H, 512], F32))
            pz = ph.enter_context(nc.sbuf_tensor("p1c_z", [H, 16], F32))
            zer = ph.enter_context(nc.sbuf_tensor("p1c_zer", [H, CHUNK], F32))
            cdc = ph.enter_context(nc.sbuf_tensor("p1c_cdc", [H, NCHUNK], F32))
            cdflat = ph.enter_context(nc.sbuf_tensor("p1c_cdf", [1, NCHUNK * H], F32))
            ps_z = ph.enter_context(nc.psum_tensor("p1c_pz", [H, 512], F32))
            ps_cd = [ph.enter_context(nc.psum_tensor(f"p1c_pcd{i}", [128, 512], F32)) for i in range(2)]

            m_ne = sch.add("scalar", lambda e: e.activation(
                out=pz[:, 0:1], in_=c_alog[:, :], func=AF.Exp), sig=True)
            m_ne2 = sch.add("vector", lambda e: e.tensor_scalar(
                out=pz[:, 0:1], in0=pz[:, 0:1], scalar1=-1.0, scalar2=None, op0=ALU.mult),
                after=[m_ne], sig=True)
            m_zer = sch.add("vector", lambda e: e.memset(zer[:, :], 0.0), sig=True)
            m_sc = None
            m_Aprev = None
            for it in range(8):
                mm = sch.add("tensor", lambda e, it=it: e.matmul(
                    ps_z[:, :], c_dtwT[:, :], st_adt[:, it * 512:(it + 1) * 512],
                    start=True, stop=True), after=([m_sc] if m_sc else []), sig=True)
                mq = sch.add("scalar", lambda e: e.activation(
                    out=pq[:, :], in_=ps_z[:, :], func=AF.Exp, bias=c_dtb[:, :]),
                    after=[mm, m_Aprev], sig=True)
                m_sc = mq
                ml = sch.add("scalar", lambda e: e.activation(
                    out=pq[:, :], in_=pq[:, :], func=AF.Ln, bias=1.0), after=[mq], sig=True)
                m_A = sch.add("vector", lambda e, it=it: e.tensor_scalar(
                    out=pA[:, it * 512:(it + 1) * 512], in0=pq[:, :], scalar1=pz[:, 0:1],
                    scalar2=None, op0=ALU.mult), after=[ml, m_ne2], sig=True)
                m_Aprev = m_A
            m_lastA = m_A
            m_scan = None
            for cch in range(NCHUNK):
                m_scan = sch.add("vector", lambda e, cch=cch: e.tensor_tensor_scan(
                    out=st_acs[:, cch * CHUNK:(cch + 1) * CHUNK],
                    data0=pA[:, cch * CHUNK:(cch + 1) * CHUNK], data1=zer[:, :],
                    initial=0.0, op0=ALU.add, op1=ALU.add),
                    after=[m_lastA, m_zer] + ([m_scan] if m_scan else []), sig=True)
            m_al = sch.add("vector", lambda e: e.tensor_scalar(
                out=st_alast[:, :],
                in0=bass.AP(st_acs, CHUNK - 1, [[TCORE, H], [CHUNK, NCHUNK]]),
                scalar1=1.0, scalar2=None, op0=ALU.mult), after=[m_scan], sig=True)
            m_oi = sch.add("vector", lambda e: e.tensor_tensor_scan(
                out=st_offs[:, :], data0=st_alast[:, :], data1=zer[:, 0:NCHUNK],
                initial=0.0, op0=ALU.add, op1=ALU.add), after=[m_al], sig=True)
            m_le = sch.add("vector", lambda e: e.tensor_scalar(
                out=st_lend[:, :], in0=st_offs[:, NCHUNK - 1:NCHUNK], scalar1=1.0,
                scalar2=None, op0=ALU.mult), after=[m_oi], sig=True)
            m_ox = sch.add("vector", lambda e: e.tensor_tensor(
                out=st_offs[:, :], in0=st_offs[:, :], in1=st_alast[:, :], op=ALU.subtract),
                after=[m_le], sig=True)
            m_wb = sch.add("vector", lambda e: e.tensor_scalar(
                out=st_wb[:, :], in0=st_offs[:, :], scalar1=-1.0, scalar2=st_lend[:, :],
                op0=ALU.mult, op1=ALU.add), after=[m_ox], sig=True)
            m_dk = sch.add("scalar", lambda e: e.activation(
                out=pz[:, 1:2], in_=st_lend[:, :], func=AF.Exp), after=[m_le], sig=True)
            sch.dma("gpsimd", ag_src[H * NSTATE:H * NSTATE + H, :], zer[0:64, 0:64],
                    stream="agd", after=[m_zer])
            sch.dma("gpsimd", ag_src[H * NSTATE:H * NSTATE + 1, 0:H], pz[:, 1:2],
                    stream="agd", after=[m_dk, ctx.full("agd")])
            m_hi = sch.add("vector", lambda e: e.tensor_scalar(
                out=st_hi[:, :], in0=st_acs[:, :], scalar1=1.0, scalar2=None, op0=ALU.mult),
                after=[m_al], sig=True)
            m_r1 = sch.add("vector", lambda e: e.tensor_tensor(
                out=pA[:, :], in0=st_acs[:, :], in1=st_hi[:, :], op=ALU.subtract),
                after=[m_hi], sig=True)
            m_lo = sch.add("vector", lambda e: e.tensor_scalar(
                out=st_lo[:, :], in0=pA[:, :], scalar1=1.0, scalar2=None, op0=ALU.mult),
                after=[m_r1], sig=True)
            m_cd = sch.add("scalar", lambda e: e.activation(
                out=cdc[:, :], in_=st_alast[:, :], func=AF.Exp), after=[m_al], sig=True)
            m_cdf = sch.dma("sync", cdflat[0:1, :], cdc[:, :], stream="cdf", after=[m_cd])
            m_rep = []
            for half in range(2):
                mr = sch.add("tensor", lambda e, half=half: e.matmul(
                    ps_cd[half][:, :], c_onesf[:, :], cdflat[0:1, half * 512:(half + 1) * 512],
                    start=True, stop=True), after=[m_cdf], sig=True)
                m_rep.append(sch.add("vector", lambda e, half=half: e.tensor_scalar(
                    out=st_cdrep[:, half * 512:(half + 1) * 512], in0=ps_cd[half][:, :],
                    scalar1=1.0, scalar2=None, op0=ALU.mult), after=[mr], sig=True))
            if dbg_level in (1, 2, 3):
                sch.dma("gpsimd", dbg["acs"], st_acs[:, :], stream="dacs", after=[m_scan])
            sch.emit(block)

        sch.barrier()
        with contextlib.ExitStack() as ph:
            wcm = ph.enter_context(nc.sbuf_tensor("p2_wcm", [H, CHUNK], BF16))
            dcm = ph.enter_context(nc.sbuf_tensor("p2_dcm", [H, CHUNK], BF16))
            ps_tr = ph.enter_context(nc.psum_tensor("p2_ptr", [128, 64], BF16))
            ps_trf = ph.enter_context(nc.psum_tensor("p2_ptrf", [128, 64], F32))
            m_wu = None
            m_du = None
            m_tre = None
            m_trf = None
            for cch in range(NCHUNK):
                for tt in range(2):
                    mt = sch.add("tensor", lambda e, cch=cch, tt=tt: e.transpose(
                        ps_trf[:, 0:64], st_acs[:, cch * CHUNK + tt * 128:cch * CHUNK + tt * 128 + 128],
                        c_eyef[0:64, 0:64]), after=[m_trf], sig=True)
                    m_trf = sch.add("vector", lambda e, cch=cch, tt=tt: e.tensor_scalar(
                        out=negacs[:, cch * 128 + tt * 64:cch * 128 + tt * 64 + 64],
                        in0=ps_trf[:, 0:64], scalar1=-1.0, scalar2=None, op0=ALU.mult),
                        after=[mt], sig=True)
            for cch in range(NCHUNK):
                tb = cch * CHUNK
                m_wcm = sch.add("scalar", lambda e, cch=cch, tb=tb: e.activation(
                    out=wcm[:, :], in_=st_acs[:, tb:tb + CHUNK], func=AF.Exp,
                    bias=st_wb[:, cch:cch + 1], scale=-1.0), after=[m_wu], sig=True)
                m_dcm = sch.add("scalar", lambda e, cch=cch, tb=tb: e.activation(
                    out=dcm[:, :], in_=st_acs[:, tb:tb + CHUNK], func=AF.Exp,
                    bias=st_alast[:, cch:cch + 1], scale=-1.0), after=[m_du], sig=True)
                for (src_t, m_src, dst, tag) in ((wcm, m_wcm, wtk_all, 0), (dcm, m_dcm, d1tk_all, 1)):
                    for tt in range(2):
                        mt = sch.add("tensor", lambda e, src_t=src_t, tt=tt: e.transpose(
                            ps_tr[:, :], src_t[:, tt * 128:(tt + 1) * 128], c_eye[0:64, 0:64]),
                            after=[m_src, m_tre], sig=True)
                        m_tre = sch.add("vector", lambda e, dst=dst, cch=cch, tt=tt: e.tensor_scalar(
                            out=dst[:, cch * 128 + tt * 64:cch * 128 + tt * 64 + 64],
                            in0=ps_tr[:, :], scalar1=1.0, scalar2=None, op0=ALU.mult),
                            after=[mt], sig=True)
                    if tag == 0:
                        m_wu = m_tre
                    else:
                        m_du = m_tre
            sch.emit(block)

        sch.barrier()
        with contextlib.ExitStack() as ph:
            xtk = [ph.enter_context(nc.sbuf_tensor(f"p2_x{i}", [128, 2 * D_INNER], BF16)) for i in range(2)]
            btk = [ph.enter_context(nc.sbuf_tensor(f"p2_b{i}", [128, 2 * GN], BF16)) for i in range(2)]
            dxw = [ph.enter_context(nc.sbuf_tensor(f"p2_dx{i}", [128, 2 * D_INNER], BF16)) for i in range(2)]
            skst = ph.enter_context(nc.sbuf_tensor("p2_sk", [128, H * D_HEAD], F32))
            ps_sk = [ph.enter_context(nc.psum_tensor(f"p2_psk{i}", [128, 512], F32)) for i in range(8)]

            m_dxu = [None, None]
            m_mm = None
            for cch in range(NCHUNK):
                tb = cch * CHUNK
                sl2 = cch % 2
                lx, lb = [], []
                for tt in range(2):
                    lx.append(sch.dma("sync", xtk[sl2][:, tt * D_INNER:(tt + 1) * D_INNER],
                                      s_xbc[0:D_INNER, tb + tt * 128:tb + tt * 128 + 128],
                                      transpose=True, stream=f"p2x{sl2}",
                                      after=([m_dxu[sl2]] if tt == 0 else [])))
                    lb.append(sch.dma("scalar", btk[sl2][:, tt * GN:(tt + 1) * GN],
                                      s_xbc[D_INNER:D_INNER + GN, tb + tt * 128:tb + tt * 128 + 128],
                                      transpose=True, stream=f"p2b{sl2}",
                                      after=([m_mm] if tt == 0 else [])))
                m_dx = []
                for tt in range(2):
                    m_dx.append(sch.add("vector", lambda e, sl2=sl2, tt=tt, cch=cch: e.tensor_tensor(
                        out=dxw[sl2][:, tt * D_INNER:(tt + 1) * D_INNER],
                        in0=xtk[sl2][:, tt * D_INNER:(tt + 1) * D_INNER],
                        in1=bass.AP(wtk_all, cch * 128 + tt * 64, [[NCHUNK * 128, 128], [1, H], [0, D_HEAD]]),
                        op=ALU.mult), after=[lx[1], m_dxu[sl2]], sig=(tt == 1)))
                for h in range(H):
                    g = h // (H // G)
                    bank, col = h // 8, (h % 8) * 64
                    for tt in range(2):
                        m_mm = sch.add("tensor", lambda e, sl2=sl2, tt=tt, h=h, g=g, bank=bank, col=col, cch=cch: e.matmul(
                            ps_sk[bank][:, col:col + 64],
                            btk[sl2][:, tt * GN + g * 128:tt * GN + g * 128 + 128],
                            dxw[sl2][:, tt * D_INNER + h * 64:tt * D_INNER + h * 64 + 64],
                            start=(cch == 0 and tt == 0 and h % 8 == 0),
                            stop=(cch == NCHUNK - 1 and tt == 1 and h % 8 == 7)),
                            after=([m_dx[1], lb[1]] if (h == 0 and tt == 0) else []),
                            sig=(h == H - 1 and tt == 1))
                m_dxu[sl2] = m_mm
            m_ev = []
            for bank in range(8):
                m_ev.append(sch.add("vector", lambda e, bank=bank: e.tensor_scalar(
                    out=skst[:, bank * 512:(bank + 1) * 512], in0=ps_sk[bank][:, :],
                    scalar1=1.0, scalar2=None, op0=ALU.mult), after=[m_mm], sig=True))
            m_sk = sch.dma("gpsimd",
                           bass.AP(ag_src_t, 0, [[64, 128], [128 * 64, H], [1, 64]]),
                           skst[:, :], stream="agd", after=[m_ev[-1]] + m_ev[:-1])
            agd_full = ctx.full("agd")

            def do_cc(e):
                rg = ([[0, 1, 2, 3]] if os.environ.get("K_CC4") else [[0, 1, 2, 3], [4, 5, 6, 7]])
                instr = e.collective_compute(
                    "AllGather", ALU.bypass,
                    replica_groups=rg,
                    ins=[ag_src.opt()], outs=[ag_dst.opt()])
                instr.then_inc(ctx.sems["cc"], 1)
                return instr
            sch.add("gpsimd", do_cc, after=[agd_full])
            ctx.cnt["cc"] += 1
            sch.emit(block)
        mstackA.close()

        sch.barrier()
        with contextlib.ExitStack() as ph:
            xnT = ph.enter_context(nc.sbuf_tensor("p4_xnT", [128, NKB * 2048], BF16))
            wg = [ph.enter_context(nc.sbuf_tensor(f"p4_wg{i}", [128, NKB * 128], BF16)) for i in range(2)]
            wu = [ph.enter_context(nc.sbuf_tensor(f"p4_wu{i}", [128, NKB * 128], BF16)) for i in range(2)]

            sg = [ph.enter_context(nc.sbuf_tensor(f"p4_sg{i}", [128, 2048], BF16)) for i in range(2)]
            hd = [ph.enter_context(nc.sbuf_tensor(f"p4_hd{i}", [128, 2048], BF16)) for i in range(2)]
            ps_g = ph.enter_context(nc.psum_tensor("p4_pg", [128, 2048], F32))
            ps_u = ph.enter_context(nc.psum_tensor("p4_pu", [128, 2048], F32))

            m_wgu = [None, None]
            m_wuu = [None, None]
            m_sil_t = [None] * 4
            m_hdn_t4 = [None] * 4
            m_sg_u = [None, None]
            m_hdn_last = None
            for th in range(2):
                tbb = th * 2048
                prevmm = [m for m in (m_wgu + m_wuu) if m]
                m_xn = [sch.dma("sync", xnT[:, k * 2048:(k + 1) * 2048],
                                s_xn[tbb:tbb + 2048, k * 128:(k + 1) * 128], transpose=True,
                                stream="p4xn", after=(prevmm if k == 0 else []) +
                                ([ctx.full("p0st0"), ctx.full("p0st1"), ctx.full("p0st2")] if th == 0 else []))
                       for k in range(NKB)]
                xn_full = ctx.full("p4xn")
                for ib in range(NIB):
                    sl = ib % 2
                    lg = sch.dma("gpsimd", wg[sl][:, :],
                                 bass.AP(gate_wT.tensor, ib * 128,
                                         [[INTER, 128], [128 * INTER, NKB], [1, 128]]),
                                 stream=f"p4wg{sl}", after=[m_wgu[sl]])
                    lu = sch.dma("gpsimd", wu[sl][:, :],
                                 bass.AP(up_wT.tensor, ib * 128,
                                         [[INTER, 128], [128 * INTER, NKB], [1, 128]]),
                                 stream=f"p4wu{sl}", after=[m_wuu[sl]])
                    mmg_t = [None] * 4
                    sil_t = [None] * 4
                    for tt in range(4):
                        for k in range(NKB):
                            mmg_t[tt] = sch.add("tensor", lambda e, sl=sl, k=k, tt=tt: e.matmul(
                                ps_g[:, tt * 512:(tt + 1) * 512], wg[sl][:, k * 128:(k + 1) * 128],
                                xnT[:, k * 2048 + tt * 512:k * 2048 + tt * 512 + 512],
                                start=(k == 0), stop=(k == NKB - 1)),
                                after=(([lg, xn_full] if tt == 0 else []) +
                                       ([m_sil_t[tt]] if k == 0 else [])),
                                sig=(k == NKB - 1))
                        sil_t[tt] = sch.add("scalar", lambda e, sl=sl, ib=ib, tt=tt: e.activation(
                            out=sg[sl][:, tt * 512:(tt + 1) * 512], in_=ps_g[:, tt * 512:(tt + 1) * 512],
                            func=AF.Silu, scale=c_gs[:, ib:ib + 1]),
                            after=[mmg_t[tt], m_sg_u[sl]], sig=True)
                        m_sil_t[tt] = sil_t[tt]
                    m_wgu[sl] = mmg_t[3]
                    mmu_t = [None] * 4
                    hdn_t = [None] * 4
                    for tt in range(4):
                        for k in range(NKB):
                            mmu_t[tt] = sch.add("tensor", lambda e, sl=sl, k=k, tt=tt: e.matmul(
                                ps_u[:, tt * 512:(tt + 1) * 512], wu[sl][:, k * 128:(k + 1) * 128],
                                xnT[:, k * 2048 + tt * 512:k * 2048 + tt * 512 + 512],
                                start=(k == 0), stop=(k == NKB - 1)),
                                after=(([lu] if tt == 0 else []) +
                                       ([m_hdn_t4[tt]] if k == 0 else [])),
                                sig=(k == NKB - 1))
                        hdn_t[tt] = sch.add("vector", lambda e, sl=sl, ib=ib, tt=tt: e.scalar_tensor_tensor(
                            out=hd[sl][:, tt * 512:(tt + 1) * 512], in0=ps_u[:, tt * 512:(tt + 1) * 512],
                            scalar=c_us[:, ib:ib + 1], in1=sg[sl][:, tt * 512:(tt + 1) * 512],
                            op0=ALU.mult, op1=ALU.mult),
                            after=[mmu_t[tt], sil_t[tt]] + ([ctx.full(f"p4hd{sl}")] if tt == 0 else []),
                            sig=True)
                        m_hdn_t4[tt] = hdn_t[tt]
                    m_wuu[sl] = mmu_t[3]
                    m_sg_u[sl] = hdn_t[3]
                    m_hdn_last = hdn_t[3]
                    sch.dma("scalar", s_hdn[ib * 128:(ib + 1) * 128, tbb:tbb + 2048],
                            hd[sl][:, :], stream=f"p4hd{sl}", after=list(hdn_t))
                    if dbg_level == 3:
                        sch.dma("scalar", dbg["hdn"][ib * 128:(ib + 1) * 128, tbb:tbb + 2048],
                                hd[sl][:, :], stream=f"p4hd{sl}", after=list(hdn_t))
            sch.emit(block)

        sch.barrier()
        with contextlib.ExitStack() as ph:
            sS = [ph.enter_context(nc.sbuf_tensor(f"p3_S{i}", [128, H * D_HEAD], F32)) for i in range(2)]
            drep = ph.enter_context(nc.sbuf_tensor("p3_drep", [128, 192], F32))
            dt3 = ph.enter_context(nc.sbuf_tensor("p3_dt3", [H, 8], F32))
            dflat = ph.enter_context(nc.sbuf_tensor("p3_dfl", [1, 256], F32))
            ps_dr = ph.enter_context(nc.psum_tensor("p3_pdr", [128, 192], F32))

            cc_ms = ("cc", 1)
            l0 = sch.dma("sync", sS[0][:, :],
                         bass.AP(ag_dst_t, 0, [[64, 128], [128 * 64, H], [1, 64]]),
                         stream="p3s0", after=[cc_ms])
            m_h = sch.add("vector", lambda e: e.tensor_scalar(
                out=hA[:, :], in0=sS[0][:, :], scalar1=c_rmask[:, 0:1], scalar2=None,
                op0=ALU.mult), after=[l0], sig=True)
            if dbg_level in (2, 3):
                sch.dma("gpsimd", dbg["agd"], ag_dst, stream="dagd", after=[cc_ms])
            ldd = None
            for r_ in range(4):
                ldd = sch.dma("scalar", dt3[:, 4 + r_:5 + r_],
                              bass.AP(ag_dst_t, (r_ * (H * NSTATE + H) + H * NSTATE) * 64,
                                      [[1, H], [1, 1]]),
                              stream="p3dd", after=([cc_ms] if r_ == 0 else []))
            m_d3 = None
            for m in range(1, 4):
                m_d3 = sch.add("vector", lambda e, m=m: e.tensor_scalar(
                    out=dt3[:, m - 1:m], in0=dt3[:, 4 + m:5 + m],
                    scalar1=c_rmask[0:64, m:m + 1], scalar2=c_rmask1[0:64, m:m + 1],
                    op0=ALU.mult, op1=ALU.add), after=[ldd] + ([m_d3] if m_d3 else []), sig=True)
            m_df = sch.dma("sync", dflat[0:1, 0:192], dt3[:, 0:3], stream="p3df", after=[m_d3])
            m_dr1 = sch.add("tensor", lambda e: e.matmul(
                ps_dr[:, :], c_onesf[:, :], dflat[0:1, 0:192], start=True, stop=True),
                after=[m_df], sig=True)
            m_dr = sch.add("vector", lambda e: e.tensor_scalar(
                out=drep[:, :], in0=ps_dr[:, :], scalar1=1.0, scalar2=None, op0=ALU.mult),
                after=[m_dr1], sig=True)
            hcur, hnxt = hA, hB
            m_hready = [m_h]
            for m in range(1, 4):
                lS = sch.dma("sync", sS[m % 2][:, :],
                             bass.AP(ag_dst_t, (m * 8256) * 64, [[64, 128], [128 * 64, H], [1, 64]]),
                             stream=f"p3s{m % 2}", after=[cc_ms] if m == 1 else [m_hready[-1]])
                m_msk = sch.add("vector", lambda e, m=m: e.tensor_scalar(
                    out=sS[m % 2][:, :], in0=sS[m % 2][:, :], scalar1=c_rmask[:, m:m + 1],
                    scalar2=None, op0=ALU.mult), after=[lS], sig=True)
                m_last = None
                for h in range(H):
                    m_last = sch.add("vector", lambda e, m=m, h=h, hcur=hcur, hnxt=hnxt: e.scalar_tensor_tensor(
                        out=hnxt[:, h * 64:(h + 1) * 64], in0=hcur[:, h * 64:(h + 1) * 64],
                        scalar=drep[:, (h * 3 + m - 1):(h * 3 + m)], in1=sS[m % 2][:, h * 64:(h + 1) * 64],
                        op0=ALU.mult, op1=ALU.add),
                        after=[m_msk, m_dr] + m_hready, sig=(h == H - 1))
                m_hready = [m_last]
                hcur, hnxt = hnxt, hcur
            m_hcomb = m_hready[0]

            xtk3 = [ph.enter_context(nc.sbuf_tensor("p3_x0", [128, 2 * D_INNER], BF16))] * 2
            btk3 = [ph.enter_context(nc.sbuf_tensor(f"p3_b{i}", [128, 2 * GN], BF16)) for i in range(2)]
            bcm = [ph.enter_context(nc.sbuf_tensor("p3_bc0", [128, 2048], BF16))] * 2
            ccm = [ph.enter_context(nc.sbuf_tensor("p3_cc0", [128, 2048], BF16))] * 2
            gcm = ph.enter_context(nc.sbuf_tensor("p3_g", [128, 8192], BF16))
            ych = ph.enter_context(nc.sbuf_tensor("p3_y", [128, 8192], BF16))
            ybuf = ph.enter_context(nc.sbuf_tensor("p3_yb", [128, 8192], BF16))
            flat = ph.enter_context(nc.sbuf_tensor("p3_flat", [3, 32 * CHUNK], BF16))
            cbm = ph.enter_context(nc.sbuf_tensor("p3_cbm", [128, G * 384], BF16))
            psb = [ph.enter_context(nc.sbuf_tensor(f"p3_p{i}", [128, 384], BF16)) for i in range(4)]
            ctl = [ph.enter_context(nc.sbuf_tensor(f"p3_ct{i}", [128, 256], BF16)) for i in range(4)]
            dxs = [ph.enter_context(nc.sbuf_tensor(f"p3_dx{i}", [128, 128], BF16)) for i in range(4)]
            grf = ph.enter_context(nc.sbuf_tensor("p3_grf", [128, 256], F32))
            grb = ph.enter_context(nc.sbuf_tensor("p3_grb", [128, 256], BF16))
            ps_d0 = ph.enter_context(nc.psum_tensor("p3_psd0", [128, 512], F32))
            ps_d1 = ph.enter_context(nc.psum_tensor("p3_psd1", [128, 512], F32))
            psd_of = lambda par: (ps_d0 if par < 2 else ps_d1)
            ps_mt = ph.enter_context(nc.sbuf_tensor("p3_mtsb", [128, 4 * 384], BF16))
            ps_et = ph.enter_context(nc.sbuf_tensor("p3_etsb", [128, 4 * 256], BF16))
            ps_y = [ph.enter_context(nc.psum_tensor(f"p3_psy{i}", [128, 512], F32))[:, 0:256] for i in range(2)]
            ps_s = [ph.enter_context(nc.psum_tensor(f"p3_pss{i}", [128, 512], F32))[:, 0:256] for i in range(2)]

            m_xu = [None, None]
            m_bu = [None, None]
            m_bcu = [None, None]
            m_ccu = [None, None]
            m_gu = None
            m_ychu = None
            m_flatu = None
            m_exps = [None] * 4
            m_pstt = [None] * 4
            m_ctt = [None] * 4
            m_yev = [None, None]
            m_hstt = [None, None]
            m_dxu = [None] * 4
            m_ymm = [None, None]
            m_smm = [None, None]
            m_cbev = None
            m_ygdma = None
            m_rmsmm = None
            m_grb = None
            for cch in range(NCHUNK):
                tb = cch * CHUNK
                sl2 = cch % 2
                lx = [sch.dma("sync", xtk3[sl2][:, tt * D_INNER:(tt + 1) * D_INNER],
                              s_xbc[0:D_INNER, tb + tt * 128:tb + tt * 128 + 128],
                              transpose=True, stream="p3x0",
                              after=([m_xu[0], m_xu[1]] if tt == 0 else [])) for tt in range(2)]
                lb = [sch.dma("scalar", btk3[sl2][:, tt * GN:(tt + 1) * GN],
                              s_xbc[D_INNER:D_INNER + GN, tb + tt * 128:tb + tt * 128 + 128],
                              transpose=True, stream=f"p3b{sl2}",
                              after=([m_bu[sl2]] if tt == 0 else [])) for tt in range(2)]
                lbc = sch.dma("sync", bcm[sl2][:, :],
                              bass.AP(s_xbc.tensor, D_INNER * TCORE + tb,
                                      [[TCORE, 128], [128 * TCORE, G], [1, CHUNK]]),
                              stream=f"p3bc{sl2}", after=[m_bcu[sl2]])
                lcc = sch.dma("scalar", ccm[sl2][:, :],
                              bass.AP(s_xbc.tensor, (D_INNER + GN) * TCORE + tb,
                                      [[TCORE, 128], [128 * TCORE, G], [1, CHUNK]]),
                              stream="p3cc0",
                              after=[m_ccu[0], m_ccu[1]] + list(m_ctt))
                lg = sch.dma("sync", gcm[:, :],
                             bass.AP(s_gate.tensor, tb,
                                     [[TCORE, 128], [128 * TCORE, NYB], [1, CHUNK]]),
                             stream="p3g", after=[m_gu])
                flat_ms = [None, None]
                m_cb = None
                for g in range(G):
                    mm1 = sch.add("tensor", lambda e, g=g, sl2=sl2: e.matmul(
                        ps_d0[:, 0:256], bcm[sl2][:, g * 256:g * 256 + 128],
                        ccm[sl2][:, g * 256:(g + 1) * 256], start=True, stop=True),
                        after=([lbc, lcc, m_exps[0], m_exps[1], m_cbev] if g == 0 else [m_cb]),
                        sig=False)
                    mm2 = sch.add("tensor", lambda e, g=g, sl2=sl2: e.matmul(
                        ps_d0[:, 256:384], bcm[sl2][:, g * 256 + 128:g * 256 + 256],
                        ccm[sl2][:, g * 256 + 128:(g + 1) * 256], start=True, stop=True),
                        sig=True)
                    m_cb = sch.add("vector", lambda e, g=g: e.tensor_tensor(
                        out=cbm[:, g * 384:(g + 1) * 384], in0=ps_d0[:, 0:384],
                        in1=c_cbmask[:, :], op=ALU.mult), after=[mm2, m_pstt[0], m_pstt[1]], sig=True)
                m_cbev = m_cb
                m_bcu[sl2] = mm2
                m_ccu[sl2] = mm2
                for h in range(H):
                    g = h // (H // G)
                    par = h % 4
                    jpar = (h // 2) % 2
                    hh = h % 32
                    if hh == 0:
                        half32 = h // 32
                        hrows = slice(half32 * 32, half32 * 32 + 32)
                        sch.dma("gpsimd", flat[0:1, :], st_hi[hrows, tb:tb + CHUNK],
                                stream="p3f", after=[m_flatu])
                        sch.dma("gpsimd", flat[1:2, :], st_lo[hrows, tb:tb + CHUNK],
                                stream="p3f")
                        flat_full = ctx.full("p3f")
                    m_r = sch.add("tensor", lambda e, h=h, par=par, hh=hh: e.matmul(
                        psd_of(par)[:, (par % 2) * 256:(par % 2) * 256 + 256], c_ones8[0:2, :],
                        flat[0:2, hh * 256:(hh + 1) * 256], start=True, stop=True),
                        after=[flat_full, m_exps[par], m_cbev if h < 4 else None], sig=True)
                    if hh == 31:
                        m_flatu = m_r
                    base = par * 384
                    m_mt0 = sch.add("scalar", lambda e, h=h, par=par, cch=cch, base=base: e.activation(
                        out=ps_mt[:, base:base + 256], in_=psd_of(par)[:, (par % 2) * 256:(par % 2) * 256 + 256],
                        func=AF.Exp, bias=negacs[:, cch * 128 + h:cch * 128 + h + 1]),
                        after=[m_r, m_pstt[par]], sig=False)
                    m_mt1 = sch.add("scalar", lambda e, h=h, par=par, cch=cch, base=base: e.activation(
                        out=ps_mt[:, base + 256:base + 384],
                        in_=psd_of(par)[:, (par % 2) * 256 + 128:(par % 2) * 256 + 256],
                        func=AF.Exp, bias=negacs[:, cch * 128 + 64 + h:cch * 128 + 64 + h + 1]),
                        sig=False)
                    m_et = sch.add("scalar", lambda e, h=h, par=par: e.activation(
                        out=ps_et[:, par * 256:par * 256 + 256],
                        in_=psd_of(par)[:, (par % 2) * 256:(par % 2) * 256 + 256], func=AF.Exp),
                        after=[m_ctt[par]], sig=True)
                    m_exps[par] = m_et
                    m_p = sch.add("vector", lambda e, par=par, g=g, base=base: e.scalar_tensor_tensor(
                        out=psb[par][:, :], in0=ps_mt[:, base:base + 384], scalar=1.0,
                        in1=cbm[:, g * 384:(g + 1) * 384], op0=ALU.min, op1=ALU.mult),
                        after=[m_et, m_ymm[jpar]], sig=True)
                    m_pstt[par] = m_p
                    m_ct = sch.add("gpsimd", lambda e, par=par, g=g, sl2=sl2: e.tensor_tensor(
                        out=ctl[par][:, :], in0=ps_et[:, par * 256:par * 256 + 256],
                        in1=ccm[sl2][:, g * 256:(g + 1) * 256], op=ALU.mult),
                        after=[m_et, m_ymm[jpar]], sig=True)
                    m_ctt[par] = m_ct
                    m_dx = []
                    for half in range(2):
                        m_dx.append(sch.add("vector", lambda e, sl2=sl2, h=h, par=par, half=half, cch=cch: e.tensor_scalar(
                            out=dxs[par][:, half * 64:(half + 1) * 64],
                            in0=xtk3[sl2][:, half * D_INNER + h * 64:half * D_INNER + h * 64 + 64],
                            scalar1=d1tk_all[:, cch * 128 + half * 64 + h:cch * 128 + half * 64 + h + 1],
                            scalar2=None, op0=ALU.mult),
                            after=[lx[1]] + ([m_dxu[par]] if half == 0 else []), sig=True))
                    r0_ = (h % 2) * 64
                    ymm1 = sch.add("tensor", lambda e, sl2=sl2, h=h, par=par, jpar=jpar, r0_=r0_: e.matmul(
                        ps_y[jpar][r0_:r0_ + 64, 0:256],
                        xtk3[sl2][:, h * 64:h * 64 + 64], psb[par][:, 0:256],
                        start=True, stop=False),
                        after=[m_p, m_yev[jpar] if h % 2 == 0 else None,
                               lx[1] if h == 0 else None], sig=False)
                    ymm2 = sch.add("tensor", lambda e, sl2=sl2, h=h, par=par, jpar=jpar, r0_=r0_: e.matmul(
                        ps_y[jpar][r0_:r0_ + 64, 128:256],
                        xtk3[sl2][:, D_INNER + h * 64:D_INNER + h * 64 + 64], psb[par][:, 256:384],
                        start=False, stop=False), sig=False)
                    hc = hcur
                    ymm3 = sch.add("tensor", lambda e, h=h, par=par, jpar=jpar, r0_=r0_, hc=hc: e.matmul(
                        ps_y[jpar][r0_:r0_ + 64, 0:256],
                        hc[:, h * 64:(h + 1) * 64], ctl[par][:, :],
                        start=False, stop=True),
                        after=[m_ct, m_hcomb if cch == 0 and h < 2 else None], sig=True)
                    m_ymm[jpar] = ymm3
                    spar = h % 2
                    smm = None
                    for half in range(2):
                        smm = sch.add("tensor", lambda e, sl2=sl2, h=h, par=par, g=g, half=half, spar=spar: e.matmul(
                            ps_s[spar][:, 0:64],
                            btk3[sl2][:, half * GN + g * 128:half * GN + g * 128 + 128],
                            dxs[par][:, half * 64:(half + 1) * 64],
                            start=(half == 0), stop=(half == 1)),
                            after=([m_dx[1], lb[1], m_hstt[spar], m_rmsmm if spar == 0 else None]
                                   if half == 0 else []), sig=(half == 1))
                    m_smm[spar] = smm
                    m_dxu[par] = smm
                    m_hstt[spar] = sch.add("vector", lambda e, h=h, spar=spar, cch=cch, hc=hcur, hn=hnxt: e.scalar_tensor_tensor(
                        out=hn[:, h * 64:(h + 1) * 64], in0=hc[:, h * 64:(h + 1) * 64],
                        scalar=st_cdrep[:, h * NCHUNK + cch:h * NCHUNK + cch + 1],
                        in1=ps_s[spar][:, 0:64], op0=ALU.mult, op1=ALU.add),
                        after=[smm] + ([m_hcomb] if cch == 0 and h < 2 else []), sig=True)
                    if h % 2 == 1:
                        jj = h // 2
                        m_yev[jpar] = sch.add("scalar", lambda e, jpar=jpar, jj=jj: e.activation(
                            out=ych[:, jj * 256:(jj + 1) * 256], in_=ps_y[jpar][:, :],
                            func=AF.Copy), after=[ymm3, m_ygdma], sig=True)
                m_xu[sl2] = smm
                m_bu[sl2] = smm
                hcur, hnxt = hnxt, hcur
                m_sq = sch.add("scalar", lambda e: e.activation(
                    out=ybuf[:, :], in_=ych[:, :], func=AF.Square),
                    after=[m_yev[0], m_yev[1], m_ygdma], sig=True)
                mrms = None
                for jb in range(NYB):
                    mrms = sch.add("tensor", lambda e, jb=jb: e.matmul(
                        ps_s[0][:, :], c_onesT[:, :], ybuf[:, jb * 256:(jb + 1) * 256],
                        start=(jb == 0), stop=(jb == NYB - 1)),
                        after=([m_sq, m_hstt[0], m_grb] if jb == 0 else []), sig=(jb == NYB - 1))
                m_rmsmm = mrms
                mg1 = sch.add("vector", lambda e: e.tensor_scalar(
                    out=grf[:, :], in0=ps_s[0][:, :], scalar1=1.0 / D_INNER, scalar2=1e-6,
                    op0=ALU.mult, op1=ALU.add), after=[mrms], sig=True)
                mg2 = sch.add("vector", lambda e: e.reciprocal(out=grf[:, :], in_=grf[:, :]),
                              after=[mg1], sig=True)
                mg3 = sch.add("scalar", lambda e: e.activation(out=grf[:, :], in_=grf[:, :],
                              func=AF.Sqrt), after=[mg2], sig=True)
                m_grb = sch.add("vector", lambda e: e.tensor_scalar(
                    out=grb[:, :], in0=grf[:, :], scalar1=1.0, scalar2=None, op0=ALU.mult),
                    after=[mg3], sig=True)
                m_sg = sch.add("scalar", lambda e: e.activation(
                    out=gcm[:, :], in_=gcm[:, :], func=AF.Silu), after=[lg, m_sq], sig=True)
                myg1 = sch.add("vector", lambda e: e.tensor_tensor(
                    out=ybuf[:, :], in0=ych[:, :],
                    in1=bass.AP(grb, 0, [[256, 128], [0, NYB], [1, 256]]),
                    op=ALU.mult), after=[m_grb, m_rmsmm], sig=True)
                myg2 = sch.add("vector", lambda e: e.tensor_tensor(
                    out=ybuf[:, :], in0=ybuf[:, :], in1=gcm[:, :], op=ALU.mult),
                    after=[myg1, m_sg], sig=True)
                m_gu = myg2
                m_ychu = myg2
                myd = sch.dma("gpsimd",
                              bass.AP(s_yg.tensor, tb, [[TCORE, 128], [128 * TCORE, NYB], [1, CHUNK]]),
                              ybuf[:, :], stream="p3yg", after=[myg2])
                if dbg_level == 3:
                    myd = sch.dma("gpsimd",
                                  bass.AP(dbg["yg"].tensor, tb, [[TCORE, 128], [128 * TCORE, NYB], [1, CHUNK]]),
                                  ybuf[:, :], stream="p3yg", after=[myg2])
                m_ygdma = myd
            sch.emit(block)

        if dbg_level in (2, 3):
            fin = Sched(ctx)
            z = stack.enter_context(nc.sbuf_tensor("zz", [128, D_MODEL], F32))
            wl = [ctx.full(s) for s in ("p3yg", "p4hd0", "p4hd1", "dagd")] + ([ctx.full("dacs")] if dbg_level in (1,2,3) else [])
            ms = fin.add("vector", lambda e: e.memset(z[:, :], 0.0), after=wl, sig=True)
            for i in range(TCORE // 128):
                fin.dma("gpsimd", out[i * 128:(i + 1) * 128, :], z[:, :], stream="fino", after=[ms])
            fms = ctx.full("fino")
            fin.add("gpsimd", lambda e, fms=fms: e.wait_ge(ctx.sems[fms[0]], fms[1]))
            fin.emit(block)
            return nc

        mstack.close()
        sch.barrier()
        NKTOT = NYB + NIB
        with contextlib.ExitStack() as ph:
            c_dsrep = ph.enter_context(nc.sbuf_tensor("c_dsrep", [128, D_MODEL], F32))
            mdsl = sch.dma("sync", c_dsrep[:, :], ds_rep, stream="p5c")
            w5 = ph.enter_context(nc.sbuf_tensor("p5_w", [128, NKTOT * 1024], BF16))
            rhs5 = [ph.enter_context(nc.sbuf_tensor(f"p5_r{i}", [128, 512], BF16)) for i in range(3)]
            ost = [ph.enter_context(nc.sbuf_tensor(f"p5_o{i}", [128, 512], BF16)) for i in range(2)]
            ps5 = [ph.enter_context(nc.psum_tensor(f"p5_ps{i}", [128, 512], F32)) for i in range(8)]
            m_prev_half = None
            mm5 = None
            for dhalf in range(2):
                m_wsc = []
                for k in range(NKTOT):
                    if k < NYB:
                        ap = bass.AP(op_wT.tensor, k * 128 * D_MODEL + dhalf * 1024,
                                     [[D_MODEL, 128], [1, 1024]])
                    else:
                        ap = bass.AP(down_wT.tensor, (k - NYB) * 128 * D_MODEL + dhalf * 1024,
                                     [[D_MODEL, 128], [1, 1024]])
                    lw = sch.dma("gpsimd", w5[:, k * 1024:(k + 1) * 1024], ap,
                                 stream=f"p5w{k % 4}",
                                 after=([m_prev_half] if k == 0 else []) +
                                       ([m_wsc[k - 4]] if k >= 4 else []))
                    if k < NYB:
                        m_wsc.append(sch.add("vector", lambda e, k=k: e.tensor_scalar(
                            out=w5[:, k * 1024:(k + 1) * 1024], in0=w5[:, k * 1024:(k + 1) * 1024],
                            scalar1=c_won[:, k:k + 1], scalar2=None, op0=ALU.mult),
                            after=[lw], sig=True))
                    else:
                        m_wsc.append(sch.add("vector", lambda e, k=k, dhalf=dhalf: e.tensor_tensor(
                            out=w5[:, k * 1024:(k + 1) * 1024], in0=w5[:, k * 1024:(k + 1) * 1024],
                            in1=c_dsrep[:, dhalf * 1024:(dhalf + 1) * 1024], op=ALU.mult),
                            after=[lw] + ([mdsl] if k == NYB else []), sig=True))
                wsc_ms = m_wsc[-1]
                m_ev5 = [None] * 8
                mm_slot = {0: None, 1: None, 2: None}
                for ttile in range(8):
                    for k in range(NKTOT):
                        if k < NYB:
                            rap = s_yg[k * 128:(k + 1) * 128, ttile * 512:(ttile + 1) * 512]
                        else:
                            rap = s_hdn[(k - NYB) * 128:(k - NYB + 1) * 128, ttile * 512:(ttile + 1) * 512]
                        rsl = k % 3
                        lr = sch.dma("sync", rhs5[rsl][:, :], rap, stream=f"p5r{rsl}",
                                     after=[mm_slot[rsl] or mm5, ctx.full(f"p5r{rsl}")])
                        for d in range(8):
                            mm5 = sch.add("tensor", lambda e, k=k, d=d, rsl=rsl: e.matmul(
                                ps5[d][:, :], w5[:, k * 1024 + d * 128:k * 1024 + d * 128 + 128],
                                rhs5[rsl][:, :], start=(k == 0), stop=(k == NKTOT - 1)),
                                after=([lr, m_wsc[k]] + (m_ev5 if k == 0 else []) if d == 0 else []),
                                sig=(d == 7))
                        mm_slot[rsl] = mm5
                    for d in range(8):
                        osl = d % 2
                        mev = sch.add("vector", lambda e, d=d, osl=osl: e.tensor_scalar(
                            out=ost[osl][:, :], in0=ps5[d][:, :], scalar1=1.0, scalar2=None,
                            op0=ALU.mult), after=[mm5, ctx.full(f"p5o{osl}")], sig=True)
                        m_ev5[d] = mev
                        sch.dma("scalar", s_inner[(dhalf * 8 + d) * 128:(dhalf * 8 + d) * 128 + 128,
                                                  ttile * 512:(ttile + 1) * 512],
                                ost[osl][:, :], stream=f"p5o{osl}", after=[mev])
                m_prev_half = mm5
            sch.emit(block)

        sch.barrier()
        with contextlib.ExitStack() as ph:
            it6 = [ph.enter_context(nc.sbuf_tensor(f"p6_i{i}", [128, D_MODEL], BF16)) for i in range(2)]
            r06 = [ph.enter_context(nc.sbuf_tensor(f"p6_r{i}", [128, D_MODEL], F32)) for i in range(2)]
            ot6 = [ph.enter_context(nc.sbuf_tensor(f"p6_o{i}", [128, D_MODEL], F32)) for i in range(2)]
            m_add = [None, None]
            for i in range(TCORE // 128):
                sl = i % 2
                li = sch.dma("sync", it6[sl][:, :],
                             s_inner[0:D_MODEL, i * 128:(i + 1) * 128], transpose=True,
                             stream=f"p6i{sl}", after=[m_add[sl]])
                lr = sch.dma("scalar", r06[sl][:, :], s_r0[i * 128:(i + 1) * 128, :],
                             stream=f"p6r{sl}", after=[m_add[sl]])
                ma = sch.add("vector", lambda e, sl=sl: e.tensor_tensor(
                    out=ot6[sl][:, :], in0=r06[sl][:, :], in1=it6[sl][:, :], op=ALU.add),
                    after=[li, lr, ctx.full(f"p6o{sl}")], sig=True)
                m_add[sl] = ma
                sch.dma("gpsimd", out[i * 128:(i + 1) * 128, :], ot6[sl][:, :],
                        stream=f"p6o{sl}", after=[ma])
            for slx in range(2):
                fms = ctx.full(f"p6o{slx}")
                sch.add("gpsimd", lambda e, fms=fms: e.wait_ge(ctx.sems[fms[0]], fms[1]))
            sch.emit(block)


    return nc



def _prep_inputs(inputs):
    x = inputs["L0_attn_x"]
    W = inputs
    ip_w = W["L0_attn_in_proj_w"]
    ip_wT_full = np.ascontiguousarray(ip_w.T)
    ip_wT = np.zeros((D_MODEL, IP_COLS), np.float32)
    ip_wT[:, 0:4096] = ip_wT_full[:, 0:4096]
    ip_wT[:, 4096:5120] = ip_wT_full[:, 4160:5184]
    ip_wT[:, 5120:6144] = ip_wT_full[:, 5184:6208]
    ip_wT[:, 6208:6272] = ip_wT_full[:, 4096:4160]
    ip_wT[:, 6272:10368] = ip_wT_full[:, 6208:10304]

    convw_r = np.ascontiguousarray(
        W["L0_attn_conv_w"][:, 0, :].reshape(NCB_XBC, 128, 4).transpose(1, 0, 2).reshape(128, NCB_XBC * 4))
    convb_r = np.ascontiguousarray(W["L0_attn_conv_b"].reshape(NCB_XBC, 128).T)
    cnw_r = np.ascontiguousarray(W["L0_attn_conv_norm_w"].reshape(NCB_XBC, 128).T)
    won_r = np.ascontiguousarray(W["L0_attn_out_norm_w"].reshape(NYB, 128).T)
    op_wT = np.ascontiguousarray(W["L0_attn_out_proj_w"].T)
    gate_wT = np.ascontiguousarray(W["L0_ffn_gate_w"].T)
    up_wT = np.ascontiguousarray(W["L0_ffn_up_w"].T)
    down_wT = np.ascontiguousarray(W["L0_ffn_down_w"].T)
    gs_r = np.ascontiguousarray(W["L0_ffn_gate_s"].reshape(NIB, 128).T)
    us_r = np.ascontiguousarray(W["L0_ffn_up_s"].reshape(NIB, 128).T)
    ds_rep = np.ascontiguousarray(np.broadcast_to(W["L0_ffn_down_s"], (128, D_MODEL)))
    nw_rep = np.ascontiguousarray(np.broadcast_to(W["L0_attn_norm_w"], (128, D_MODEL)))

    eye = np.eye(128, dtype=BF)
    ones8 = np.ones((8, 128), BF)
    onesT = np.ones((128, 128), BF)
    cbm = np.zeros((128, 384), BF)
    p = np.arange(128)[:, None]
    c = np.arange(256)[None, :]
    cbm[:, 0:256] = (c >= p).astype(BF)
    cbm[:, 256:384] = (c[:, :128] >= p).astype(BF)

    shared = {
        "ip_wT": ip_wT, "convw_r": convw_r, "convb_r": convb_r, "cnw_r": cnw_r,
        "a_log": W["L0_attn_A_log"].reshape(H, 1).astype(np.float32),
        "dt_b": W["L0_attn_dt_b"].reshape(H, 1).astype(np.float32),
        "dt_wT": np.ascontiguousarray(W["L0_attn_dt_w"].T),
        "won_r": won_r, "op_wT": op_wT,
        "gate_wT": gate_wT, "up_wT": up_wT, "down_wT": down_wT,
        "gs_r": gs_r, "us_r": us_r, "ds_rep": ds_rep, "nw_rep": nw_rep,
        "eye128": eye, "ones8": ones8, "onesT": onesT, "cb_mask": cbm,
        "ones1f": np.ones((1, 128), np.float32),
        "eye128f": np.eye(128, dtype=np.float32),
    }

    in_maps = []
    for core in range(8):
        b, q = core // 4, core % 4
        t0 = q * TCORE
        xh = np.zeros((TT, D_MODEL), np.float32)
        if q > 0:
            xh[:HALO] = x[b, t0 - HALO:t0]
        xh[HALO:] = x[b, t0:t0 + TCORE]
        rm = (np.arange(4)[None, :] < q).astype(np.float32)
        m = dict(shared)
        m["x_h"] = xh
        m["rmask"] = np.ascontiguousarray(np.broadcast_to(rm, (128, 4)))
        m["rmask1"] = np.ascontiguousarray(np.broadcast_to(1.0 - rm, (128, 4)))
        in_maps.append(m)
    return in_maps


def kernel(**inputs):
    dbg_level = inputs.pop("_dbg_level", 99)
    in_maps = _prep_inputs(inputs)
    nc = build_program(dbg_level=dbg_level)
    res = run_bass_kernel_spmd(nc, in_maps, core_ids=list(range(8)))
    outs = [res.results[c]["out"] for c in range(8)]
    full = np.empty((2, 16384, D_MODEL), np.float32)
    for core in range(8):
        b, q = core // 4, core % 4
        full[b, q * TCORE:(q + 1) * TCORE] = outs[core]
    if dbg_level < 99:
        kernel._dbg = res.results
    return full


# version 09
# speedup vs baseline: 1.1046x; 1.0097x over previous
import contextlib
import numpy as np
import ml_dtypes

import concourse.bass as bass
import concourse.mybir as mybir
from concourse.bass_utils import run_bass_kernel_spmd

F32 = mybir.dt.float32
BF16 = mybir.dt.bfloat16
AF = mybir.ActivationFunctionType
ALU = mybir.AluOpType
BF = ml_dtypes.bfloat16

D_MODEL = 2048
D_INNER = 4096
H = 64
D_HEAD = 64
NSTATE = 128
G = 8
CHUNK = 256
GN = G * NSTATE
CONV_DIM = D_INNER + 2 * GN
IP_COLS = 10368
INTER = 5632
TCORE = 4096
HALO = 128
TT = TCORE + HALO
NCHUNK = TCORE // CHUNK
NCB_XBC = CONV_DIM // 128
NCB_ALL = IP_COLS // 128
NKB = D_MODEL // 128
NIB = INTER // 128
NYB = D_INNER // 128

ENGINES = ("tensor", "vector", "scalar", "gpsimd", "sync")
DMA_QUEUES = ("sync", "scalar", "gpsimd", "vector")


class Sched:

    def __init__(self, ctx):
        self.ctx = ctx
        self.items = {e: [] for e in ENGINES}
        self.pending_barrier = {}

    def barrier(self):
        ms = [(n, c) for (n, c) in self.ctx.cnt.items() if c > 0]
        self.pending_barrier = {e: list(ms) for e in ENGINES}

    def add(self, eng, fn, after=(), sig=False, stream=None):
        ctx = self.ctx
        waits = []
        for m in after:
            if m is None:
                continue
            (sem, val) = m
            if sem == "e_" + eng:
                assert val <= ctx.cnt[sem], f"same-engine forward wait {sem} {val} > {ctx.cnt[sem]}"
            if sem.startswith("s_"):
                assert val == ctx.cnt[sem], \
                    f"unsound prefix wait on dma stream {sem}: {val} != {ctx.cnt[sem]}"
            waits.append((sem, val))
        pb = self.pending_barrier.pop(eng, None)
        if pb:
            seen = {s for (s, _) in waits}
            waits = [(s, v) for (s, v) in pb if s not in seen] + waits
        if eng in ("vector", "scalar") and stream is None:
            sig = True
        rec = {"fn": fn, "waits": waits, "inc": None}
        ms = None
        if stream is not None:
            sem = ctx.stream(stream)
            ctx.cnt[sem] += 16
            rec["inc"] = (sem, 16)
            ms = (sem, ctx.cnt[sem])
        elif sig:
            sem = "e_" + eng
            ctx.cnt[sem] += 1
            rec["inc"] = (sem, 1)
            ms = (sem, ctx.cnt[sem])
        self.items[eng].append(rec)
        return ms

    def dma(self, eng, out, in_, stream=None, after=(), **kw):
        assert stream is not None, "every dma needs a stream"
        return self.add(eng, lambda e: e.dma_start(out=out, in_=in_, **kw),
                        after=after, stream=stream)

    def emit(self, block):
        ctx = self.ctx
        for eng in ENGINES:
            recs = self.items[eng]
            if not recs:
                continue

            def make(recs):
                def body(e):
                    last_wait = {}
                    for r in recs:
                        for (sem, val) in r["waits"]:
                            if last_wait.get(sem, -1) >= val:
                                continue
                            last_wait[sem] = val
                            e.wait_ge(ctx.sems[sem], val)
                        instr = r["fn"](e)
                        if r["inc"] is not None:
                            sem, n = r["inc"]
                            assert instr is not None, "inc on op returning no instruction"
                            instr.then_inc(ctx.sems[sem], n)
                return body

            getattr(block, eng)(make(recs))
        self.items = {e: [] for e in ENGINES}


class Ctx:
    def __init__(self, nc):
        self.nc = nc
        self.sems = {}
        self.cnt = {}

    def make_sems(self, stack):
        self.stack = stack
        names = ["e_" + e for e in ENGINES] + ["cc"]
        for n in names:
            self.sems[n] = stack.enter_context(self.nc.semaphore(n))
            self.cnt[n] = 0

    def stream(self, name):
        n = "s_" + name
        if n not in self.sems:
            self.sems[n] = self.stack.enter_context(self.nc.semaphore(n))
            self.cnt[n] = 0
        return n

    def full(self, name):
        n = self.stream(name)
        return (n, self.cnt[n])


def ceil_div(a, b):
    return (a + b - 1) // b



def build_program(dbg_level=99):
    nc = bass.Bass("TRN2", target_bir_lowering=False)

    def din(name, shape, dt=F32):
        return nc.dram_tensor(name, shape, dt, kind="ExternalInput").ap()

    x_h = din("x_h", [TT, D_MODEL])
    ip_wT = din("ip_wT", [D_MODEL, IP_COLS])
    convw_r = din("convw_r", [128, NCB_XBC * 4])
    convb_r = din("convb_r", [128, NCB_XBC])
    cnw_r = din("cnw_r", [128, NCB_XBC])
    a_log = din("a_log", [H, 1])
    dt_b = din("dt_b", [H, 1])
    dt_wT = din("dt_wT", [H, H])
    won_r = din("won_r", [128, NYB])
    op_wT = din("op_wT", [D_INNER, D_MODEL])
    gate_wT = din("gate_wT", [D_MODEL, INTER])
    up_wT = din("up_wT", [D_MODEL, INTER])
    down_wT = din("down_wT", [INTER, D_MODEL])
    gs_r = din("gs_r", [128, NIB])
    us_r = din("us_r", [128, NIB])
    ds_rep = din("ds_rep", [128, D_MODEL])
    nw_rep = din("nw_rep", [128, D_MODEL])
    eye128 = din("eye128", [128, 128], BF16)
    ones8 = din("ones8", [8, 128], BF16)
    ones1f = din("ones1f", [1, 128])
    eye128f = din("eye128f", [128, 128])
    onesT = din("onesT", [128, 128], BF16)
    cb_mask = din("cb_mask", [128, 384], BF16)
    rmask = din("rmask", [128, 4])
    rmask1 = din("rmask1", [128, 4])

    out = nc.dram_tensor("out", [TCORE, D_MODEL], F32, kind="ExternalOutput").ap()

    s_xr = nc.dram_tensor("s_xr", [TT, D_MODEL], BF16).ap()
    s_xn = nc.dram_tensor("s_xn", [TCORE, D_MODEL], BF16).ap()
    s_r0 = nc.dram_tensor("s_r0", [TCORE, D_MODEL], F32).ap()
    s_co = nc.dram_tensor("s_co", [CONV_DIM, TCORE], BF16).ap()
    s_xbc = nc.dram_tensor("s_xbc", [CONV_DIM, TCORE], BF16).ap()
    s_gate = nc.dram_tensor("s_gate", [D_INNER, TCORE], BF16).ap()
    s_hdn = nc.dram_tensor("s_hdn", [INTER, TCORE], BF16).ap()
    s_yg = nc.dram_tensor("s_yg", [D_INNER, TCORE], BF16).ap()
    s_inner = nc.dram_tensor("s_inner", [D_MODEL, TCORE], BF16).ap()
    ag_src_t = nc.dram_tensor("ag_src", [H * NSTATE + H, D_HEAD], F32)
    ag_dst_t = nc.dram_tensor("ag_dst", [4 * (H * NSTATE + H), D_HEAD], F32)
    ag_src = ag_src_t.ap()
    ag_dst = ag_dst_t.ap()

    dbg = {}
    if dbg_level < 99:
        dbg["xn"] = nc.dram_tensor("d_xn", [TCORE, D_MODEL], BF16, kind="ExternalOutput").ap()
        dbg["xr"] = nc.dram_tensor("d_xr", [TT, D_MODEL], BF16, kind="ExternalOutput").ap()
        dbg["r0"] = nc.dram_tensor("d_r0", [TCORE, D_MODEL], F32, kind="ExternalOutput").ap()
    if dbg_level in (1, 2, 3):
        dbg["co"] = nc.dram_tensor("d_co", [CONV_DIM, TCORE], BF16, kind="ExternalOutput").ap()
        dbg["xbc"] = nc.dram_tensor("d_xbc", [CONV_DIM, TCORE], BF16, kind="ExternalOutput").ap()
        dbg["gate"] = nc.dram_tensor("d_gate", [D_INNER, TCORE], BF16, kind="ExternalOutput").ap()
        dbg["acs"] = nc.dram_tensor("d_acs", [H, TCORE], F32, kind="ExternalOutput").ap()
    if dbg_level in (2, 3):
        dbg["agd"] = nc.dram_tensor("d_agd", [4 * (H * NSTATE + H), D_HEAD], F32, kind="ExternalOutput").ap()
    if dbg_level == 3:
        dbg["yg"] = nc.dram_tensor("d_yg", [D_INNER, TCORE], BF16, kind="ExternalOutput").ap()
        dbg["hdn"] = nc.dram_tensor("d_hdn", [INTER, TCORE], BF16, kind="ExternalOutput").ap()

    ctx = Ctx(nc)

    with contextlib.ExitStack() as stack:
        ctx.make_sems(stack)
        block = stack.enter_context(nc.Block())
        sch = Sched(ctx)

        sb = lambda name, shape, dt=F32: stack.enter_context(nc.sbuf_tensor(name, shape, dt))
        c_convw = sb("c_convw", [128, NCB_XBC * 4])
        c_convb = sb("c_convb", [128, NCB_XBC])
        c_cnw = sb("c_cnw", [128, NCB_XBC])
        c_alog = sb("c_alog", [H, 1])
        c_dtb = sb("c_dtb", [H, 1])
        c_dtwT = sb("c_dtwT", [H, H], BF16)
        c_won = sb("c_won", [128, NYB])
        c_gs = sb("c_gs", [128, NIB])
        c_us = sb("c_us", [128, NIB])
        c_eye = sb("c_eye", [128, 128], BF16)
        c_ones8 = sb("c_ones8", [8, 128], BF16)
        c_onesf = sb("c_onesf", [1, 128])
        c_eyef = sb("c_eyef", [128, 128])
        c_onesT = sb("c_onesT", [128, 128], BF16)
        c_cbmask = sb("c_cbmask", [128, 384], BF16)
        c_rmask = sb("c_rmask", [128, 4])
        c_rmask1 = sb("c_rmask1", [128, 4])
        mstack = stack.enter_context(contextlib.ExitStack())
        msb = lambda name, shape, dt=F32: mstack.enter_context(nc.sbuf_tensor(name, shape, dt))
        mstackA = stack.enter_context(contextlib.ExitStack())
        st_hi = msb("st_hi", [H, TCORE], BF16)
        st_lo = msb("st_lo", [H, TCORE], BF16)
        st_alast = msb("st_alast", [H, NCHUNK])
        st_offs = msb("st_offs", [H, NCHUNK])
        st_wb = msb("st_wb", [H, NCHUNK])
        st_lend = msb("st_lend", [H, 1])
        st_cdrep = msb("st_cdrep", [128, NCHUNK * H])
        hA = msb("p3_hA", [128, H * D_HEAD], BF16)
        hB = msb("p3_hB", [128, H * D_HEAD], BF16)
        d1tk_all = msb("d1tk_all", [128, NCHUNK * 128], F32)
        negacs = msb("negacs", [128, NCHUNK * 128], F32)
        st_adt = mstackA.enter_context(nc.sbuf_tensor("st_adt", [H, TCORE], BF16))
        st_acs = mstackA.enter_context(nc.sbuf_tensor("st_acs", [H, TCORE], F32))
        wtk_all = mstackA.enter_context(nc.sbuf_tensor("wtk_all", [128, NCHUNK * 128], BF16))

        pre = [
            (c_convw, convw_r), (c_convb, convb_r), (c_cnw, cnw_r),
            (c_alog, a_log), (c_dtb, dt_b), (c_won, won_r),
            (c_gs, gs_r), (c_us, us_r),
            (c_eye, eye128), (c_ones8, ones8), (c_onesT, onesT), (c_onesf, ones1f),
            (c_eyef, eye128f),
            (c_cbmask, cb_mask), (c_rmask, rmask), (c_rmask1, rmask1),
        ]
        for (dst, srcap) in pre:
            sch.dma("sync", dst[:, :], srcap, stream="cld")
        sch.dma("gpsimd", c_dtwT[:, :], dt_wT, stream="cld2")
        ALL_CONSTS = [ctx.full("cld"), ctx.full("cld2")]

        NT0 = TT // 128
        import os
        NT0 = min(NT0, int(os.environ.get("K_NT0", "99")))
        with contextlib.ExitStack() as ph:
            c_nwrep = ph.enter_context(nc.sbuf_tensor("c_nwrep", [128, D_MODEL], F32))
            sch.dma("sync", c_nwrep[:, :], nw_rep, stream="cld")
            ALL_CONSTS = [ctx.full("cld"), ctx.full("cld2")]
            NB0 = 3
            xt = [ph.enter_context(nc.sbuf_tensor(f"p0_x{i}", [128, D_MODEL], F32)) for i in range(NB0)]
            xnb = [ph.enter_context(nc.sbuf_tensor(f"p0_xn{i}", [128, D_MODEL], BF16)) for i in range(NB0)]
            r0t = [ph.enter_context(nc.sbuf_tensor(f"p0_r0{i}", [128, D_MODEL], F32)) for i in range(NB0)]
            xrt = [ph.enter_context(nc.sbuf_tensor(f"p0_xr{i}", [128, D_MODEL], BF16)) for i in range(NB0)]
            bst = ph.enter_context(nc.sbuf_tensor("p0_bst", [128, 24], F32))
            mv = ph.enter_context(nc.sbuf_tensor("p0_mv", [128, 2], F32))
            tiny = ph.enter_context(nc.sbuf_tensor("p0_tiny", [128, 8], F32))
            last_use = [[] for _ in range(NB0)]
            for i in range(NT0):
                b = i % NB0
                ld = sch.dma("sync", xt[b][:, :], x_h[i * 128:(i + 1) * 128, :],
                             stream=f"p0ld{b}",
                             after=(ALL_CONSTS if i == 0 else []) + last_use[b])
                last_use[b] = []
                ms_bs = None
                for j in range(4):
                    ms_bs = sch.add("vector", lambda e, b=b, j=j: e.bn_stats(
                        out=bst[:, j * 6:(j + 1) * 6], in_=xt[b][:, j * 512:(j + 1) * 512]),
                        after=[ld], sig=True)
                ms_ag = sch.add("vector", lambda e: e.bn_aggr(out=mv[:, :], in_=bst[:, :]),
                                after=[ms_bs], sig=True)
                ms_e1 = sch.add("vector", lambda e: e.tensor_scalar(
                    out=tiny[:, 0:1], in0=mv[:, 1:2], scalar1=1e-5, scalar2=None, op0=ALU.add),
                    after=[ms_ag], sig=True)
                ms_r1 = sch.add("vector", lambda e: e.reciprocal(out=tiny[:, 1:2], in_=tiny[:, 0:1]),
                                after=[ms_e1], sig=True)
                ms_rstd = sch.add("scalar", lambda e: e.activation(
                    out=tiny[:, 2:3], in_=tiny[:, 1:2], func=AF.Sqrt), after=[ms_r1], sig=True)
                ms_fsq = sch.add("vector", lambda e: e.tensor_scalar(
                    out=tiny[:, 3:4], in0=mv[:, 1:2], scalar1=tiny[:, 1:2], scalar2=1e-6,
                    op0=ALU.mult, op1=ALU.add), after=[ms_r1], sig=True)
                ms_f0 = sch.add("vector", lambda e: e.reciprocal(out=tiny[:, 4:5], in_=tiny[:, 3:4]),
                                after=[ms_fsq], sig=True)
                ms_f = sch.add("scalar", lambda e: e.activation(
                    out=tiny[:, 4:5], in_=tiny[:, 4:5], func=AF.Sqrt), after=[ms_f0], sig=True)
                ms_nb = sch.add("vector", lambda e: e.tensor_scalar(
                    out=tiny[:, 5:6], in0=mv[:, 0:1], scalar1=tiny[:, 2:3], scalar2=-1.0,
                    op0=ALU.mult, op1=ALU.mult), after=[ms_rstd], sig=True)
                ms_r1p = sch.add("vector", lambda e: e.tensor_scalar(
                    out=tiny[:, 6:7], in0=tiny[:, 2:3], scalar1=1.0, scalar2=None, op0=ALU.add),
                    after=[ms_rstd], sig=True)
                ms_xn = sch.add("scalar", lambda e, b=b: e.activation(
                    out=xnb[b][:, :], in_=xt[b][:, :], func=AF.Identity,
                    bias=tiny[:, 5:6], scale=tiny[:, 2:3]), after=[ms_nb], sig=True)
                ms_r0 = None
                if i >= 1:
                    ms_r0 = sch.add("scalar", lambda e, b=b: e.activation(
                        out=r0t[b][:, :], in_=xt[b][:, :], func=AF.Identity,
                        bias=tiny[:, 5:6], scale=tiny[:, 6:7]), after=[ms_r1p, ms_nb], sig=True)
                ms_xr = sch.add("vector", lambda e, b=b: e.scalar_tensor_tensor(
                    out=xrt[b][:, :], in0=xnb[b][:, :], scalar=tiny[:, 4:5], in1=c_nwrep[:, :],
                    op0=ALU.mult, op1=ALU.mult), after=[ms_xn, ms_f], sig=True)
                sch.dma("gpsimd", s_xr[i * 128:(i + 1) * 128, :], xrt[b][:, :],
                        stream=f"p0xr{b}", after=[ms_xr])
                if dbg_level < 99:
                    sch.dma("gpsimd", dbg["xr"][i * 128:(i + 1) * 128, :], xrt[b][:, :],
                            stream=f"p0xr{b}", after=[ms_xr])
                if i >= 1:
                    t0 = (i - 1) * 128
                    sch.dma("scalar", s_xn[t0:t0 + 128, :], xnb[b][:, :],
                            stream=f"p0st{b}", after=[ms_xn])
                    sch.dma("scalar", s_r0[t0:t0 + 128, :], r0t[b][:, :],
                            stream=f"p0st{b}", after=[ms_r0])
                    if dbg_level < 99:
                        sch.dma("scalar", dbg["xn"][t0:t0 + 128, :], xnb[b][:, :],
                                stream=f"p0st{b}", after=[ms_xn])
                        sch.dma("scalar", dbg["r0"][t0:t0 + 128, :], r0t[b][:, :],
                                stream=f"p0st{b}", after=[ms_r0])
                last_use[b] = [ms_xr, ctx.full(f"p0xr{b}"), ctx.full(f"p0st{b}")]
            sch.emit(block)

        if dbg_level == 0:
            fin = Sched(ctx)
            z = stack.enter_context(nc.sbuf_tensor("zz", [128, D_MODEL], F32))
            wl = [ctx.full(f"p0st{b}") for b in range(NB0)] + [ctx.full(f"p0xr{b}") for b in range(NB0)]
            ms = fin.add("vector", lambda e: e.memset(z[:, :], 0.0), after=wl, sig=True)
            for i in range(TCORE // 128):
                fin.dma("gpsimd", out[i * 128:(i + 1) * 128, :], z[:, :],
                        stream="fino", after=[ms])
            fms = ctx.full("fino")
            fin.add("gpsimd", lambda e, fms=fms: e.wait_ge(ctx.sems[fms[0]], fms[1]))
            fin.emit(block)
            return nc

        sch.barrier()
        import os
        NCB_RUN = min(NCB_ALL, int(os.environ.get("K_CB", "99")))
        NSUP = min(2, int(os.environ.get("K_SUP", "2")))
        with contextlib.ExitStack() as ph:
            xrT = ph.enter_context(nc.sbuf_tensor("p1_xrT", [128, NKB * 2048], BF16))
            xrTh = ph.enter_context(nc.sbuf_tensor("p1_xrTh", [128, NKB * 128], BF16))
            wsl = [ph.enter_context(nc.sbuf_tensor(f"p1_w{i}", [128, NKB * 128], BF16)) for i in range(2)]

            dg = [ph.enter_context(nc.sbuf_tensor(f"p1_dg{i}", [128, 4 * 128], BF16)) for i in range(2)]
            cin = [ph.enter_context(nc.sbuf_tensor(f"p1_cin{i}", [128, 2051], BF16)) for i in range(2)]
            stash = ph.enter_context(nc.sbuf_tensor("p1_stash", [128, NCB_XBC * 3], BF16))
            cot = [ph.enter_context(nc.sbuf_tensor(f"p1_co{i}", [128, 1024], BF16)) for i in range(2)]
            sqt = [ph.enter_context(nc.sbuf_tensor(f"p1_sq{i}", [128, 1024], BF16)) for i in range(2)]
            gstg = None
            rcb = ph.enter_context(nc.sbuf_tensor("p1_rcb", [128, 1024], BF16))
            rcf = ph.enter_context(nc.sbuf_tensor("p1_rcf", [128, 1024], F32))
            cob = [ph.enter_context(nc.sbuf_tensor(f"p1_cob{i}", [128, 1024], BF16)) for i in range(4)]
            xbb = [ph.enter_context(nc.sbuf_tensor(f"p1_xbb{i}", [128, 1024], BF16)) for i in range(4)]
            ps_ip = [ph.enter_context(nc.psum_tensor(f"p1_pip{i}", [128, 512], F32)) for i in range(2)]
            ps_cv = [ph.enter_context(nc.psum_tensor(f"p1_pcv{i}", [128, 512], F32)) for i in range(2)]
            ps_sq = [ph.enter_context(nc.psum_tensor(f"p1_psq{i}", [128, 512], F32)) for i in range(4)]

            ms_dguse = [None, None]
            p0_done = [ctx.full(f"p0xr{b}") for b in range(NB0)] + [ctx.full(f"p0st{b}") for b in range(NB0)]

            ms_xrh = [sch.dma("sync", xrTh[:, k * 128:(k + 1) * 128],
                              s_xr[0:HALO, k * 128:(k + 1) * 128], transpose=True,
                              stream="p1xrh", after=p0_done) for k in range(NKB)]
            xrh_full = ctx.full("p1xrh")
            W_ENG = "gpsimd"
            wuse = [None, None]
            ms_stash = {}
            ms_cinw = [None, None]
            ms_cinr = [None, None]
            ms_ipevac = [None, None]
            ms_convmm = [None, None]
            ms_coscl = [None, None]
            ms_gevac = [None, None]
            ms_sqmm = None
            ms_rcread = None
            for c in range(min(NCB_XBC, NCB_RUN)):
                sl = c % 2
                wld = sch.dma(W_ENG, wsl[sl][:, :],
                              bass.AP(ip_wT.tensor, c * 128,
                                      [[IP_COLS, 128], [128 * IP_COLS, NKB], [1, 128]]),
                              stream=f"p1w{sl}", after=[wuse[sl]] + (ALL_CONSTS if c < 2 else []))
                lw = [wld, xrh_full] + ([ms_ipevac[0]] if ms_ipevac[0] else [])
                mm = None
                for k in range(NKB):
                    mm = sch.add("tensor", lambda e, sl=sl, k=k: e.matmul(
                        ps_ip[0][:, 0:128], wsl[sl][:, k * 128:(k + 1) * 128],
                        xrTh[:, k * 128:(k + 1) * 128],
                        start=(k == 0), stop=(k == NKB - 1)),
                        after=(lw if k == 0 else []), sig=(k == NKB - 1))
                wuse[sl] = mm
                ms_stash[c] = sch.add("vector", lambda e, c=c: e.tensor_scalar(
                    out=stash[:, c * 3:c * 3 + 3], in0=ps_ip[0][:, 125:128], scalar1=1.0,
                    scalar2=None, op0=ALU.mult), after=[mm], sig=True)
                ms_ipevac[0] = ms_stash[c]

            ms_tt = [None] * 4
            ms_sil = [None] * 4
            p1b_mr4 = None
            for sup in range(NSUP):
                tbase = HALO + sup * 2048
                mbase = sup * 2048
                wl = p0_done if sup == 0 else []
                prev_mm = [m for m in (wuse + ms_ipevac) if m]
                ms_xrT = [sch.dma("sync", xrT[:, k * 2048:(k + 1) * 2048],
                                  s_xr[tbase:tbase + 2048, k * 128:(k + 1) * 128], transpose=True,
                                  stream="p1xrT", after=wl + (prev_mm if k == 0 else []))
                          for k in range(NKB)]
                xrT_full = ctx.full("p1xrT")
                pend_tail = []
                p1b_units = []
                if NCB_RUN >= NCB_XBC:
                    for half in range(2):
                        def rc_unit(half=half):
                            nonlocal ms_rcread, p1b_mr4
                            mr1 = []
                            for tt in range(2):
                                mr1.append(sch.add("vector", lambda e, tt=tt, half=half: e.tensor_scalar(
                                    out=rcf[:, tt * 512:tt * 512 + 512], in0=ps_sq[half * 2 + tt][:, :],
                                    scalar1=1.0 / CONV_DIM, scalar2=1e-6, op0=ALU.mult, op1=ALU.add),
                                    after=[ms_sqmm], sig=True))
                            ms_rcread = mr1[1]
                            mr2 = sch.add("vector", lambda e: e.reciprocal(out=rcf[:, :], in_=rcf[:, :]),
                                          after=[mr1[0], mr1[1]], sig=True)
                            mr3 = sch.add("scalar", lambda e: e.activation(out=rcf[:, :], in_=rcf[:, :],
                                          func=AF.Sqrt), after=[mr2], sig=True)
                            p1b_mr4 = sch.add("vector", lambda e: e.tensor_scalar(
                                out=rcb[:, :], in0=rcf[:, :], scalar1=1.0, scalar2=None, op0=ALU.mult),
                                after=[mr3], sig=True)
                        p1b_units.append(rc_unit)
                        for cb in range(NCB_XBC):
                            def blk(cb=cb, half=half, mbase=mbase):
                                pofs = half * 1024
                                sl = cb % 4
                                ldb = sch.dma("gpsimd", cob[sl][:, :],
                                              s_co[cb * 128:(cb + 1) * 128, mbase + pofs:mbase + pofs + 1024],
                                              stream=f"p1cb{sl}",
                                              after=[ctx.full("p1co0"), ctx.full("p1co1"),
                                                     ms_tt[sl], ms_sil[sl]])
                                mtt = sch.add("vector", lambda e, sl=sl: e.tensor_tensor(
                                    out=cob[sl][:, :], in0=cob[sl][:, :], in1=rcb[:, :], op=ALU.mult),
                                    after=[ldb, p1b_mr4, ms_sil[sl]], sig=True)
                                ms_tt[sl] = mtt
                                msl = sch.add("scalar", lambda e, cb=cb, sl=sl: e.activation(
                                    out=xbb[sl][:, :], in_=cob[sl][:, :], func=AF.Silu,
                                    scale=c_cnw[:, cb:cb + 1]),
                                    after=[mtt, ms_sil[sl], ctx.full(f"p1xb{sl}")], sig=True)
                                ms_sil[sl] = msl
                                sch.dma("scalar",
                                        s_xbc[cb * 128:(cb + 1) * 128, mbase + pofs:mbase + pofs + 1024],
                                        xbb[sl][:, :], stream=f"p1xb{sl}", after=[msl])
                                if dbg_level in (1, 2, 3):
                                    sch.dma("scalar",
                                            dbg["xbc"][cb * 128:(cb + 1) * 128, mbase + pofs:mbase + pofs + 1024],
                                            xbb[sl][:, :], stream=f"p1xb{sl}", after=[msl])
                            p1b_units.append(blk)
                for c in range(NCB_RUN):
                    sl = c % 2
                    is_xbc = c < NCB_XBC
                    is_adt = c == 48
                    wld = sch.dma(W_ENG, wsl[sl][:, :],
                                  bass.AP(ip_wT.tensor, c * 128,
                                          [[IP_COLS, 128], [128 * IP_COLS, NKB], [1, 128]]),
                                  stream=f"p1w{sl}", after=[wuse[sl]])
                    mm = None
                    m_ev = [None, None, None, None]
                    m1 = None
                    for tt in range(4):
                        pb = tt % 2
                        for k in range(NKB):
                            mm = sch.add("tensor", lambda e, sl=sl, k=k, tt=tt, pb=pb: e.matmul(
                                ps_ip[pb][:, :], wsl[sl][:, k * 128:(k + 1) * 128],
                                xrT[:, k * 2048 + tt * 512:k * 2048 + tt * 512 + 512],
                                start=(k == 0), stop=(k == NKB - 1)),
                                after=(([wld, xrT_full] if tt == 0 else []) +
                                       ([ms_ipevac[pb]] if k == 0 else [])),
                                sig=(k == NKB - 1))
                        if is_xbc and tt == 0:
                            m1 = sch.add("vector", lambda e, c=c, sl=sl: e.tensor_scalar(
                                out=cin[sl][:, 0:3], in0=stash[:, c * 3:c * 3 + 3], scalar1=1.0,
                                scalar2=None, op0=ALU.mult),
                                after=[ms_stash[c], ms_cinr[sl]], sig=True)
                        extra = [m1] if (is_xbc and tt == 0) else []
                        if (not is_xbc) and (not is_adt):
                            extra = [ms_gevac[sl], ctx.full(f"p1g{sl}"), ctx.full(f"p1co{sl}")]
                        if is_xbc:
                            dstap = lambda sl=sl, tt=tt: cin[sl][:, 3 + tt * 512:3 + tt * 512 + 512]
                        elif is_adt:
                            dstap = lambda sl=sl, tt=tt, mbase=mbase: st_adt[:, mbase + tt * 512:mbase + tt * 512 + 512]
                        else:
                            dstap = lambda sl=sl, tt=tt: cot[sl][:, (tt % 2) * 512:(tt % 2) * 512 + 512]
                        if is_adt:
                            mev = sch.add("vector", lambda e, pb=pb, dstap=dstap: e.tensor_scalar(
                                out=dstap(), in0=ps_ip[pb][64:128, :], scalar1=1.0,
                                scalar2=None, op0=ALU.mult), after=[mm], sig=True)
                        elif tt % 2 == 0:
                            mev = sch.add("vector", lambda e, pb=pb, dstap=dstap: e.tensor_scalar(
                                out=dstap(), in0=ps_ip[pb][:, :], scalar1=1.0,
                                scalar2=None, op0=ALU.mult), after=[mm] + extra, sig=True)
                        else:
                            mev = sch.add("scalar", lambda e, pb=pb, dstap=dstap: e.activation(
                                out=dstap(), in_=ps_ip[pb][:, :], func=AF.Copy),
                                after=[mm] + extra, sig=True)
                        m_ev[tt] = mev
                        ms_ipevac[pb] = mev
                        if (not is_xbc) and (not is_adt) and tt % 2 == 1:
                            gc = c - 49
                            half = tt // 2
                            gst = sch.dma("scalar",
                                          s_gate[gc * 128:(gc + 1) * 128,
                                                 mbase + half * 1024:mbase + half * 1024 + 1024],
                                          cot[sl][:, 0:1024],
                                          stream=f"p1g{sl}", after=[m_ev[tt - 1], m_ev[tt]])
                            if dbg_level in (1, 2, 3):
                                gst = sch.dma("scalar",
                                              dbg["gate"][gc * 128:(gc + 1) * 128,
                                                          mbase + half * 1024:mbase + half * 1024 + 1024],
                                              cot[sl][:, 0:1024], stream=f"p1g{sl}",
                                              after=[m_ev[tt - 1], m_ev[tt]])
                            ms_gevac[sl] = gst
                    wuse[sl] = m_ev[3]
                    if pend_tail and c >= 1:
                        pend_tail.pop(0)()
                    if is_xbc:
                        ms_cinw[sl] = m_ev[3]
                        ms_stash[c] = sch.add("vector", lambda e, c=c, sl=sl: e.tensor_scalar(
                            out=stash[:, c * 3:c * 3 + 3], in0=cin[sl][:, 2048:2051], scalar1=1.0,
                            scalar2=None, op0=ALU.mult), after=[m_ev[2], m_ev[3]], sig=True)

                        def conv_tail(c=c, sl=sl, m_ev=m_ev, sup=sup, mbase=mbase):
                            nonlocal ms_sqmm, ms_rcread
                            mdg = None
                            for k4 in range(4):
                                mdg = sch.add("vector", lambda e, c=c, sl=sl, k4=k4: e.tensor_scalar(
                                    out=dg[sl][:, k4 * 128:(k4 + 1) * 128],
                                    in0=c_eye[:, :], scalar1=c_convw[:, c * 4 + k4:c * 4 + k4 + 1],
                                    scalar2=None, op0=ALU.mult),
                                    after=([ms_dguse[sl]] + (ALL_CONSTS if sup == 0 and c < 2 else [])
                                           if k4 == 0 else []),
                                    sig=(k4 == 3))
                            cmm = None
                            for rnd in range(2):
                                for k4 in range(4):
                                    for tt in range(2):
                                        cmm = sch.add("tensor", lambda e, sl=sl, k4=k4, tt=tt, rnd=rnd: e.matmul(
                                            ps_cv[tt][:, :],
                                            dg[sl][:, k4 * 128:(k4 + 1) * 128],
                                            cin[sl][:, rnd * 1024 + k4 + tt * 512:rnd * 1024 + k4 + tt * 512 + 512],
                                            start=(k4 == 0), stop=(k4 == 3)),
                                            after=([m_ev[0], m_ev[1], m_ev[2], m_ev[3], mdg,
                                                    ms_coscl[0], ms_coscl[1]]
                                                   if (k4 == 0 and tt == 0 and rnd == 0) else
                                                   ([ms_coscl[0], ms_coscl[1]] if (k4 == 0 and tt == 0) else [])),
                                            sig=(k4 == 3 and tt == 1))
                                s1 = []
                                s2 = []
                                for tt in range(2):
                                    s1.append(sch.add("scalar", lambda e, c=c, sl=sl, tt=tt: e.activation(
                                        out=cot[sl][:, tt * 512:tt * 512 + 512], in_=ps_cv[tt][:, :],
                                        func=AF.Identity, bias=c_convb[:, c:c + 1], scale=1.0),
                                        after=([cmm, ctx.full(f"p1co{sl}"), ctx.full(f"p1g{sl}")] if tt == 0 else []),
                                        sig=(tt == 1)))
                                for tt in range(2):
                                    s2.append(sch.add("scalar", lambda e, c=c, sl=sl, tt=tt: e.activation(
                                        out=sqt[sl][:, tt * 512:tt * 512 + 512], in_=ps_cv[tt][:, :],
                                        func=AF.Square, bias=c_convb[:, c:c + 1], scale=1.0),
                                        after=[cmm], sig=(tt == 1)))
                                ms_coscl[0] = s1[1]
                                ms_coscl[1] = s2[1]
                                sch.dma("gpsimd", s_co[c * 128:(c + 1) * 128,
                                                       mbase + rnd * 1024:mbase + rnd * 1024 + 1024],
                                        cot[sl][:, :], stream=f"p1co{sl}", after=[s1[1]])
                                if dbg_level in (1, 2, 3):
                                    sch.dma("gpsimd", dbg["co"][c * 128:(c + 1) * 128,
                                                                mbase + rnd * 1024:mbase + rnd * 1024 + 1024],
                                            cot[sl][:, :], stream=f"p1co{sl}", after=[s1[1]])
                                for tt in range(2):
                                    ms_sqmm = sch.add("tensor", lambda e, tt=tt, sl=sl, c=c, rnd=rnd: e.matmul(
                                        ps_sq[rnd * 2 + tt][:, :], c_onesT[:, :],
                                        sqt[sl][:, tt * 512:tt * 512 + 512],
                                        start=(c == 0), stop=(c == min(NCB_XBC, NCB_RUN) - 1)),
                                        after=[s2[1]] + ([ms_rcread] if c == 0 else []),
                                        sig=(tt == 1 and c == min(NCB_XBC, NCB_RUN) - 1 and rnd == 1))
                            ms_cinr[sl] = cmm
                            ms_dguse[sl] = cmm

                        pend_tail.append(conv_tail)
                    if c >= 49:
                        for _ in range(4):
                            if p1b_units:
                                p1b_units.pop(0)()
                while pend_tail:
                    pend_tail.pop(0)()
                while p1b_units:
                    p1b_units.pop(0)()
                ms_sqmm = None
            sch.emit(block)

        if dbg_level == 1:
            fin = Sched(ctx)
            z = stack.enter_context(nc.sbuf_tensor("zz", [128, D_MODEL], F32))
            wl = [ctx.full(s) for s in ["p1xb0", "p1xb1", "p1xb2", "p1xb3", "p1g0", "p1g1", "p1co0", "p1co1"]]
            ms = fin.add("vector", lambda e: e.memset(z[:, :], 0.0), after=wl, sig=True)
            for i in range(TCORE // 128):
                fin.dma("gpsimd", out[i * 128:(i + 1) * 128, :], z[:, :], stream="fino", after=[ms])
            fms = ctx.full("fino")
            fin.add("gpsimd", lambda e, fms=fms: e.wait_ge(ctx.sems[fms[0]], fms[1]))
            fin.emit(block)
            return nc

        sch.barrier()
        with contextlib.ExitStack() as ph:
            pA = ph.enter_context(nc.sbuf_tensor("p1c_A", [H, TCORE], F32))
            pq = ph.enter_context(nc.sbuf_tensor("p1c_q", [H, 512], F32))
            pz = ph.enter_context(nc.sbuf_tensor("p1c_z", [H, 16], F32))
            zer = ph.enter_context(nc.sbuf_tensor("p1c_zer", [H, CHUNK], F32))
            cdc = ph.enter_context(nc.sbuf_tensor("p1c_cdc", [H, NCHUNK], F32))
            cdflat = ph.enter_context(nc.sbuf_tensor("p1c_cdf", [1, NCHUNK * H], F32))
            ps_z = ph.enter_context(nc.psum_tensor("p1c_pz", [H, 512], F32))
            ps_cd = [ph.enter_context(nc.psum_tensor(f"p1c_pcd{i}", [128, 512], F32)) for i in range(2)]

            m_ne = sch.add("scalar", lambda e: e.activation(
                out=pz[:, 0:1], in_=c_alog[:, :], func=AF.Exp), sig=True)
            m_ne2 = sch.add("vector", lambda e: e.tensor_scalar(
                out=pz[:, 0:1], in0=pz[:, 0:1], scalar1=-1.0, scalar2=None, op0=ALU.mult),
                after=[m_ne], sig=True)
            m_zer = sch.add("vector", lambda e: e.memset(zer[:, :], 0.0), sig=True)
            m_sc = None
            m_Aprev = None
            for it in range(8):
                mm = sch.add("tensor", lambda e, it=it: e.matmul(
                    ps_z[:, :], c_dtwT[:, :], st_adt[:, it * 512:(it + 1) * 512],
                    start=True, stop=True), after=([m_sc] if m_sc else []), sig=True)
                mq = sch.add("scalar", lambda e: e.activation(
                    out=pq[:, :], in_=ps_z[:, :], func=AF.Exp, bias=c_dtb[:, :]),
                    after=[mm, m_Aprev], sig=True)
                m_sc = mq
                ml = sch.add("scalar", lambda e: e.activation(
                    out=pq[:, :], in_=pq[:, :], func=AF.Ln, bias=1.0), after=[mq], sig=True)
                m_A = sch.add("vector", lambda e, it=it: e.tensor_scalar(
                    out=pA[:, it * 512:(it + 1) * 512], in0=pq[:, :], scalar1=pz[:, 0:1],
                    scalar2=None, op0=ALU.mult), after=[ml, m_ne2], sig=True)
                m_Aprev = m_A
            m_lastA = m_A
            m_scan = None
            for cch in range(NCHUNK):
                m_scan = sch.add("vector", lambda e, cch=cch: e.tensor_tensor_scan(
                    out=st_acs[:, cch * CHUNK:(cch + 1) * CHUNK],
                    data0=pA[:, cch * CHUNK:(cch + 1) * CHUNK], data1=zer[:, :],
                    initial=0.0, op0=ALU.add, op1=ALU.add),
                    after=[m_lastA, m_zer] + ([m_scan] if m_scan else []), sig=True)
            m_al = sch.add("vector", lambda e: e.tensor_scalar(
                out=st_alast[:, :],
                in0=bass.AP(st_acs, CHUNK - 1, [[TCORE, H], [CHUNK, NCHUNK]]),
                scalar1=1.0, scalar2=None, op0=ALU.mult), after=[m_scan], sig=True)
            m_oi = sch.add("vector", lambda e: e.tensor_tensor_scan(
                out=st_offs[:, :], data0=st_alast[:, :], data1=zer[:, 0:NCHUNK],
                initial=0.0, op0=ALU.add, op1=ALU.add), after=[m_al], sig=True)
            m_le = sch.add("vector", lambda e: e.tensor_scalar(
                out=st_lend[:, :], in0=st_offs[:, NCHUNK - 1:NCHUNK], scalar1=1.0,
                scalar2=None, op0=ALU.mult), after=[m_oi], sig=True)
            m_ox = sch.add("vector", lambda e: e.tensor_tensor(
                out=st_offs[:, :], in0=st_offs[:, :], in1=st_alast[:, :], op=ALU.subtract),
                after=[m_le], sig=True)
            m_wb = sch.add("vector", lambda e: e.tensor_scalar(
                out=st_wb[:, :], in0=st_offs[:, :], scalar1=-1.0, scalar2=st_lend[:, :],
                op0=ALU.mult, op1=ALU.add), after=[m_ox], sig=True)
            m_dk = sch.add("scalar", lambda e: e.activation(
                out=pz[:, 1:2], in_=st_lend[:, :], func=AF.Exp), after=[m_le], sig=True)
            sch.dma("gpsimd", ag_src[H * NSTATE:H * NSTATE + H, :], zer[0:64, 0:64],
                    stream="agd", after=[m_zer])
            sch.dma("gpsimd", ag_src[H * NSTATE:H * NSTATE + 1, 0:H], pz[:, 1:2],
                    stream="agd", after=[m_dk, ctx.full("agd")])
            m_hi = sch.add("vector", lambda e: e.tensor_scalar(
                out=st_hi[:, :], in0=st_acs[:, :], scalar1=1.0, scalar2=None, op0=ALU.mult),
                after=[m_al], sig=True)
            m_r1 = sch.add("vector", lambda e: e.tensor_tensor(
                out=pA[:, :], in0=st_acs[:, :], in1=st_hi[:, :], op=ALU.subtract),
                after=[m_hi], sig=True)
            m_lo = sch.add("vector", lambda e: e.tensor_scalar(
                out=st_lo[:, :], in0=pA[:, :], scalar1=1.0, scalar2=None, op0=ALU.mult),
                after=[m_r1], sig=True)
            m_cd = sch.add("scalar", lambda e: e.activation(
                out=cdc[:, :], in_=st_alast[:, :], func=AF.Exp), after=[m_al], sig=True)
            m_cdf = sch.dma("sync", cdflat[0:1, :], cdc[:, :], stream="cdf", after=[m_cd])
            m_rep = []
            for half in range(2):
                mr = sch.add("tensor", lambda e, half=half: e.matmul(
                    ps_cd[half][:, :], c_onesf[:, :], cdflat[0:1, half * 512:(half + 1) * 512],
                    start=True, stop=True), after=[m_cdf], sig=True)
                m_rep.append(sch.add("vector", lambda e, half=half: e.tensor_scalar(
                    out=st_cdrep[:, half * 512:(half + 1) * 512], in0=ps_cd[half][:, :],
                    scalar1=1.0, scalar2=None, op0=ALU.mult), after=[mr], sig=True))
            if dbg_level in (1, 2, 3):
                sch.dma("gpsimd", dbg["acs"], st_acs[:, :], stream="dacs", after=[m_scan])
            sch.emit(block)

        sch.barrier()
        with contextlib.ExitStack() as ph:
            wcm = ph.enter_context(nc.sbuf_tensor("p2_wcm", [H, CHUNK], BF16))
            dcm = ph.enter_context(nc.sbuf_tensor("p2_dcm", [H, CHUNK], BF16))
            ps_tr = ph.enter_context(nc.psum_tensor("p2_ptr", [128, 64], BF16))
            ps_trf = ph.enter_context(nc.psum_tensor("p2_ptrf", [128, 64], F32))
            m_wu = None
            m_du = None
            m_tre = None
            m_trf = None
            for cch in range(NCHUNK):
                for tt in range(2):
                    mt = sch.add("tensor", lambda e, cch=cch, tt=tt: e.transpose(
                        ps_trf[:, 0:64], st_acs[:, cch * CHUNK + tt * 128:cch * CHUNK + tt * 128 + 128],
                        c_eyef[0:64, 0:64]), after=[m_trf], sig=True)
                    m_trf = sch.add("vector", lambda e, cch=cch, tt=tt: e.tensor_scalar(
                        out=negacs[:, cch * 128 + tt * 64:cch * 128 + tt * 64 + 64],
                        in0=ps_trf[:, 0:64], scalar1=-1.0, scalar2=None, op0=ALU.mult),
                        after=[mt], sig=True)
            for cch in range(NCHUNK):
                tb = cch * CHUNK
                m_wcm = sch.add("scalar", lambda e, cch=cch, tb=tb: e.activation(
                    out=wcm[:, :], in_=st_acs[:, tb:tb + CHUNK], func=AF.Exp,
                    bias=st_wb[:, cch:cch + 1], scale=-1.0), after=[m_wu], sig=True)
                m_dcm = sch.add("scalar", lambda e, cch=cch, tb=tb: e.activation(
                    out=dcm[:, :], in_=st_acs[:, tb:tb + CHUNK], func=AF.Exp,
                    bias=st_alast[:, cch:cch + 1], scale=-1.0), after=[m_du], sig=True)
                for (src_t, m_src, dst, tag) in ((wcm, m_wcm, wtk_all, 0), (dcm, m_dcm, d1tk_all, 1)):
                    for tt in range(2):
                        mt = sch.add("tensor", lambda e, src_t=src_t, tt=tt: e.transpose(
                            ps_tr[:, :], src_t[:, tt * 128:(tt + 1) * 128], c_eye[0:64, 0:64]),
                            after=[m_src, m_tre], sig=True)
                        m_tre = sch.add("vector", lambda e, dst=dst, cch=cch, tt=tt: e.tensor_scalar(
                            out=dst[:, cch * 128 + tt * 64:cch * 128 + tt * 64 + 64],
                            in0=ps_tr[:, :], scalar1=1.0, scalar2=None, op0=ALU.mult),
                            after=[mt], sig=True)
                    if tag == 0:
                        m_wu = m_tre
                    else:
                        m_du = m_tre
            sch.emit(block)

        sch.barrier()
        with contextlib.ExitStack() as ph:
            xtk = [ph.enter_context(nc.sbuf_tensor(f"p2_x{i}", [128, 2 * D_INNER], BF16)) for i in range(2)]
            btk = [ph.enter_context(nc.sbuf_tensor(f"p2_b{i}", [128, 2 * GN], BF16)) for i in range(2)]
            dxw = [ph.enter_context(nc.sbuf_tensor(f"p2_dx{i}", [128, 2 * D_INNER], BF16)) for i in range(2)]
            skst = ph.enter_context(nc.sbuf_tensor("p2_sk", [128, H * D_HEAD], F32))
            ps_sk = [ph.enter_context(nc.psum_tensor(f"p2_psk{i}", [128, 512], F32)) for i in range(8)]

            m_dxu = [None, None]
            m_mm = None
            for cch in range(NCHUNK):
                tb = cch * CHUNK
                sl2 = cch % 2
                lx, lb = [], []
                for tt in range(2):
                    lx.append(sch.dma("sync", xtk[sl2][:, tt * D_INNER:(tt + 1) * D_INNER],
                                      s_xbc[0:D_INNER, tb + tt * 128:tb + tt * 128 + 128],
                                      transpose=True, stream=f"p2x{sl2}",
                                      after=([m_dxu[sl2]] if tt == 0 else [])))
                    lb.append(sch.dma("scalar", btk[sl2][:, tt * GN:(tt + 1) * GN],
                                      s_xbc[D_INNER:D_INNER + GN, tb + tt * 128:tb + tt * 128 + 128],
                                      transpose=True, stream=f"p2b{sl2}",
                                      after=([m_mm] if tt == 0 else [])))
                m_dx = []
                for tt in range(2):
                    m_dx.append(sch.add("vector", lambda e, sl2=sl2, tt=tt, cch=cch: e.tensor_tensor(
                        out=dxw[sl2][:, tt * D_INNER:(tt + 1) * D_INNER],
                        in0=xtk[sl2][:, tt * D_INNER:(tt + 1) * D_INNER],
                        in1=bass.AP(wtk_all, cch * 128 + tt * 64, [[NCHUNK * 128, 128], [1, H], [0, D_HEAD]]),
                        op=ALU.mult), after=[lx[1], m_dxu[sl2]], sig=(tt == 1)))
                for h in range(H):
                    g = h // (H // G)
                    bank, col = h // 8, (h % 8) * 64
                    for tt in range(2):
                        m_mm = sch.add("tensor", lambda e, sl2=sl2, tt=tt, h=h, g=g, bank=bank, col=col, cch=cch: e.matmul(
                            ps_sk[bank][:, col:col + 64],
                            btk[sl2][:, tt * GN + g * 128:tt * GN + g * 128 + 128],
                            dxw[sl2][:, tt * D_INNER + h * 64:tt * D_INNER + h * 64 + 64],
                            start=(cch == 0 and tt == 0 and h % 8 == 0),
                            stop=(cch == NCHUNK - 1 and tt == 1 and h % 8 == 7)),
                            after=([m_dx[1], lb[1]] if (h == 0 and tt == 0) else []),
                            sig=(h == H - 1 and tt == 1))
                m_dxu[sl2] = m_mm
            m_ev = []
            for bank in range(8):
                m_ev.append(sch.add("vector", lambda e, bank=bank: e.tensor_scalar(
                    out=skst[:, bank * 512:(bank + 1) * 512], in0=ps_sk[bank][:, :],
                    scalar1=1.0, scalar2=None, op0=ALU.mult), after=[m_mm], sig=True))
            m_sk = sch.dma("gpsimd",
                           bass.AP(ag_src_t, 0, [[64, 128], [128 * 64, H], [1, 64]]),
                           skst[:, :], stream="agd", after=[m_ev[-1]] + m_ev[:-1])
            agd_full = ctx.full("agd")

            def do_cc(e):
                rg = ([[0, 1, 2, 3]] if os.environ.get("K_CC4") else [[0, 1, 2, 3], [4, 5, 6, 7]])
                instr = e.collective_compute(
                    "AllGather", ALU.bypass,
                    replica_groups=rg,
                    ins=[ag_src.opt()], outs=[ag_dst.opt()])
                instr.then_inc(ctx.sems["cc"], 1)
                return instr
            sch.add("gpsimd", do_cc, after=[agd_full])
            ctx.cnt["cc"] += 1
            sch.emit(block)
        mstackA.close()

        sch.barrier()
        with contextlib.ExitStack() as ph:
            xnT = ph.enter_context(nc.sbuf_tensor("p4_xnT", [128, NKB * 2048], BF16))
            wg = [ph.enter_context(nc.sbuf_tensor(f"p4_wg{i}", [128, NKB * 128], BF16)) for i in range(2)]
            wu = [ph.enter_context(nc.sbuf_tensor(f"p4_wu{i}", [128, NKB * 128], BF16)) for i in range(2)]

            sg = [ph.enter_context(nc.sbuf_tensor(f"p4_sg{i}", [128, 2048], BF16)) for i in range(2)]
            hd = [ph.enter_context(nc.sbuf_tensor(f"p4_hd{i}", [128, 2048], BF16)) for i in range(2)]
            ps_g = ph.enter_context(nc.psum_tensor("p4_pg", [128, 2048], F32))
            ps_u = ph.enter_context(nc.psum_tensor("p4_pu", [128, 2048], F32))

            m_wgu = [None, None]
            m_wuu = [None, None]
            m_sil_t = [None] * 4
            m_hdn_t4 = [None] * 4
            m_sg_u = [None, None]
            m_hdn_last = None
            for th in range(2):
                tbb = th * 2048
                prevmm = [m for m in (m_wgu + m_wuu) if m]
                m_xn = [sch.dma("sync", xnT[:, k * 2048:(k + 1) * 2048],
                                s_xn[tbb:tbb + 2048, k * 128:(k + 1) * 128], transpose=True,
                                stream="p4xn", after=(prevmm if k == 0 else []) +
                                ([ctx.full("p0st0"), ctx.full("p0st1"), ctx.full("p0st2")] if th == 0 else []))
                       for k in range(NKB)]
                xn_full = ctx.full("p4xn")
                for ib in range(NIB):
                    sl = ib % 2
                    lg = sch.dma("gpsimd", wg[sl][:, :],
                                 bass.AP(gate_wT.tensor, ib * 128,
                                         [[INTER, 128], [128 * INTER, NKB], [1, 128]]),
                                 stream=f"p4wg{sl}", after=[m_wgu[sl]])
                    lu = sch.dma("gpsimd", wu[sl][:, :],
                                 bass.AP(up_wT.tensor, ib * 128,
                                         [[INTER, 128], [128 * INTER, NKB], [1, 128]]),
                                 stream=f"p4wu{sl}", after=[m_wuu[sl]])
                    mmg_t = [None] * 4
                    sil_t = [None] * 4
                    for tt in range(4):
                        for k in range(NKB):
                            mmg_t[tt] = sch.add("tensor", lambda e, sl=sl, k=k, tt=tt: e.matmul(
                                ps_g[:, tt * 512:(tt + 1) * 512], wg[sl][:, k * 128:(k + 1) * 128],
                                xnT[:, k * 2048 + tt * 512:k * 2048 + tt * 512 + 512],
                                start=(k == 0), stop=(k == NKB - 1)),
                                after=(([lg, xn_full] if tt == 0 else []) +
                                       ([m_sil_t[tt]] if k == 0 else [])),
                                sig=(k == NKB - 1))
                        sil_t[tt] = sch.add("scalar", lambda e, sl=sl, ib=ib, tt=tt: e.activation(
                            out=sg[sl][:, tt * 512:(tt + 1) * 512], in_=ps_g[:, tt * 512:(tt + 1) * 512],
                            func=AF.Silu, scale=c_gs[:, ib:ib + 1]),
                            after=[mmg_t[tt], m_sg_u[sl]], sig=True)
                        m_sil_t[tt] = sil_t[tt]
                    m_wgu[sl] = mmg_t[3]
                    mmu_t = [None] * 4
                    hdn_t = [None] * 4
                    for tt in range(4):
                        for k in range(NKB):
                            mmu_t[tt] = sch.add("tensor", lambda e, sl=sl, k=k, tt=tt: e.matmul(
                                ps_u[:, tt * 512:(tt + 1) * 512], wu[sl][:, k * 128:(k + 1) * 128],
                                xnT[:, k * 2048 + tt * 512:k * 2048 + tt * 512 + 512],
                                start=(k == 0), stop=(k == NKB - 1)),
                                after=(([lu] if tt == 0 else []) +
                                       ([m_hdn_t4[tt]] if k == 0 else [])),
                                sig=(k == NKB - 1))
                        hdn_t[tt] = sch.add("vector", lambda e, sl=sl, ib=ib, tt=tt: e.scalar_tensor_tensor(
                            out=hd[sl][:, tt * 512:(tt + 1) * 512], in0=ps_u[:, tt * 512:(tt + 1) * 512],
                            scalar=c_us[:, ib:ib + 1], in1=sg[sl][:, tt * 512:(tt + 1) * 512],
                            op0=ALU.mult, op1=ALU.mult),
                            after=[mmu_t[tt], sil_t[tt]] + ([ctx.full(f"p4hd{sl}")] if tt == 0 else []),
                            sig=True)
                        m_hdn_t4[tt] = hdn_t[tt]
                    m_wuu[sl] = mmu_t[3]
                    m_sg_u[sl] = hdn_t[3]
                    m_hdn_last = hdn_t[3]
                    sch.dma("scalar", s_hdn[ib * 128:(ib + 1) * 128, tbb:tbb + 2048],
                            hd[sl][:, :], stream=f"p4hd{sl}", after=list(hdn_t))
                    if dbg_level == 3:
                        sch.dma("scalar", dbg["hdn"][ib * 128:(ib + 1) * 128, tbb:tbb + 2048],
                                hd[sl][:, :], stream=f"p4hd{sl}", after=list(hdn_t))
            sch.emit(block)

        sch.barrier()
        with contextlib.ExitStack() as ph:
            sS = [ph.enter_context(nc.sbuf_tensor(f"p3_S{i}", [128, H * D_HEAD], F32)) for i in range(2)]
            drep = ph.enter_context(nc.sbuf_tensor("p3_drep", [128, 192], F32))
            dt3 = ph.enter_context(nc.sbuf_tensor("p3_dt3", [H, 8], F32))
            dflat = ph.enter_context(nc.sbuf_tensor("p3_dfl", [1, 256], F32))
            ps_dr = ph.enter_context(nc.psum_tensor("p3_pdr", [128, 192], F32))

            cc_ms = ("cc", 1)
            l0 = sch.dma("sync", sS[0][:, :],
                         bass.AP(ag_dst_t, 0, [[64, 128], [128 * 64, H], [1, 64]]),
                         stream="p3s0", after=[cc_ms])
            m_h = sch.add("vector", lambda e: e.tensor_scalar(
                out=hA[:, :], in0=sS[0][:, :], scalar1=c_rmask[:, 0:1], scalar2=None,
                op0=ALU.mult), after=[l0], sig=True)
            if dbg_level in (2, 3):
                sch.dma("gpsimd", dbg["agd"], ag_dst, stream="dagd", after=[cc_ms])
            ldd = None
            for r_ in range(4):
                ldd = sch.dma("scalar", dt3[:, 4 + r_:5 + r_],
                              bass.AP(ag_dst_t, (r_ * (H * NSTATE + H) + H * NSTATE) * 64,
                                      [[1, H], [1, 1]]),
                              stream="p3dd", after=([cc_ms] if r_ == 0 else []))
            m_d3 = None
            for m in range(1, 4):
                m_d3 = sch.add("vector", lambda e, m=m: e.tensor_scalar(
                    out=dt3[:, m - 1:m], in0=dt3[:, 4 + m:5 + m],
                    scalar1=c_rmask[0:64, m:m + 1], scalar2=c_rmask1[0:64, m:m + 1],
                    op0=ALU.mult, op1=ALU.add), after=[ldd] + ([m_d3] if m_d3 else []), sig=True)
            m_df = sch.dma("sync", dflat[0:1, 0:192], dt3[:, 0:3], stream="p3df", after=[m_d3])
            m_dr1 = sch.add("tensor", lambda e: e.matmul(
                ps_dr[:, :], c_onesf[:, :], dflat[0:1, 0:192], start=True, stop=True),
                after=[m_df], sig=True)
            m_dr = sch.add("vector", lambda e: e.tensor_scalar(
                out=drep[:, :], in0=ps_dr[:, :], scalar1=1.0, scalar2=None, op0=ALU.mult),
                after=[m_dr1], sig=True)
            hcur, hnxt = hA, hB
            m_hready = [m_h]
            for m in range(1, 4):
                lS = sch.dma("sync", sS[m % 2][:, :],
                             bass.AP(ag_dst_t, (m * 8256) * 64, [[64, 128], [128 * 64, H], [1, 64]]),
                             stream=f"p3s{m % 2}", after=[cc_ms] if m == 1 else [m_hready[-1]])
                m_msk = sch.add("vector", lambda e, m=m: e.tensor_scalar(
                    out=sS[m % 2][:, :], in0=sS[m % 2][:, :], scalar1=c_rmask[:, m:m + 1],
                    scalar2=None, op0=ALU.mult), after=[lS], sig=True)
                m_last = None
                for h in range(H):
                    m_last = sch.add("vector", lambda e, m=m, h=h, hcur=hcur, hnxt=hnxt: e.scalar_tensor_tensor(
                        out=hnxt[:, h * 64:(h + 1) * 64], in0=hcur[:, h * 64:(h + 1) * 64],
                        scalar=drep[:, (h * 3 + m - 1):(h * 3 + m)], in1=sS[m % 2][:, h * 64:(h + 1) * 64],
                        op0=ALU.mult, op1=ALU.add),
                        after=[m_msk, m_dr] + m_hready, sig=(h == H - 1))
                m_hready = [m_last]
                hcur, hnxt = hnxt, hcur
            m_hcomb = m_hready[0]

            xtk3 = [ph.enter_context(nc.sbuf_tensor("p3_x0", [128, 2 * D_INNER], BF16))] * 2
            btk3 = [ph.enter_context(nc.sbuf_tensor(f"p3_b{i}", [128, 2 * GN], BF16)) for i in range(2)]
            bcm = [ph.enter_context(nc.sbuf_tensor("p3_bc0", [128, 2048], BF16))] * 2
            ccm = [ph.enter_context(nc.sbuf_tensor("p3_cc0", [128, 2048], BF16))] * 2
            gcm = ph.enter_context(nc.sbuf_tensor("p3_g", [128, 8192], BF16))
            ych = ph.enter_context(nc.sbuf_tensor("p3_y", [128, 8192], BF16))
            ybuf = ph.enter_context(nc.sbuf_tensor("p3_yb", [128, 8192], BF16))
            flat = ph.enter_context(nc.sbuf_tensor("p3_flat", [3, 32 * CHUNK], BF16))
            cbm = ph.enter_context(nc.sbuf_tensor("p3_cbm", [128, G * 384], BF16))
            psb = [ph.enter_context(nc.sbuf_tensor(f"p3_p{i}", [128, 384], BF16)) for i in range(4)]
            ctl = [ph.enter_context(nc.sbuf_tensor(f"p3_ct{i}", [128, 256], BF16)) for i in range(4)]
            dxs = [ph.enter_context(nc.sbuf_tensor(f"p3_dx{i}", [128, 128], BF16)) for i in range(4)]
            grf = ph.enter_context(nc.sbuf_tensor("p3_grf", [128, 256], F32))
            grb = ph.enter_context(nc.sbuf_tensor("p3_grb", [128, 256], BF16))
            ps_d0 = ph.enter_context(nc.psum_tensor("p3_psd0", [128, 512], F32))
            ps_d1 = ph.enter_context(nc.psum_tensor("p3_psd1", [128, 512], F32))
            psd_of = lambda par: (ps_d0 if par < 2 else ps_d1)
            ps_mt = ph.enter_context(nc.sbuf_tensor("p3_mtsb", [128, 4 * 384], BF16))
            ps_et = ph.enter_context(nc.sbuf_tensor("p3_etsb", [128, 4 * 256], BF16))
            ps_y = [ph.enter_context(nc.psum_tensor(f"p3_psy{i}", [128, 512], F32))[:, 0:256] for i in range(2)]
            ps_s = [ph.enter_context(nc.psum_tensor(f"p3_pss{i}", [128, 512], F32))[:, 0:256] for i in range(2)]

            m_xu = [None, None]
            m_bu = [None, None]
            m_bcu = [None, None]
            m_ccu = [None, None]
            m_gu = None
            m_ychu = None
            m_flatu = None
            m_exps = [None] * 4
            m_pstt = [None] * 4
            m_ctt = [None] * 4
            m_yev = [None, None]
            m_hstt = [None, None]
            m_dxu = [None] * 4
            m_ymm = [None, None]
            m_smm = [None, None]
            m_cbev = None
            m_ygdma = None
            m_rmsmm = None
            m_grb = None
            for cch in range(NCHUNK):
                tb = cch * CHUNK
                sl2 = cch % 2
                lx = [sch.dma("sync", xtk3[sl2][:, tt * D_INNER:(tt + 1) * D_INNER],
                              s_xbc[0:D_INNER, tb + tt * 128:tb + tt * 128 + 128],
                              transpose=True, stream="p3x0",
                              after=([m_xu[0], m_xu[1]] if tt == 0 else [])) for tt in range(2)]
                lb = [sch.dma("scalar", btk3[sl2][:, tt * GN:(tt + 1) * GN],
                              s_xbc[D_INNER:D_INNER + GN, tb + tt * 128:tb + tt * 128 + 128],
                              transpose=True, stream=f"p3b{sl2}",
                              after=([m_bu[sl2]] if tt == 0 else [])) for tt in range(2)]
                lbc = sch.dma("sync", bcm[sl2][:, :],
                              bass.AP(s_xbc.tensor, D_INNER * TCORE + tb,
                                      [[TCORE, 128], [128 * TCORE, G], [1, CHUNK]]),
                              stream=f"p3bc{sl2}", after=[m_bcu[sl2]])
                lcc = sch.dma("scalar", ccm[sl2][:, :],
                              bass.AP(s_xbc.tensor, (D_INNER + GN) * TCORE + tb,
                                      [[TCORE, 128], [128 * TCORE, G], [1, CHUNK]]),
                              stream="p3cc0",
                              after=[m_ccu[0], m_ccu[1]] + list(m_ctt))
                lg = sch.dma("sync", gcm[:, :],
                             bass.AP(s_gate.tensor, tb,
                                     [[TCORE, 128], [128 * TCORE, NYB], [1, CHUNK]]),
                             stream="p3g", after=[m_gu])
                flat_ms = [None, None]
                m_cb = None
                for g in range(G):
                    mm1 = sch.add("tensor", lambda e, g=g, sl2=sl2: e.matmul(
                        ps_d0[:, 0:256], bcm[sl2][:, g * 256:g * 256 + 128],
                        ccm[sl2][:, g * 256:(g + 1) * 256], start=True, stop=True),
                        after=([lbc, lcc, m_exps[0], m_exps[1], m_cbev] if g == 0 else [m_cb]),
                        sig=False)
                    mm2 = sch.add("tensor", lambda e, g=g, sl2=sl2: e.matmul(
                        ps_d0[:, 256:384], bcm[sl2][:, g * 256 + 128:g * 256 + 256],
                        ccm[sl2][:, g * 256 + 128:(g + 1) * 256], start=True, stop=True),
                        sig=True)
                    m_cb = sch.add("vector", lambda e, g=g: e.tensor_tensor(
                        out=cbm[:, g * 384:(g + 1) * 384], in0=ps_d0[:, 0:384],
                        in1=c_cbmask[:, :], op=ALU.mult), after=[mm2, m_pstt[0], m_pstt[1]], sig=True)
                m_cbev = m_cb
                m_bcu[sl2] = mm2
                m_ccu[sl2] = mm2
                for h in range(H):
                    g = h // (H // G)
                    par = h % 4
                    jpar = (h // 2) % 2
                    hh = h % 32
                    if hh == 0:
                        half32 = h // 32
                        hrows = slice(half32 * 32, half32 * 32 + 32)
                        sch.dma("gpsimd", flat[0:1, :], st_hi[hrows, tb:tb + CHUNK],
                                stream="p3f", after=[m_flatu])
                        sch.dma("gpsimd", flat[1:2, :], st_lo[hrows, tb:tb + CHUNK],
                                stream="p3f")
                        flat_full = ctx.full("p3f")
                    m_r = sch.add("tensor", lambda e, h=h, par=par, hh=hh: e.matmul(
                        psd_of(par)[:, (par % 2) * 256:(par % 2) * 256 + 256], c_ones8[0:2, :],
                        flat[0:2, hh * 256:(hh + 1) * 256], start=True, stop=True),
                        after=[flat_full, m_exps[par], m_cbev if h < 4 else None], sig=True)
                    if hh == 31:
                        m_flatu = m_r
                    base = par * 384
                    m_mt0 = sch.add("scalar", lambda e, h=h, par=par, cch=cch, base=base: e.activation(
                        out=ps_mt[:, base:base + 256], in_=psd_of(par)[:, (par % 2) * 256:(par % 2) * 256 + 256],
                        func=AF.Exp, bias=negacs[:, cch * 128 + h:cch * 128 + h + 1]),
                        after=[m_r, m_pstt[par]], sig=False)
                    m_mt1 = sch.add("scalar", lambda e, h=h, par=par, cch=cch, base=base: e.activation(
                        out=ps_mt[:, base + 256:base + 384],
                        in_=psd_of(par)[:, (par % 2) * 256 + 128:(par % 2) * 256 + 256],
                        func=AF.Exp, bias=negacs[:, cch * 128 + 64 + h:cch * 128 + 64 + h + 1]),
                        sig=False)
                    m_et = sch.add("scalar", lambda e, h=h, par=par: e.activation(
                        out=ps_et[:, par * 256:par * 256 + 256],
                        in_=psd_of(par)[:, (par % 2) * 256:(par % 2) * 256 + 256], func=AF.Exp),
                        after=[m_ctt[par]], sig=True)
                    m_exps[par] = m_et
                    m_p = sch.add("vector", lambda e, par=par, g=g, base=base: e.scalar_tensor_tensor(
                        out=psb[par][:, :], in0=ps_mt[:, base:base + 384], scalar=1.0,
                        in1=cbm[:, g * 384:(g + 1) * 384], op0=ALU.min, op1=ALU.mult),
                        after=[m_et, m_ymm[jpar]], sig=True)
                    m_pstt[par] = m_p
                    m_ct = sch.add("gpsimd", lambda e, par=par, g=g, sl2=sl2: e.tensor_tensor(
                        out=ctl[par][:, :], in0=ps_et[:, par * 256:par * 256 + 256],
                        in1=ccm[sl2][:, g * 256:(g + 1) * 256], op=ALU.mult),
                        after=[m_et, m_ymm[jpar]], sig=True)
                    m_ctt[par] = m_ct
                    m_dx = []
                    for half in range(2):
                        m_dx.append(sch.add("vector", lambda e, sl2=sl2, h=h, par=par, half=half, cch=cch: e.tensor_scalar(
                            out=dxs[par][:, half * 64:(half + 1) * 64],
                            in0=xtk3[sl2][:, half * D_INNER + h * 64:half * D_INNER + h * 64 + 64],
                            scalar1=d1tk_all[:, cch * 128 + half * 64 + h:cch * 128 + half * 64 + h + 1],
                            scalar2=None, op0=ALU.mult),
                            after=[lx[1]] + ([m_dxu[par]] if half == 0 else []), sig=True))
                    r0_ = (h % 2) * 64
                    ymm1 = sch.add("tensor", lambda e, sl2=sl2, h=h, par=par, jpar=jpar, r0_=r0_: e.matmul(
                        ps_y[jpar][r0_:r0_ + 64, 0:256],
                        xtk3[sl2][:, h * 64:h * 64 + 64], psb[par][:, 0:256],
                        start=True, stop=False),
                        after=[m_p, m_yev[jpar] if h % 2 == 0 else None,
                               lx[1] if h == 0 else None], sig=False)
                    ymm2 = sch.add("tensor", lambda e, sl2=sl2, h=h, par=par, jpar=jpar, r0_=r0_: e.matmul(
                        ps_y[jpar][r0_:r0_ + 64, 128:256],
                        xtk3[sl2][:, D_INNER + h * 64:D_INNER + h * 64 + 64], psb[par][:, 256:384],
                        start=False, stop=False), sig=False)
                    hc = hcur
                    ymm3 = sch.add("tensor", lambda e, h=h, par=par, jpar=jpar, r0_=r0_, hc=hc: e.matmul(
                        ps_y[jpar][r0_:r0_ + 64, 0:256],
                        hc[:, h * 64:(h + 1) * 64], ctl[par][:, :],
                        start=False, stop=True),
                        after=[m_ct, m_hcomb if cch == 0 and h < 2 else None], sig=True)
                    m_ymm[jpar] = ymm3
                    spar = h % 2
                    smm = None
                    for half in range(2):
                        smm = sch.add("tensor", lambda e, sl2=sl2, h=h, par=par, g=g, half=half, spar=spar: e.matmul(
                            ps_s[spar][:, 0:64],
                            btk3[sl2][:, half * GN + g * 128:half * GN + g * 128 + 128],
                            dxs[par][:, half * 64:(half + 1) * 64],
                            start=(half == 0), stop=(half == 1)),
                            after=([m_dx[1], lb[1], m_hstt[spar], m_rmsmm if spar == 0 else None]
                                   if half == 0 else []), sig=(half == 1))
                    m_smm[spar] = smm
                    m_dxu[par] = smm
                    m_hstt[spar] = sch.add("vector", lambda e, h=h, spar=spar, cch=cch, hc=hcur, hn=hnxt: e.scalar_tensor_tensor(
                        out=hn[:, h * 64:(h + 1) * 64], in0=hc[:, h * 64:(h + 1) * 64],
                        scalar=st_cdrep[:, h * NCHUNK + cch:h * NCHUNK + cch + 1],
                        in1=ps_s[spar][:, 0:64], op0=ALU.mult, op1=ALU.add),
                        after=[smm] + ([m_hcomb] if cch == 0 and h < 2 else []), sig=True)
                    if h % 2 == 1:
                        jj = h // 2
                        m_yev[jpar] = sch.add("scalar", lambda e, jpar=jpar, jj=jj: e.activation(
                            out=ych[:, jj * 256:(jj + 1) * 256], in_=ps_y[jpar][:, :],
                            func=AF.Copy), after=[ymm3, m_ygdma], sig=True)
                m_xu[sl2] = smm
                m_bu[sl2] = smm
                hcur, hnxt = hnxt, hcur
                m_sq = sch.add("scalar", lambda e: e.activation(
                    out=ybuf[:, :], in_=ych[:, :], func=AF.Square),
                    after=[m_yev[0], m_yev[1], m_ygdma], sig=True)
                mrms = None
                for jb in range(NYB):
                    mrms = sch.add("tensor", lambda e, jb=jb: e.matmul(
                        ps_s[0][:, :], c_onesT[:, :], ybuf[:, jb * 256:(jb + 1) * 256],
                        start=(jb == 0), stop=(jb == NYB - 1)),
                        after=([m_sq, m_hstt[0], m_grb] if jb == 0 else []), sig=(jb == NYB - 1))
                m_rmsmm = mrms
                mg1 = sch.add("vector", lambda e: e.tensor_scalar(
                    out=grf[:, :], in0=ps_s[0][:, :], scalar1=1.0 / D_INNER, scalar2=1e-6,
                    op0=ALU.mult, op1=ALU.add), after=[mrms], sig=True)
                mg2 = sch.add("vector", lambda e: e.reciprocal(out=grf[:, :], in_=grf[:, :]),
                              after=[mg1], sig=True)
                mg3 = sch.add("scalar", lambda e: e.activation(out=grf[:, :], in_=grf[:, :],
                              func=AF.Sqrt), after=[mg2], sig=True)
                m_grb = sch.add("vector", lambda e: e.tensor_scalar(
                    out=grb[:, :], in0=grf[:, :], scalar1=1.0, scalar2=None, op0=ALU.mult),
                    after=[mg3], sig=True)
                m_sg = sch.add("scalar", lambda e: e.activation(
                    out=gcm[:, :], in_=gcm[:, :], func=AF.Silu), after=[lg, m_sq], sig=True)
                myg1 = sch.add("vector", lambda e: e.tensor_tensor(
                    out=ybuf[:, :], in0=ych[:, :],
                    in1=bass.AP(grb, 0, [[256, 128], [0, NYB], [1, 256]]),
                    op=ALU.mult), after=[m_grb, m_rmsmm], sig=True)
                myg2 = sch.add("vector", lambda e: e.tensor_tensor(
                    out=ybuf[:, :], in0=ybuf[:, :], in1=gcm[:, :], op=ALU.mult),
                    after=[myg1, m_sg], sig=True)
                m_gu = myg2
                m_ychu = myg2
                myd = sch.dma("gpsimd",
                              bass.AP(s_yg.tensor, tb, [[TCORE, 128], [128 * TCORE, NYB], [1, CHUNK]]),
                              ybuf[:, :], stream="p3yg", after=[myg2])
                if dbg_level == 3:
                    myd = sch.dma("gpsimd",
                                  bass.AP(dbg["yg"].tensor, tb, [[TCORE, 128], [128 * TCORE, NYB], [1, CHUNK]]),
                                  ybuf[:, :], stream="p3yg", after=[myg2])
                m_ygdma = myd
            sch.emit(block)

        if dbg_level in (2, 3):
            fin = Sched(ctx)
            z = stack.enter_context(nc.sbuf_tensor("zz", [128, D_MODEL], F32))
            wl = [ctx.full(s) for s in ("p3yg", "p4hd0", "p4hd1", "dagd")] + ([ctx.full("dacs")] if dbg_level in (1,2,3) else [])
            ms = fin.add("vector", lambda e: e.memset(z[:, :], 0.0), after=wl, sig=True)
            for i in range(TCORE // 128):
                fin.dma("gpsimd", out[i * 128:(i + 1) * 128, :], z[:, :], stream="fino", after=[ms])
            fms = ctx.full("fino")
            fin.add("gpsimd", lambda e, fms=fms: e.wait_ge(ctx.sems[fms[0]], fms[1]))
            fin.emit(block)
            return nc

        mstack.close()
        sch.barrier()
        NKTOT = NYB + NIB
        with contextlib.ExitStack() as ph:
            c_dsrep = ph.enter_context(nc.sbuf_tensor("c_dsrep", [128, D_MODEL], F32))
            mdsl = sch.dma("sync", c_dsrep[:, :], ds_rep, stream="p5c")
            w5 = ph.enter_context(nc.sbuf_tensor("p5_w", [128, NKTOT * 1024], BF16))
            rhs5 = [ph.enter_context(nc.sbuf_tensor(f"p5_r{i}", [128, 512], BF16)) for i in range(4)]
            ost = [ph.enter_context(nc.sbuf_tensor(f"p5_o{i}", [128, 512], BF16)) for i in range(4)]
            ps5 = [ph.enter_context(nc.psum_tensor(f"p5_ps{i}", [128, 512], F32)) for i in range(8)]
            m_prev_half = None
            mm5 = None
            for dhalf in range(2):
                m_wsc = []
                for k in range(NKTOT):
                    if k < NYB:
                        ap = bass.AP(op_wT.tensor, k * 128 * D_MODEL + dhalf * 1024,
                                     [[D_MODEL, 128], [1, 1024]])
                    else:
                        ap = bass.AP(down_wT.tensor, (k - NYB) * 128 * D_MODEL + dhalf * 1024,
                                     [[D_MODEL, 128], [1, 1024]])
                    lw = sch.dma("gpsimd", w5[:, k * 1024:(k + 1) * 1024], ap,
                                 stream=f"p5w{k % 4}",
                                 after=([m_prev_half] if k == 0 else []) +
                                       ([m_wsc[k - 4]] if k >= 4 else []))
                    if k < NYB:
                        m_wsc.append(sch.add("vector", lambda e, k=k: e.tensor_scalar(
                            out=w5[:, k * 1024:(k + 1) * 1024], in0=w5[:, k * 1024:(k + 1) * 1024],
                            scalar1=c_won[:, k:k + 1], scalar2=None, op0=ALU.mult),
                            after=[lw], sig=True))
                    else:
                        m_wsc.append(sch.add("vector", lambda e, k=k, dhalf=dhalf: e.tensor_tensor(
                            out=w5[:, k * 1024:(k + 1) * 1024], in0=w5[:, k * 1024:(k + 1) * 1024],
                            in1=c_dsrep[:, dhalf * 1024:(dhalf + 1) * 1024], op=ALU.mult),
                            after=[lw] + ([mdsl] if k == NYB else []), sig=True))
                wsc_ms = m_wsc[-1]
                m_ev5 = [None] * 8
                mm_slot = {0: None, 1: None, 2: None, 3: None}
                for ttile in range(8):
                    for k in range(NKTOT):
                        if k < NYB:
                            rap = s_yg[k * 128:(k + 1) * 128, ttile * 512:(ttile + 1) * 512]
                        else:
                            rap = s_hdn[(k - NYB) * 128:(k - NYB + 1) * 128, ttile * 512:(ttile + 1) * 512]
                        rsl = k % 4
                        lr = sch.dma("sync", rhs5[rsl][:, :], rap, stream=f"p5r{rsl}",
                                     after=[mm_slot[rsl] or mm5, ctx.full(f"p5r{rsl}")])
                        for d in range(8):
                            mm5 = sch.add("tensor", lambda e, k=k, d=d, rsl=rsl: e.matmul(
                                ps5[d][:, :], w5[:, k * 1024 + d * 128:k * 1024 + d * 128 + 128],
                                rhs5[rsl][:, :], start=(k == 0), stop=(k == NKTOT - 1)),
                                after=([lr, m_wsc[k]] + (m_ev5 if k == 0 else []) if d == 0 else []),
                                sig=(d == 7))
                        mm_slot[rsl] = mm5
                    for d in range(8):
                        osl = d % 4
                        mev = sch.add("vector", lambda e, d=d, osl=osl: e.tensor_scalar(
                            out=ost[osl][:, :], in0=ps5[d][:, :], scalar1=1.0, scalar2=None,
                            op0=ALU.mult), after=[mm5, ctx.full(f"p5o{osl}")], sig=True)
                        m_ev5[d] = mev
                        sch.dma("scalar", s_inner[(dhalf * 8 + d) * 128:(dhalf * 8 + d) * 128 + 128,
                                                  ttile * 512:(ttile + 1) * 512],
                                ost[osl][:, :], stream=f"p5o{osl}", after=[mev])
                m_prev_half = mm5
            sch.emit(block)

        sch.barrier()
        with contextlib.ExitStack() as ph:
            it6 = [ph.enter_context(nc.sbuf_tensor(f"p6_i{i}", [128, D_MODEL], BF16)) for i in range(2)]
            r06 = [ph.enter_context(nc.sbuf_tensor(f"p6_r{i}", [128, D_MODEL], F32)) for i in range(2)]
            ot6 = [ph.enter_context(nc.sbuf_tensor(f"p6_o{i}", [128, D_MODEL], F32)) for i in range(2)]
            m_add = [None, None]
            for i in range(TCORE // 128):
                sl = i % 2
                li = sch.dma("sync", it6[sl][:, :],
                             s_inner[0:D_MODEL, i * 128:(i + 1) * 128], transpose=True,
                             stream=f"p6i{sl}", after=[m_add[sl]])
                lr = sch.dma("scalar", r06[sl][:, :], s_r0[i * 128:(i + 1) * 128, :],
                             stream=f"p6r{sl}", after=[m_add[sl]])
                ma = sch.add("vector", lambda e, sl=sl: e.tensor_tensor(
                    out=ot6[sl][:, :], in0=r06[sl][:, :], in1=it6[sl][:, :], op=ALU.add),
                    after=[li, lr, ctx.full(f"p6o{sl}")], sig=True)
                m_add[sl] = ma
                sch.dma("gpsimd", out[i * 128:(i + 1) * 128, :], ot6[sl][:, :],
                        stream=f"p6o{sl}", after=[ma])
            for slx in range(2):
                fms = ctx.full(f"p6o{slx}")
                sch.add("gpsimd", lambda e, fms=fms: e.wait_ge(ctx.sems[fms[0]], fms[1]))
            sch.emit(block)


    return nc



def _prep_inputs(inputs):
    x = inputs["L0_attn_x"]
    W = inputs
    ip_w = W["L0_attn_in_proj_w"]
    ip_wT_full = np.ascontiguousarray(ip_w.T)
    ip_wT = np.zeros((D_MODEL, IP_COLS), np.float32)
    ip_wT[:, 0:4096] = ip_wT_full[:, 0:4096]
    ip_wT[:, 4096:5120] = ip_wT_full[:, 4160:5184]
    ip_wT[:, 5120:6144] = ip_wT_full[:, 5184:6208]
    ip_wT[:, 6208:6272] = ip_wT_full[:, 4096:4160]
    ip_wT[:, 6272:10368] = ip_wT_full[:, 6208:10304]

    convw_r = np.ascontiguousarray(
        W["L0_attn_conv_w"][:, 0, :].reshape(NCB_XBC, 128, 4).transpose(1, 0, 2).reshape(128, NCB_XBC * 4))
    convb_r = np.ascontiguousarray(W["L0_attn_conv_b"].reshape(NCB_XBC, 128).T)
    cnw_r = np.ascontiguousarray(W["L0_attn_conv_norm_w"].reshape(NCB_XBC, 128).T)
    won_r = np.ascontiguousarray(W["L0_attn_out_norm_w"].reshape(NYB, 128).T)
    op_wT = np.ascontiguousarray(W["L0_attn_out_proj_w"].T)
    gate_wT = np.ascontiguousarray(W["L0_ffn_gate_w"].T)
    up_wT = np.ascontiguousarray(W["L0_ffn_up_w"].T)
    down_wT = np.ascontiguousarray(W["L0_ffn_down_w"].T)
    gs_r = np.ascontiguousarray(W["L0_ffn_gate_s"].reshape(NIB, 128).T)
    us_r = np.ascontiguousarray(W["L0_ffn_up_s"].reshape(NIB, 128).T)
    ds_rep = np.ascontiguousarray(np.broadcast_to(W["L0_ffn_down_s"], (128, D_MODEL)))
    nw_rep = np.ascontiguousarray(np.broadcast_to(W["L0_attn_norm_w"], (128, D_MODEL)))

    eye = np.eye(128, dtype=BF)
    ones8 = np.ones((8, 128), BF)
    onesT = np.ones((128, 128), BF)
    cbm = np.zeros((128, 384), BF)
    p = np.arange(128)[:, None]
    c = np.arange(256)[None, :]
    cbm[:, 0:256] = (c >= p).astype(BF)
    cbm[:, 256:384] = (c[:, :128] >= p).astype(BF)

    shared = {
        "ip_wT": ip_wT, "convw_r": convw_r, "convb_r": convb_r, "cnw_r": cnw_r,
        "a_log": W["L0_attn_A_log"].reshape(H, 1).astype(np.float32),
        "dt_b": W["L0_attn_dt_b"].reshape(H, 1).astype(np.float32),
        "dt_wT": np.ascontiguousarray(W["L0_attn_dt_w"].T),
        "won_r": won_r, "op_wT": op_wT,
        "gate_wT": gate_wT, "up_wT": up_wT, "down_wT": down_wT,
        "gs_r": gs_r, "us_r": us_r, "ds_rep": ds_rep, "nw_rep": nw_rep,
        "eye128": eye, "ones8": ones8, "onesT": onesT, "cb_mask": cbm,
        "ones1f": np.ones((1, 128), np.float32),
        "eye128f": np.eye(128, dtype=np.float32),
    }

    in_maps = []
    for core in range(8):
        b, q = core // 4, core % 4
        t0 = q * TCORE
        xh = np.zeros((TT, D_MODEL), np.float32)
        if q > 0:
            xh[:HALO] = x[b, t0 - HALO:t0]
        xh[HALO:] = x[b, t0:t0 + TCORE]
        rm = (np.arange(4)[None, :] < q).astype(np.float32)
        m = dict(shared)
        m["x_h"] = xh
        m["rmask"] = np.ascontiguousarray(np.broadcast_to(rm, (128, 4)))
        m["rmask1"] = np.ascontiguousarray(np.broadcast_to(1.0 - rm, (128, 4)))
        in_maps.append(m)
    return in_maps


def kernel(**inputs):
    dbg_level = inputs.pop("_dbg_level", 99)
    in_maps = _prep_inputs(inputs)
    nc = build_program(dbg_level=dbg_level)
    res = run_bass_kernel_spmd(nc, in_maps, core_ids=list(range(8)))
    outs = [res.results[c]["out"] for c in range(8)]
    full = np.empty((2, 16384, D_MODEL), np.float32)
    for core in range(8):
        b, q = core // 4, core % 4
        full[b, q * TCORE:(q + 1) * TCORE] = outs[core]
    if dbg_level < 99:
        kernel._dbg = res.results
    return full
